# version 11
# baseline (speedup 1.0000x reference)
import numpy as np
from contextlib import ExitStack

import concourse.bass as bass
import concourse.mybir as mybir
from concourse.bass_utils import run_bass_kernel_spmd

F32 = mybir.dt.float32
BF16 = mybir.dt.bfloat16
I32 = mybir.dt.int32
AF = mybir.ActivationFunctionType
ALU = mybir.AluOpType

C = 2048
CT = 16
TB = 1024
HF = 5632
NJ = 44
CHUNK = 11
NCHUNK = 4
D = 128
NQH = 16
NKV = 4
SEQ = 8192
KT = 64
NBLK = 4
NW_12 = HF * C
NW_3 = C * HF
EPS_BIT = 1e-8
EPS_LN = 1e-5
EPS_QK = 1e-6
INV_SQRT_D = float(1.0 / np.sqrt(128.0))
SHIFT_C = float(np.sqrt(128.0) + 1e-3)
ETS = 6

ENGINES = ("sync", "gpsimd", "vector", "scalar", "tensor")


class Rec:
    def __init__(self):
        self.ops = {e: [] for e in ENGINES}
        self.cnt = {}
        self.sems = None

    def inc(self, sem, amt):
        self.cnt[sem] = self.cnt.get(sem, 0) + amt
        return self.cnt[sem]

    def cur(self, sem):
        return self.cnt.get(sem, 0)

    def on(self, eng, fn):
        self.ops[eng].append(fn)

    def wait(self, eng, sem, val=None):
        v = self.cur(sem) if val is None else val
        if v > 0:
            self.on(eng, lambda e, s=sem, vv=v: e.wait_ge(self.sems[s], vv))

    def emit_inc(self, eng, sem, amt, builder):
        v = self.inc(sem, amt)
        self.on(eng, lambda e, b=builder, s=sem, a=amt: b(e).then_inc(self.sems[s], a))
        return v


def build_program():
    nc = bass.Bass("TRN2", target_bir_lowering=False, debug=False)
    r = Rec()

    meta = nc.dram_tensor("meta", [1, 16], I32, kind="ExternalInput").ap()
    xT = nc.dram_tensor("xT", [NBLK, C, TB], F32, kind="ExternalInput").ap()
    w1th = nc.dram_tensor("w1th", [1024, HF], F32, kind="ExternalInput").ap()
    w2th = nc.dram_tensor("w2th", [1024, HF], F32, kind="ExternalInput").ap()
    w3th = nc.dram_tensor("w3th", [2816, C], F32, kind="ExternalInput").ap()
    wq = nc.dram_tensor("wq", [C, 2048], F32, kind="ExternalInput").ap()
    wk = nc.dram_tensor("wk", [C, 512], F32, kind="ExternalInput").ap()
    wv = nc.dram_tensor("wv", [C, 512], F32, kind="ExternalInput").ap()
    wo = nc.dram_tensor("wo", [2048, C], F32, kind="ExternalInput").ap()
    yT = nc.dram_tensor("yT", [C, TB], F32, kind="ExternalOutput").ap()

    sh_w1T = nc.dram_tensor("sh_w1T", [C, HF], BF16, addr_space="Shared").ap()
    sh_w2T = nc.dram_tensor("sh_w2T", [C, HF], BF16, addr_space="Shared").ap()
    sh_w3T = nc.dram_tensor("sh_w3T", [HF, C], BF16, addr_space="Shared").ap()
    sh_kT = nc.dram_tensor("sh_kT", [NKV * 128, SEQ], BF16, addr_space="Shared").ap()
    sh_v = nc.dram_tensor("sh_v", [SEQ, NKV * 128], BF16, addr_space="Shared").ap()
    sh_gm = nc.dram_tensor("sh_gm", [2, 16], F32, addr_space="Shared").ap()
    sh_fl = nc.dram_tensor("sh_fl", [2, 16], I32, addr_space="Shared").ap()
    xsp = nc.dram_tensor("xsp", [C, TB], F32).ap()
    qsp = nc.dram_tensor("qsp", [NQH * 128, TB], BF16).ap()

    es = ExitStack()

    def sb(name, shape, dt):
        return es.enter_context(nc.sbuf_tensor(name, shape, dt))

    with es:
        mt = sb("mt", [1, 16], I32)
        ident = sb("ident", [128, 128], BF16)
        ones_l = sb("ones_l", [128, 1], F32)
        ones_lb = sb("ones_lb", [128, 1], BF16)
        ones_b = sb("ones_b", [1, 128], F32)
        b_eln = sb("b_eln", [1, 1], F32)
        b_shift = sb("b_shift", [128, 1], F32)
        b_e8 = sb("b_e8", [1, 1], F32)
        b_eqk = sb("b_eqk", [1, 1], F32)

        xT_a = sb("xT_a", [128, CT * TB], F32)
        lnT_a = sb("lnT_a", [128, CT * TB], BF16)
        arena_t = sb("arena", [128, 33792], BF16)
        arena = arena_t[:, :]
        w12b = sb("w12b", [128, 4 * C], BF16)
        etb = sb("etb", [128, ETS * 512], BF16)
        sgb = sb("sgb", [128, TB], BF16)
        sqb = sb("sqb", [128, TB], F32)
        tmpf = sb("tmpf", [128, TB], F32)
        rowsA = sb("rowsA", [1, TB], F32)
        rowsB = sb("rowsB", [1, TB], F32)
        g_acc = sb("g_acc", [128, 56], F32)
        g3 = sb("g3", [128, 4], F32)
        gpart = sb("gpart", [1, 4], F32)
        gm2 = sb("gm2", [1, 8], F32)
        gmrow = sb("gmrow", [1, 4], F32)
        rinv = sb("rinv", [1, 4], F32)
        rcol = sb("rcol", [128, 4], F32)
        o_sb = sb("o_sb", [128, 128], BF16)
        rc1 = sb("rc1", [128, 1], F32)
        flg = sb("flg", [1, 32], I32)

        w3ch = arena[:, 0:CHUNK * C]
        hact = arena[:, CHUNK * C: CHUNK * C + CHUNK * TB]
        qT_a = arena[:, 0:NQH * TB]
        kTg = arena[:, 16384:16384 + SEQ]
        vg = arena[:, 24576:24576 + KT * 128]
        wob = arena[:, 0:2 * 4 * C]
        wkvb = arena[:, 16384:16384 + CT * 512]
        wqb = arena[:, 24576:24576 + 2 * C]
        vstage = arena[:, 28672:28672 + 8 * 512]
        arena_f32 = arena_t[:, :].bitcast(F32)
        kv_sq = [arena_f32[:, 0:1024], arena_f32[:, 1024:2048]]
        kv_ev = [arena_f32[:, 2048:3072], arena_f32[:, 3072:4096]]
        q_sq = [arena_f32[:, 8192:9216], arena_f32[:, 9216:10240]]
        q_ev = [arena_f32[:, 10240:11264], arena_f32[:, 11264:12288]]
        kout2 = vstage[:, 0:1024]
        qin0 = arena_f32[:, 0:2816]
        qin1 = arena_f32[:, 2816:5632]
        qpos = arena[:, 11264:11264 + 2816]
        qneg = arena[:, 14080:14080 + 2816]
        qout0 = arena[:, 16896:16896 + 2816]
        qout1 = arena[:, 19712:19712 + 2816]
        gdump = arena[:, 16896:16896 + 2816]

        def psum(name):
            return es.enter_context(nc.psum_tensor(name, [128, 1024], F32))
        p0 = psum("p0")
        p1 = psum("p1")
        p2 = psum("p2")
        p3 = psum("p3")

        sems = {}
        for s in ["s_meta", "s_x", "s_g", "s_q", "s_scr", "s_lw", "s_kv", "s_sp",
                  "s_out", "s_aw", "s_pl",
                  "c_po", "c_pe", "c_dv", "c_ac", "c_ev", "c_rdy"]:
            sems[s] = es.enter_context(nc.semaphore(s))
        r.sems = sems
        regs = {}

        r.emit_inc("sync", "s_meta", 16, lambda e: e.dma_start(out=mt[:], in_=meta))

        def init_sync(e):
            regs["s_par"] = e.alloc_register("s_par")
            e.wait_ge(sems["s_meta"], 16)
            e.reg_mov(regs["s_par"], e.value_load(mt[0:1, 0:1]))
        r.on("sync", init_sync)

        def init_gp(e):
            for nm in ("g_par", "g_nonce", "g_cnt", "g_w16", "g_t0", "g_go"):
                regs[nm] = e.alloc_register(nm)
            e.wait_ge(sems["s_meta"], 16)
            e.reg_mov(regs["g_par"], e.value_load(mt[0:1, 0:1]))
            e.reg_mov(regs["g_nonce"], e.value_load(mt[0:1, 1:2]))
            e.reg_mov(regs["g_cnt"], 0)
        r.on("gpsimd", init_gp)

        def setup_pool(e):
            e.memset(ident[:], 0.0)
            e.affine_select(out=ident[:], in_=ident[:], compare_op=ALU.not_equal,
                            fill=1.0, base=0, pattern=[[-1, 128]], channel_multiplier=1)
            e.memset(ones_l[:], 1.0)
            e.memset(ones_lb[:], 1.0)
            e.memset(ones_b[:], 1.0)
            e.memset(b_eln[:], EPS_LN)
            e.memset(b_shift[:], -SHIFT_C)
            e.memset(b_e8[:], EPS_BIT)
            e.memset(b_eqk[:], EPS_QK)
            e.memset(g_acc[:], 0.0)
        r.on("gpsimd", setup_pool)
        SETUP_PO = r.emit_inc("gpsimd", "c_po", 1, lambda e: e.engine_nop())

        def par_dma(out_even, out_odd, in_ap, sem):
            v = r.inc(sem, 16)
            def f(e, oe=out_even, oo=out_odd, ia=in_ap, s=sem):
                with e.If_eq(regs["s_par"], 0):
                    e.dma_start(out=oe, in_=ia).then_inc(r.sems[s], 16)
                with e.Else():
                    e.dma_start(out=oo, in_=ia).then_inc(r.sems[s], 16)
                e.end_ifs()
            r.on("sync", f)
            return v

        def flag_write(col):
            return par_dma(sh_fl[0:1, col:col + 1], sh_fl[1:2, col:col + 1],
                           mt[0:1, 1:2], "s_kv")

        def poll_both(col):
            v = r.inc("c_rdy", 1)
            def f(e, col=col):
                for row in (0, 1):
                    e.reg_mov(regs["g_go"], 1)
                    with e.While(regs["g_go"]):
                        e.reg_add(regs["g_cnt"], regs["g_cnt"], 1)
                        e.dma_start(out=flg.ap().rearrange("o (r c) -> o r c", r=2),
                                    in_=sh_fl.rearrange("(o r) c -> o r c", o=1)).then_inc(r.sems["s_pl"], 16)
                        e.reg_mov(regs["g_w16"], regs["g_cnt"])
                        e.reg_mul(regs["g_w16"], regs["g_w16"], 16)
                        e.wait_ge(r.sems["s_pl"], regs["g_w16"])
                        e.reg_mov(regs["g_t0"], e.value_load(flg[0:1, row * 16 + col: row * 16 + col + 1]))
                        e.reg_alu(regs["g_go"], regs["g_t0"], regs["g_nonce"], ALU.not_equal)
                e.engine_nop().then_inc(r.sems["c_rdy"], 1)
            r.on("gpsimd", f)
            return v

        def phase_barrier(engs=("tensor", "vector", "scalar")):
            for eng in engs:
                for s in ("c_pe", "c_dv", "c_ac", "c_ev", "s_kv", "s_sp", "s_scr"):
                    r.wait(eng, s)

        for ct in range(CT):
            r.emit_inc("sync", "s_x", 16,
                       lambda e, ct=ct: e.dma_start(out=xT_a[:, ct * TB:(ct + 1) * TB],
                                                    in_=xT[0, ct * 128:(ct + 1) * 128, :]))
        X_OWN = r.cur("s_x")

        def gamma_stage(jobs, gacc_lo, gacc_hi, g3_cols, gm_cols, flag_col, nw_scales):
            ac_base = r.cur("c_ac")
            r.wait("sync", "c_dv")
            r.wait("scalar", "s_scr")
            for i, (dram, b, hh, width, col) in enumerate(jobs):
                qbuf = qin0 if i % 2 == 0 else qin1
                if i >= 2:
                    r.wait("sync", "c_ac", ac_base + (i - 2) + 1)
                v = r.emit_inc("sync", "s_g", 16,
                               lambda e, b=b, hh=hh, d=dram, w=width, qb=qbuf:
                               e.dma_start(out=qb[:, 0:w],
                                           in_=d[b * 128:(b + 1) * 128, hh * 2816: hh * 2816 + w]))
                r.wait("scalar", "s_g", v)
                if i == 0:
                    r.wait("scalar", "c_po", SETUP_PO)
                r.emit_inc("scalar", "c_ac", 1,
                           lambda e, w=width, col=col, qb=qbuf:
                           e.activation(out=gdump[:, 0:w], in_=qb[:, 0:w],
                                        func=AF.Abs, accum_out=g_acc[:, col:col + 1]))
            GA = r.cur("c_ac")
            r.wait("vector", "c_ac", GA)
            for ci, (lo, hi) in enumerate(zip(gacc_lo, gacc_hi)):
                r.emit_inc("vector", "c_dv", 1,
                           lambda e, lo=lo, hi=hi, gc=g3_cols[ci]:
                           e.reduce_sum(g3[:, gc:gc + 1], g_acc[:, lo:hi], axis=mybir.AxisListType.X))
            GS = r.cur("c_dv")
            r.wait("tensor", "c_dv", GS)
            r.wait("tensor", "c_po", SETUP_PO)
            nclz = len(g3_cols)
            gc0 = g3_cols[0]
            v = r.emit_inc("tensor", "c_pe", 1,
                           lambda e, gc0=gc0, n=nclz: e.matmul(out=p0[0:1, gc0:gc0 + n], lhsT=ones_l[:],
                                                               rhs=g3[:, gc0:gc0 + n], start=True, stop=True))
            r.wait("vector", "c_pe", v)
            GP = r.emit_inc("vector", "c_dv", 1,
                            lambda e, gc0=gc0, n=nclz: e.tensor_copy(out=gpart[0:1, 0:n], in_=p0[0:1, gc0:gc0 + n]))
            r.wait("sync", "c_dv", GP)
            par_dma(sh_gm[0:1, gm_cols[0]:gm_cols[1]], sh_gm[1:2, gm_cols[0]:gm_cols[1]],
                    gpart[0:1, 0:nclz], "s_kv")
            r.wait("sync", "s_kv")
            flag_write(flag_col)
            RDY = poll_both(flag_col)
            r.wait("sync", "c_rdy", RDY)
            GM2 = r.emit_inc("sync", "s_q", 16,
                             lambda e, a=gm_cols[0]:
                             e.dma_start(out=gm2.ap().rearrange("o (r c) -> o r c", r=2),
                                         in_=sh_gm[0:2, a:a + 4].rearrange("(o r) c -> o r c", o=1)))
            r.wait("vector", "s_q", GM2)
            GT = r.emit_inc("vector", "c_dv", 1,
                            lambda e, n=nclz: e.tensor_add(gmrow[0:1, 0:n], gm2[0:1, 0:n], gm2[0:1, 4:4 + n]))
            r.wait("scalar", "c_dv", GT)
            for ci, sc in enumerate(nw_scales):
                r.emit_inc("scalar", "c_ac", 1,
                           lambda e, ci=ci, sc=sc: e.activation(out=rinv[0:1, ci:ci + 1], in_=gmrow[0:1, ci:ci + 1],
                                                                func=AF.Identity, scale=sc, bias=b_e8[:, :]))
            r.emit_inc("scalar", "c_ac", 1,
                       lambda e, n=nclz: e.activation(out=rinv[0:1, 0:n], in_=rinv[0:1, 0:n], func=AF.Ln))
            RI = r.emit_inc("scalar", "c_ac", 1,
                            lambda e, n=nclz: e.activation(out=rinv[0:1, 0:n], in_=rinv[0:1, 0:n],
                                                           func=AF.Exp, scale=-1.0))
            r.wait("tensor", "c_ac", RI)
            v = r.emit_inc("tensor", "c_pe", 1,
                           lambda e, n=nclz: e.matmul(out=p0[:, 8:8 + n], lhsT=ones_b[:, :],
                                                      rhs=rinv[0:1, 0:n], start=True, stop=True))
            r.wait("vector", "c_pe", v)
            RC = r.emit_inc("vector", "c_dv", 1,
                            lambda e, gc0=g3_cols[0], n=nclz:
                            e.tensor_copy(out=rcol[:, gc0:gc0 + n], in_=p0[:, 8:8 + n]))
            return RC

        def quant_stage(jobs, rc_ready, flag_map, ibase):
            dv_done = {}
            scr_done = {}
            r.wait("vector", "c_dv", rc_ready)
            r.wait("sync", "c_ac")
            r.wait("vector", "s_scr")
            for i0, (wn, dram, b, hh, width, shdst, half) in enumerate(jobs):
                i = ibase + i0
                qbuf = qin0 if i % 2 == 0 else qin1
                obuf = qout0 if i % 2 == 0 else qout1
                wi = {"w1": 0, "w2": 1, "w3": 2}[wn]
                if i0 >= 2:
                    r.wait("sync", "c_dv", dv_done[i0 - 2])
                v = r.emit_inc("sync", "s_q", 16,
                               lambda e, b=b, hh=hh, d=dram, w=width, qb=qbuf:
                               e.dma_start(out=qb[:, 0:w],
                                           in_=d[b * 128:(b + 1) * 128, hh * 2816: hh * 2816 + w]))
                r.wait("vector", "s_q", v)
                if i0 >= 2:
                    r.wait("vector", "s_scr", scr_done[i0 - 2])
                r.emit_inc("vector", "c_dv", 1,
                           lambda e, w=width, wi=wi, qb=qbuf:
                           e.tensor_scalar(out=qpos[:, 0:w], in0=qb[:, 0:w],
                                           scalar1=rcol[:, wi:wi + 1], scalar2=0.5,
                                           op0=ALU.mult, op1=ALU.is_ge))
                r.emit_inc("vector", "c_dv", 1,
                           lambda e, w=width, wi=wi, qb=qbuf:
                           e.tensor_scalar(out=qneg[:, 0:w], in0=qb[:, 0:w],
                                           scalar1=rcol[:, wi:wi + 1], scalar2=-0.5,
                                           op0=ALU.mult, op1=ALU.is_le))
                r.emit_inc("vector", "c_dv", 1,
                           lambda e, w=width, ob=obuf:
                           e.tensor_sub(ob[:, 0:w], qpos[:, 0:w], qneg[:, 0:w]))
                if wn == "w3":
                    r.emit_inc("vector", "c_dv", 1,
                               lambda e, w=width, ob=obuf:
                               e.tensor_scalar_mul(ob[:, 0:w], ob[:, 0:w], 0.5))
                dv_done[i0] = r.cur("c_dv")
                r.wait("sync", "c_dv", dv_done[i0])
                if wn == "w3":
                    scr_done[i0] = par_dma(shdst[b * 128:(b + 1) * 128, :],
                                           shdst[half + b * 128: half + (b + 1) * 128, :],
                                           obuf[:, 0:width], "s_scr")
                else:
                    scr_done[i0] = par_dma(
                        shdst[b * 128:(b + 1) * 128, hh * 2816: (hh + 1) * 2816],
                        shdst[half + b * 128: half + (b + 1) * 128, hh * 2816: (hh + 1) * 2816],
                        obuf[:, 0:width], "s_scr")
                if (wn, b, hh) in flag_map:
                    r.wait("sync", "s_scr")
                    flag_write(flag_map[(wn, b, hh)])
            return ibase + len(jobs)

        g12jobs = [(w1th, b, hh, 2816, b * 2 + hh) for b in range(8) for hh in range(2)] + \
                  [(w2th, b, hh, 2816, 16 + b * 2 + hh) for b in range(8) for hh in range(2)]
        g3jobs = [(w3th, b, 0, C, 32 + b) for b in range(22)]
        q12jobs = [("w1", w1th, b, hh, 2816, sh_w1T, 1024) for b in range(8) for hh in range(2)] + \
                  [("w2", w2th, b, hh, 2816, sh_w2T, 1024) for b in range(8) for hh in range(2)]
        q3jobs = [("w3", w3th, b, 0, C, sh_w3T, 2816) for b in range(22)]

        def emit_ln():
            phase_barrier()
            pe_hist = {}
            for ct in range(CT):
                sq_t = sqb if ct % 2 == 0 else tmpf
                if ct >= 2:
                    r.wait("scalar", "c_pe", pe_hist[ct - 2])
                va = r.emit_inc("scalar", "c_ac", 1,
                                lambda e, ct=ct, sq_t=sq_t: e.activation(out=sq_t[:, :],
                                                              in_=xT_a[:, ct * TB:(ct + 1) * TB],
                                                              func=AF.Square))
                r.wait("tensor", "c_ac", va)
                for th in range(2):
                    r.emit_inc("tensor", "c_pe", 1,
                               lambda e, ct=ct, th=th:
                               e.matmul(out=p2[0:1, th * 512:(th + 1) * 512], lhsT=ones_l[:],
                                        rhs=xT_a[:, ct * TB + th * 512: ct * TB + (th + 1) * 512],
                                        start=(ct == 0), stop=(ct == CT - 1)))
                    r.emit_inc("tensor", "c_pe", 1,
                               lambda e, th=th, ct=ct, sq_t=sq_t:
                               e.matmul(out=p3[0:1, th * 512:(th + 1) * 512], lhsT=ones_l[:],
                                        rhs=sq_t[:, th * 512:(th + 1) * 512],
                                        start=(ct == 0), stop=(ct == CT - 1)))
                pe_hist[ct] = r.cur("c_pe")
            r.wait("vector", "c_pe")
            r.emit_inc("vector", "c_dv", 1, lambda e: e.tensor_scalar_mul(rowsA[:, :], p2[0:1, :], 1.0 / C))
            r.emit_inc("vector", "c_dv", 1, lambda e: e.tensor_scalar_mul(rowsB[:, :], p3[0:1, :], 1.0 / C))
            r.emit_inc("vector", "c_dv", 1, lambda e: e.tensor_mul(sqb[0:1, 0:TB], rowsA[:, :], rowsA[:, :]))
            v_dv = r.emit_inc("vector", "c_dv", 1,
                              lambda e: e.tensor_sub(rowsB[:, :], rowsB[:, :], sqb[0:1, 0:TB]))
            r.wait("scalar", "c_dv", v_dv)
            r.emit_inc("scalar", "c_ac", 1,
                       lambda e: e.activation(out=rowsB[:, :], in_=rowsB[:, :], func=AF.Ln, bias=b_eln[:, :]))
            v_ac = r.emit_inc("scalar", "c_ac", 1,
                              lambda e: e.activation(out=rowsB[:, :], in_=rowsB[:, :], func=AF.Exp, scale=-0.5))
            r.wait("tensor", "c_ac", v_ac)
            for th in range(2):
                r.emit_inc("tensor", "c_pe", 1,
                           lambda e, th=th: e.matmul(out=p2[:, th * 512:(th + 1) * 512], lhsT=ones_b[:, :],
                                                     rhs=rowsA[0:1, th * 512:(th + 1) * 512], start=True, stop=True))
                r.emit_inc("tensor", "c_pe", 1,
                           lambda e, th=th: e.matmul(out=p3[:, th * 512:(th + 1) * 512], lhsT=ones_b[:, :],
                                                     rhs=rowsB[0:1, th * 512:(th + 1) * 512], start=True, stop=True))
            r.wait("vector", "c_pe")
            for ct in range(CT):
                for th in range(2):
                    r.emit_inc("vector", "c_dv", 1,
                               lambda e, ct=ct, th=th:
                               e.tensor_sub(tmpf[:, th * 512:(th + 1) * 512],
                                            xT_a[:, ct * TB + th * 512: ct * TB + (th + 1) * 512],
                                            p2[:, th * 512:(th + 1) * 512]))
                    r.emit_inc("vector", "c_dv", 1,
                               lambda e, ct=ct, th=th:
                               e.tensor_mul(lnT_a[:, ct * TB + th * 512: ct * TB + (th + 1) * 512],
                                            tmpf[:, th * 512:(th + 1) * 512],
                                            p3[:, th * 512:(th + 1) * 512]))
            return r.cur("c_dv")

        def emit_ffn_pass(ln_dv, pending=None):
            pending = list(pending or [])
            defer_w3 = bool(pending)
            phase_barrier(("tensor",))
            r.wait("sync", "c_rdy")
            r.wait("sync", "c_pe")
            r.wait("sync", "c_dv")
            hj_pe = {}
            sig_ac = {}
            hct_dv = {}
            y_done_prev = 0
            for chn in range(NCHUNK):
                for j in range(CHUNK):
                    jj = chn * CHUNK + j
                    buf = jj % 2
                    if jj >= 2:
                        r.wait("sync", "c_pe", hj_pe[jj - 2])
                    r.emit_inc("sync", "s_lw", 16,
                               lambda e, jj=jj, buf=buf:
                               e.dma_start(out=w12b[:, buf * C:(buf + 1) * C]
                                           .rearrange("p (ct h) -> p ct h", ct=CT),
                                           in_=sh_w1T[:, jj * 128:(jj + 1) * 128]
                                           .rearrange("(ct p) h -> p ct h", p=128)))
                    v12 = r.emit_inc("sync", "s_lw", 16,
                                     lambda e, jj=jj, buf=buf:
                                     e.dma_start(out=w12b[:, (2 + buf) * C:(3 + buf) * C]
                                                 .rearrange("p (ct h) -> p ct h", ct=CT),
                                                 in_=sh_w2T[:, jj * 128:(jj + 1) * 128]
                                                 .rearrange("(ct p) h -> p ct h", p=128)))
                    if chn == 0 and defer_w3:
                        pass
                    else:
                        if chn >= 1:
                            r.wait("sync", "c_pe", y_done_prev)
                        r.emit_inc("sync", "s_lw", 16,
                                   lambda e, jj=jj, j=j:
                                   e.dma_start(out=w3ch[:, j * C:(j + 1) * C],
                                               in_=sh_w3T[jj * 128:(jj + 1) * 128, :]))
                    r.wait("tensor", "s_lw", v12)
                    if jj == 0:
                        r.wait("tensor", "c_dv", ln_dv)
                    if jj >= 1:
                        r.wait("tensor", "c_ac", sig_ac[jj - 1])
                    for th in range(2):
                        for ct in range(CT):
                            r.emit_inc("tensor", "c_pe", 1,
                                       lambda e, th=th, ct=ct, buf=buf:
                                       e.matmul(out=p0[:, th * 512:(th + 1) * 512],
                                                lhsT=w12b[:, buf * C + ct * 128: buf * C + (ct + 1) * 128],
                                                rhs=lnT_a[:, ct * TB + th * 512: ct * TB + (th + 1) * 512],
                                                start=(ct == 0), stop=(ct == CT - 1)))
                    v_x1 = r.cur("c_pe")
                    if jj >= 1:
                        r.wait("tensor", "c_dv", hct_dv[jj - 1])
                    for th in range(2):
                        for ct in range(CT):
                            r.emit_inc("tensor", "c_pe", 1,
                                       lambda e, th=th, ct=ct, buf=buf:
                                       e.matmul(out=p1[:, th * 512:(th + 1) * 512],
                                                lhsT=w12b[:, (2 + buf) * C + ct * 128: (2 + buf) * C + (ct + 1) * 128],
                                                rhs=lnT_a[:, ct * TB + th * 512: ct * TB + (th + 1) * 512],
                                                start=(ct == 0), stop=(ct == CT - 1)))
                    hj_pe[jj] = r.cur("c_pe")
                    r.wait("scalar", "c_pe", v_x1)
                    if jj >= 1:
                        r.wait("scalar", "c_dv", hct_dv[jj - 1])
                    sig_ac[jj] = r.emit_inc("scalar", "c_ac", 1,
                                            lambda e: e.activation(out=sgb[:, :], in_=p0[:, :], func=AF.Sigmoid))
                    r.wait("vector", "c_ac", sig_ac[jj])
                    r.wait("vector", "c_pe", hj_pe[jj])
                    hct_dv[jj] = r.emit_inc("vector", "c_dv", 1,
                                            lambda e, j=j: e.tensor_mul(hact[:, j * TB:(j + 1) * TB],
                                                                        sgb[:, :], p1[:, :]))
                    if pending and chn == 0:
                        if j == 6:
                            r.wait("vector", "s_kv")
                        for _ in range(3):
                            if pending:
                                pending.pop(0)()
                while pending:
                    pending.pop(0)()
                if chn == 0 and defer_w3:
                    r.wait("sync", "c_dv")
                    r.wait("sync", "c_ac")
                    for j2 in range(CHUNK):
                        jj2 = chn * CHUNK + j2
                        r.emit_inc("sync", "s_lw", 16,
                                   lambda e, jj2=jj2, j2=j2:
                                   e.dma_start(out=w3ch[:, j2 * C:(j2 + 1) * C],
                                               in_=sh_w3T[jj2 * 128:(jj2 + 1) * 128, :]))
                if chn == 0:
                    r.wait("tensor", "c_dv")
                    r.wait("tensor", "c_ac")
                r.wait("tensor", "c_dv", hct_dv[chn * CHUNK + CHUNK - 1])
                r.wait("tensor", "s_lw")
                yev = {}
                for ct in range(CT):
                    py = p2 if ct % 2 == 0 else p3
                    if ct >= 2:
                        r.wait("tensor", "c_ev", yev[ct - 2])
                    for j in range(CHUNK):
                        for th in range(2):
                            r.emit_inc("tensor", "c_pe", 1,
                                       lambda e, j=j, th=th, ct=ct, py=py:
                                       e.matmul(out=py[:, th * 512:(th + 1) * 512],
                                                lhsT=w3ch[:, j * C + ct * 128: j * C + (ct + 1) * 128],
                                                rhs=hact[:, j * TB + th * 512: j * TB + (th + 1) * 512],
                                                start=(j == 0), stop=(j == CHUNK - 1)))
                    v_y = r.cur("c_pe")
                    r.wait("vector", "c_pe", v_y)
                    for th in range(2):
                        r.emit_inc("vector", "c_dv", 1,
                                   lambda e, ct=ct, th=th, py=py:
                                   e.tensor_add(xT_a[:, ct * TB + th * 512: ct * TB + (th + 1) * 512],
                                                xT_a[:, ct * TB + th * 512: ct * TB + (th + 1) * 512],
                                                py[:, th * 512:(th + 1) * 512]))
                    yev[ct] = r.emit_inc("vector", "c_ev", 1, lambda e: e.engine_nop())
                y_done_prev = r.cur("c_pe")
            return r.cur("c_ev")

        kraw = [arena[:, 16384 + i * TB: 16384 + (i + 1) * TB] for i in range(NKV)]
        wkv2 = arena[:, 24576:24576 + CT * 512]

        def emit_qk(slot, ln_dv, do_q):
            phase_barrier()
            r.wait("gpsimd", "c_pe")
            r.wait("gpsimd", "c_dv")
            tiles = []
            if do_q:
                tiles += [("q", h) for h in range(NQH)]
            tiles += [("k", g) for g in range(NKV)]
            loads = {}
            evac_v = {}
            kload_v = None
            for i, (kind, idx) in enumerate(tiles):
                pp = (p0, p1)[i % 2]
                if kind == "q":
                    buf = i % 2
                    if i >= 2:
                        r.wait("gpsimd", "c_pe", evac_v.get(i - 2, (0, 0))[1])
                    loads[i] = r.emit_inc("gpsimd", "s_aw", 16,
                                          lambda e, idx=idx, buf=buf:
                                          e.dma_start(out=wqb[:, buf * C:(buf + 1) * C]
                                                      .rearrange("p (ct d) -> p ct d", ct=CT),
                                                      in_=wq[:, idx * 128:(idx + 1) * 128]
                                                      .rearrange("(ct p) d -> p ct d", p=128)))
                elif kload_v is None:
                    if do_q:
                        r.wait("gpsimd", "c_pe", evac_v[i - 1][1])
                    kload_v = r.emit_inc("gpsimd", "s_aw", 16,
                                         lambda e: e.dma_start(
                                             out=wkv2.rearrange("p (ct d) -> p ct d", ct=CT),
                                             in_=wk.rearrange("(ct p) d -> p ct d", p=128)))
                    loads[i] = kload_v
                else:
                    loads[i] = kload_v
                r.wait("tensor", "s_aw", loads[i])
                if i == 0:
                    r.wait("tensor", "c_dv", ln_dv)
                if i >= 2:
                    r.wait("tensor", "c_ac", evac_v[i - 2][0])
                for th in range(2):
                    for ct in range(CT):
                        def mk(e, kind=kind, idx=idx, th=th, ct=ct, pp=pp, buf=(i % 2)):
                            if kind == "q":
                                lh = wqb[:, buf * C + ct * 128: buf * C + (ct + 1) * 128]
                            else:
                                lh = wkv2[:, ct * 512 + idx * 128: ct * 512 + idx * 128 + 128]
                            return e.matmul(out=pp[:, th * 512:(th + 1) * 512], lhsT=lh,
                                            rhs=lnT_a[:, ct * TB + th * 512: ct * TB + (th + 1) * 512],
                                            start=(ct == 0), stop=(ct == CT - 1))
                        r.emit_inc("tensor", "c_pe", 1, mk)
                proj_v = r.cur("c_pe")
                r.wait("scalar", "c_pe", proj_v)
                dst = qT_a[:, idx * TB:(idx + 1) * TB] if kind == "q" else kraw[idx]
                va = r.emit_inc("scalar", "c_ac", 1,
                                lambda e, pp=pp, dst=dst: e.activation(out=dst, in_=pp[:, :], func=AF.Identity))
                evac_v[i] = (va, proj_v)
            EV_ALL = r.cur("c_ac")
            r.wait("gpsimd", "c_pe")
            v_wv = r.emit_inc("gpsimd", "s_aw", 16,
                              lambda e: e.dma_start(
                                  out=wkvb.rearrange("p (ct d) -> p ct d", ct=CT),
                                  in_=wv.rearrange("(ct p) d -> p ct d", p=128)))
            vdv = {}
            for tt in range(8):
                pv = p2 if tt % 2 == 0 else p3
                r.wait("tensor", "s_aw", v_wv)
                r.wait("tensor", "c_dv", ln_dv)
                if tt >= 2:
                    r.wait("tensor", "c_dv", vdv[tt - 2])
                for ct in range(CT):
                    r.emit_inc("tensor", "c_pe", 1,
                               lambda e, tt=tt, ct=ct, pv=pv:
                               e.matmul(out=pv[:, 0:512],
                                        lhsT=lnT_a[:, ct * TB + tt * 128: ct * TB + tt * 128 + 128],
                                        rhs=wkvb[:, ct * 512:(ct + 1) * 512],
                                        start=(ct == 0), stop=(ct == CT - 1)))
                v_v = r.cur("c_pe")
                r.wait("vector", "c_pe", v_v)
                if tt == 0:
                    r.wait("vector", "s_kv")
                vdv[tt] = r.emit_inc("vector", "c_dv", 1,
                                     lambda e, tt=tt, pv=pv:
                                     e.tensor_copy(out=vstage[:, tt * 512:(tt + 1) * 512], in_=pv[:, 0:512]))
            def v_write_cl():
                r.wait("sync", "c_dv", vdv[7])
                par_dma(sh_v[slot * TB:(slot + 1) * TB, :].rearrange("(a p) d -> p a d", p=128),
                        sh_v[4096 + slot * TB: 4096 + (slot + 1) * TB, :].rearrange("(a p) d -> p a d", p=128),
                        vstage.rearrange("p (a d) -> p a d", a=8), "s_kv")
            pending = [v_write_cl]
            sq_v = {}
            ss_v = {}
            le_v = {}
            bc_v = {}
            mu_v = {}
            wr_v = {}

            def tile_chain(i, kind, idx):
                pn = (p2, p3)[i % 2]
                sq_t = (sqb, tmpf)[i % 2]
                row_t = (rowsA, rowsB)[i % 2]
                src_t = qT_a[:, idx * TB:(idx + 1) * TB] if kind == "q" else kraw[idx]
                r.wait("scalar", "c_ac", evac_v[i][0])
                if i >= 2:
                    r.wait("scalar", "c_pe", ss_v[i - 2])
                sq_v[i] = r.emit_inc("scalar", "c_ac", 1,
                                     lambda e, src_t=src_t, sq_t=sq_t:
                                     e.activation(out=sq_t[:, :], in_=src_t, func=AF.Square))
                r.wait("tensor", "c_ac", sq_v[i])
                if i >= 2:
                    r.wait("tensor", "c_dv", mu_v[i - 2])
                elif i < 2:
                    r.wait("tensor", "c_dv", vdv[7] if (6 + i) not in vdv else vdv[6 + i])
                for th in range(2):
                    r.emit_inc("tensor", "c_pe", 1,
                               lambda e, th=th, sq_t=sq_t, pn=pn:
                               e.matmul(out=pn[0:1, th * 512:(th + 1) * 512], lhsT=ones_l[:],
                                        rhs=sq_t[:, th * 512:(th + 1) * 512], start=True, stop=True))
                ss_v[i] = r.cur("c_pe")
                r.wait("scalar", "c_pe", ss_v[i])
                if i >= 2:
                    r.wait("scalar", "c_pe", bc_v[i - 2])
                r.emit_inc("scalar", "c_ac", 1,
                           lambda e, pn=pn, row_t=row_t: e.activation(out=row_t[:, :], in_=pn[0:1, :], func=AF.Ln,
                                                                      scale=1.0 / D, bias=b_eqk[:, :]))
                le_v[i] = r.emit_inc("scalar", "c_ac", 1,
                                     lambda e, row_t=row_t: e.activation(out=row_t[:, :], in_=row_t[:, :],
                                                                         func=AF.Exp, scale=-0.5))
                r.wait("tensor", "c_ac", le_v[i])
                for th in range(2):
                    r.emit_inc("tensor", "c_pe", 1,
                               lambda e, th=th, pn=pn, row_t=row_t:
                               e.matmul(out=pn[:, th * 512:(th + 1) * 512], lhsT=ones_b[:, :],
                                        rhs=row_t[0:1, th * 512:(th + 1) * 512], start=True, stop=True))
                bc_v[i] = r.cur("c_pe")
                r.wait("vector", "c_pe", bc_v[i])
                if kind == "k" and i >= 2 and (i - 2) in wr_v:
                    r.wait("vector", "s_kv", wr_v[i - 2])
                mu_v[i] = r.emit_inc("vector", "c_dv", 1,
                                     lambda e, src_t=src_t, pn=pn:
                                     e.tensor_mul(src_t, src_t, pn[:, :]))
                if kind == "k":
                    r.wait("sync", "c_dv", mu_v[i])
                    wr_v[i] = par_dma(
                        sh_kT[idx * 128:(idx + 1) * 128, slot * TB:(slot + 1) * TB],
                        sh_kT[idx * 128:(idx + 1) * 128, 4096 + slot * TB: 4096 + (slot + 1) * TB],
                        src_t, "s_kv")

            for i, (kind, idx) in enumerate(tiles):
                pending.append(lambda i=i, kind=kind, idx=idx: tile_chain(i, kind, idx))

            def flag_cl():
                r.wait("sync", "s_kv")
                flag_write(4 + slot)
            pending.append(flag_cl)
            if do_q:
                def spill_q_cl():
                    r.wait("sync", "c_dv", mu_v[NQH - 1] if (NQH - 1) in mu_v else r.cur("c_dv"))
                    for h in range(NQH):
                        r.wait("sync", "c_dv", mu_v.get(h, 0))
                        r.emit_inc("sync", "s_sp", 16,
                                   lambda e, h=h: e.dma_start(out=qsp[h * 128:(h + 1) * 128, :],
                                                              in_=qT_a[:, h * TB:(h + 1) * TB]))
                pending.append(spill_q_cl)
            return pending

        for eng in ("scalar", "tensor", "vector"):
            r.wait(eng, "s_x", X_OWN)
        ln1 = emit_ln()
        RC12 = gamma_stage(g12jobs, [0, 16], [16, 32], [0, 1], (0, 2), 0, [1.0 / NW_12, 1.0 / NW_12])
        nxt = quant_stage(q12jobs, RC12, {("w1", 7, 1): 1, ("w2", 7, 1): 2}, 0)
        poll_both(1)
        poll_both(2)
        RDY12 = r.cur("c_rdy")
        RC3 = gamma_stage(g3jobs, [32], [54], [2], (4, 5), 8, [1.0 / NW_3])
        quant_stage(q3jobs, RC3, {("w3", 21, 0): 3}, nxt)
        poll_both(3)
        emit_ffn_pass(ln1)
        ln2 = emit_ln()
        ln2_marks = (r.cur("c_dv"), r.cur("c_ac"), r.cur("c_pe"))
        own_pending = emit_qk(0, ln2, do_q=True)
        r.wait("sync", "c_dv", ln2_marks[0])
        r.wait("sync", "c_ac", ln2_marks[1])
        r.wait("sync", "c_pe", ln2_marks[2])
        for ct in range(CT):
            r.emit_inc("sync", "s_sp", 16,
                       lambda e, ct=ct: e.dma_start(out=xsp[ct * 128:(ct + 1) * 128, :],
                                                    in_=xT_a[:, ct * TB:(ct + 1) * TB]))
        SPILL_V = None

        prev_marks = ln2_marks
        pend = own_pending
        for blk in range(1, NBLK):
            r.wait("sync", "c_dv", prev_marks[0])
            r.wait("sync", "c_ac", prev_marks[1])
            r.wait("sync", "c_pe", prev_marks[2])
            for ct in range(CT):
                r.emit_inc("sync", "s_x", 16,
                           lambda e, ct=ct, blk=blk: e.dma_start(out=xT_a[:, ct * TB:(ct + 1) * TB],
                                                                 in_=xT[blk, ct * 128:(ct + 1) * 128, :]))
            vx = r.cur("s_x")
            for eng in ("scalar", "tensor", "vector"):
                r.wait(eng, "s_x", vx)
            lnb = emit_ln()
            emit_ffn_pass(lnb, pending=pend)
            ln2b = emit_ln()
            prev_marks = (r.cur("c_dv"), r.cur("c_ac"), r.cur("c_pe"))
            pend = emit_qk(blk, ln2b, do_q=False)
        while pend:
            pend.pop(0)()

        for s_ in range(4):
            poll_both(4 + s_)
        KV_RDY = r.cur("c_rdy")
        r.wait("sync", "c_rdy", KV_RDY)
        for s_ in ("c_dv", "c_ev", "c_pe", "c_ac"):
            r.wait("sync", s_)
        for h in range(NQH):
            r.emit_inc("sync", "s_q", 16,
                       lambda e, h=h: e.dma_start(out=qT_a[:, h * TB:(h + 1) * TB],
                                                  in_=qsp[h * 128:(h + 1) * 128, :]))
        QT_V = r.cur("s_q")
        for ct in range(CT):
            r.emit_inc("sync", "s_sp", 16,
                       lambda e, ct=ct: e.dma_start(out=xT_a[:, ct * TB:(ct + 1) * TB],
                                                    in_=xsp[ct * 128:(ct + 1) * 128, :]))
        XRL_V = r.cur("s_sp")

        ot_dv = 0
        exp_vals = []
        otm_vals = {}
        it = 0
        for g in range(NKV):
            if g >= 1:
                r.wait("sync", "c_pe")
            r.emit_inc("sync", "s_q", 16,
                       lambda e, g=g: e.dma_start(out=kTg, in_=sh_kT[g * 128:(g + 1) * 128, :]))
            KV_V = r.emit_inc("sync", "s_q", 16,
                              lambda e, g=g: e.dma_start(
                                  out=vg.rearrange("p (kt d) -> p kt d", kt=KT),
                                  in_=sh_v[:, g * 128:(g + 1) * 128].rearrange("(kt p) d -> p kt d", p=128)))
            for h4 in range(4):
                h = g * 4 + h4
                for qb in range(2):
                    pX = (p2, p3)[it % 2]
                    if it == 0:
                        r.wait("tensor", "s_q", max(KV_V, QT_V))
                        r.wait("tensor", "c_ac")
                        r.wait("tensor", "c_dv")
                    elif qb == 0 and h4 == 0:
                        r.wait("tensor", "s_q", KV_V)

                    def emit_pv(p_, first, pX=pX):
                        r.wait("tensor", "c_ac", exp_vals[-1 - ((len(exp_vals) - 1) - p_ - it * 32)]
                               if False else exp_vals[it * 32 + p_])
                        if first and it >= 2:
                            r.wait("tensor", "c_dv", otm_vals[it - 2])
                        for k2 in (2 * p_, 2 * p_ + 1):
                            r.emit_inc("tensor", "c_pe", 1,
                                       lambda e, k2=k2, pX=pX:
                                       e.matmul(out=pX[:, 0:512],
                                                lhsT=vg[:, k2 * 128:(k2 + 1) * 128],
                                                rhs=etb[:, (k2 % ETS) * 512:((k2 % ETS) + 1) * 512],
                                                start=(k2 == 0), stop=(k2 == KT - 1)))
                            r.emit_inc("tensor", "c_pe", 1,
                                       lambda e, k2=k2, pX=pX:
                                       e.matmul(out=pX[0:1, 512:1024],
                                                lhsT=ones_lb[:],
                                                rhs=etb[:, (k2 % ETS) * 512:((k2 % ETS) + 1) * 512],
                                                start=(k2 == 0), stop=(k2 == KT - 1)))

                    for p_ in range(KT // 2):
                        for half in (0, 1):
                            kt = 2 * p_ + half
                            gidx = it * KT + kt
                            stg = p0 if p_ % 2 == 0 else p1
                            if gidx >= 4 and (gidx - 4) // 2 < len(exp_vals):
                                r.wait("tensor", "c_ac", exp_vals[(gidx - 4) // 2])
                            r.emit_inc("tensor", "c_pe", 1,
                                       lambda e, kt=kt, h=h, qb=qb, stg=stg, half=half:
                                       e.matmul(out=stg[:, half * 512:(half + 1) * 512],
                                                lhsT=kTg[:, kt * 128:(kt + 1) * 128],
                                                rhs=qT_a[:, h * TB + qb * 512: h * TB + (qb + 1) * 512],
                                                start=True, stop=True))
                        v_st = r.cur("c_pe")
                        slot = (2 * p_) % ETS
                        stg = p0 if p_ % 2 == 0 else p1
                        r.wait("scalar", "c_pe", v_st)
                        va = r.emit_inc("scalar", "c_ac", 1,
                                        lambda e, slot=slot, stg=stg:
                                        e.activation(out=etb[:, slot * 512: (slot + 2) * 512],
                                                     in_=stg[:, :], func=AF.Exp,
                                                     scale=INV_SQRT_D, bias=b_shift[:, :]))
                        exp_vals.append(va)
                        if p_ >= 1:
                            emit_pv(p_ - 1, first=(p_ == 1))
                    emit_pv(KT // 2 - 1, first=False)
                    v_pv = r.cur("c_pe")
                    r.wait("scalar", "c_pe", v_pv)
                    r.emit_inc("scalar", "c_ac", 1,
                               lambda e, pX=pX: e.activation(out=rowsB[0:1, 0:512], in_=pX[0:1, 512:1024],
                                                             func=AF.Ln))
                    v_rc = r.emit_inc("scalar", "c_ac", 1,
                                      lambda e: e.activation(out=rowsB[0:1, 0:512], in_=rowsB[0:1, 0:512],
                                                             func=AF.Exp, scale=-1.0))
                    r.wait("tensor", "c_ac", v_rc)
                    v_bc = r.emit_inc("tensor", "c_pe", 1,
                                      lambda e, pX=pX: e.matmul(out=pX[:, 512:1024], lhsT=ones_b[:, :],
                                                                rhs=rowsB[0:1, 0:512], start=True, stop=True))
                    r.wait("vector", "c_pe", v_bc)
                    r.emit_inc("vector", "c_dv", 1,
                               lambda e, pX=pX: e.tensor_copy(out=tmpf[:, 0:512], in_=pX[:, 0:512]))
                    otm_vals[it] = r.emit_inc("vector", "c_dv", 1,
                                              lambda e, h=h, qb=qb, pX=pX:
                                              e.tensor_mul(lnT_a[:, h * TB + qb * 512: h * TB + (qb + 1) * 512],
                                                           tmpf[:, 0:512], pX[:, 512:1024]))
                    it += 1
        ot_dv = otm_vals[it - 1]

        wo_pe = {}
        for chw in range(4):
            buf = chw % 2
            if chw <= 1:
                r.wait("gpsimd", "c_dv", ot_dv)
            else:
                r.wait("gpsimd", "c_pe", wo_pe[chw - 2])
            v_wo = r.emit_inc("gpsimd", "s_aw", 16,
                              lambda e, chw=chw, buf=buf:
                              e.dma_start(out=wob[:, buf * 4 * C:(buf + 1) * 4 * C]
                                          .rearrange("p (b c) -> p b c", b=4),
                                          in_=wo[chw * 512:(chw + 1) * 512, :]
                                          .rearrange("(b p) c -> p b c", p=128)))
            r.wait("tensor", "s_aw", v_wo)
            r.wait("tensor", "c_dv", ot_dv)
            if chw == 0:
                r.wait("tensor", "s_sp", XRL_V)
                r.wait("vector", "s_sp", XRL_V)
            yev = {}
            for ct in range(CT):
                pyw = p2 if ct % 2 == 0 else p3
                if ct >= 2:
                    r.wait("tensor", "c_ev", yev[ct - 2])
                for b4 in range(4):
                    hd = chw * 4 + b4
                    for th in range(2):
                        r.emit_inc("tensor", "c_pe", 1,
                                   lambda e, b4=b4, hd=hd, th=th, ct=ct, pyw=pyw, buf=buf:
                                   e.matmul(out=pyw[:, th * 512:(th + 1) * 512],
                                            lhsT=wob[:, buf * 4 * C + b4 * C + ct * 128:
                                                     buf * 4 * C + b4 * C + (ct + 1) * 128],
                                            rhs=lnT_a[:, hd * TB + th * 512: hd * TB + (th + 1) * 512],
                                            start=(b4 == 0), stop=(b4 == 3)))
                v_y = r.cur("c_pe")
                r.wait("vector", "c_pe", v_y)
                for th in range(2):
                    r.emit_inc("vector", "c_dv", 1,
                               lambda e, ct=ct, th=th, pyw=pyw:
                               e.tensor_add(xT_a[:, ct * TB + th * 512: ct * TB + (th + 1) * 512],
                                            xT_a[:, ct * TB + th * 512: ct * TB + (th + 1) * 512],
                                            pyw[:, th * 512:(th + 1) * 512]))
                yev[ct] = r.emit_inc("vector", "c_ev", 1, lambda e: e.engine_nop())
            wo_pe[chw] = r.cur("c_pe")

        ln3 = emit_ln()
        ffn3 = emit_ffn_pass(ln3)
        r.wait("sync", "c_ev", ffn3)
        for ct in range(CT):
            r.emit_inc("sync", "s_out", 16,
                       lambda e, ct=ct: e.dma_start(out=yT[ct * 128:(ct + 1) * 128, :],
                                                    in_=xT_a[:, ct * TB:(ct + 1) * TB]))
        r.wait("sync", "s_out")

        with nc.Block() as block:
            @block.sync
            def _(e):
                for f in r.ops["sync"]:
                    f(e)

            @block.gpsimd
            def _(e):
                for f in r.ops["gpsimd"]:
                    f(e)

            @block.vector
            def _(e):
                for f in r.ops["vector"]:
                    f(e)

            @block.scalar
            def _(e):
                for f in r.ops["scalar"]:
                    f(e)

            @block.tensor
            def _(e):
                for f in r.ops["tensor"]:
                    f(e)

    return nc


_NC_CACHE = {}


def kernel(L0_attn_x_q, L0_attn_Wq, L0_attn_Wk, L0_attn_Wv, L0_attn_Wo,
           L0_ffn_w1, L0_ffn_w2, L0_ffn_w3):
    x = np.ascontiguousarray(np.asarray(L0_attn_x_q, dtype=np.float32)[0])
    wq_ = np.ascontiguousarray(np.asarray(L0_attn_Wq, dtype=np.float32).reshape(C, 2048))
    wk_ = np.ascontiguousarray(np.asarray(L0_attn_Wk, dtype=np.float32).reshape(C, 512))
    wv_ = np.ascontiguousarray(np.asarray(L0_attn_Wv, dtype=np.float32).reshape(C, 512))
    wo_ = np.ascontiguousarray(np.asarray(L0_attn_Wo, dtype=np.float32))
    w1T = np.ascontiguousarray(np.asarray(L0_ffn_w1, dtype=np.float32).T)
    w2T = np.ascontiguousarray(np.asarray(L0_ffn_w2, dtype=np.float32).T)
    w3T = np.ascontiguousarray(np.asarray(L0_ffn_w3, dtype=np.float32).T)

    xTb = np.ascontiguousarray(x.reshape(8, TB, C).transpose(0, 2, 1))

    nonce = int(np.random.default_rng().integers(1 << 20, (1 << 31) - 1))
    in_maps = []
    for core in range(8):
        par = core % 2
        blocks = [core] + [b for b in range(par, 8, 2) if b != core]
        m = np.zeros((1, 16), np.int32)
        m[0, 0] = par
        m[0, 1] = nonce
        in_maps.append({
            "meta": m,
            "xT": np.ascontiguousarray(xTb[blocks]),
            "w1th": np.ascontiguousarray(w1T[par * 1024:(par + 1) * 1024, :]),
            "w2th": np.ascontiguousarray(w2T[par * 1024:(par + 1) * 1024, :]),
            "w3th": np.ascontiguousarray(w3T[par * 2816:(par + 1) * 2816, :]),
            "wq": wq_, "wk": wk_, "wv": wv_, "wo": wo_,
        })
    if "nc" not in _NC_CACHE:
        _NC_CACHE["nc"] = build_program()
    res = run_bass_kernel_spmd(_NC_CACHE["nc"], in_maps, core_ids=list(range(8)))
    out = np.empty((SEQ, C), np.float32)
    for core in range(8):
        out[core * TB:(core + 1) * TB, :] = res.results[core]["yT"].T
    return out[None, :, :]


# version 12
# speedup vs baseline: 1.3356x; 1.3356x over previous
import numpy as np
from contextlib import ExitStack

import concourse.bass as bass
import concourse.mybir as mybir
from concourse.bass_utils import run_bass_kernel_spmd

F32 = mybir.dt.float32
BF16 = mybir.dt.bfloat16
I32 = mybir.dt.int32
AF = mybir.ActivationFunctionType
ALU = mybir.AluOpType

C = 2048
CT = 16
TB = 1024
HF = 5632
NJ = 44
CHUNK = 11
NCHUNK = 4
D = 128
NQH = 16
NKV = 4
SEQ = 8192
KT = 64
NBLK = 4
NW_12 = HF * C
NW_3 = C * HF
EPS_BIT = 1e-8
EPS_LN = 1e-5
EPS_QK = 1e-6
INV_SQRT_D = float(1.0 / np.sqrt(128.0))
SHIFT_C = float(np.sqrt(128.0) + 1e-3)
ETS = 6

ENGINES = ("sync", "gpsimd", "vector", "scalar", "tensor")


class Rec:
    def __init__(self):
        self.ops = {e: [] for e in ENGINES}
        self.cnt = {}
        self.sems = None

    def inc(self, sem, amt):
        self.cnt[sem] = self.cnt.get(sem, 0) + amt
        return self.cnt[sem]

    def cur(self, sem):
        return self.cnt.get(sem, 0)

    def on(self, eng, fn):
        self.ops[eng].append(fn)

    def wait(self, eng, sem, val=None):
        v = self.cur(sem) if val is None else val
        if v > 0:
            self.on(eng, lambda e, s=sem, vv=v: e.wait_ge(self.sems[s], vv))

    def emit_inc(self, eng, sem, amt, builder):
        v = self.inc(sem, amt)
        self.on(eng, lambda e, b=builder, s=sem, a=amt: b(e).then_inc(self.sems[s], a))
        return v


def build_program():
    nc = bass.Bass("TRN2", target_bir_lowering=False, debug=False)
    r = Rec()

    meta = nc.dram_tensor("meta", [1, 16], I32, kind="ExternalInput").ap()
    xT = nc.dram_tensor("xT", [NBLK, C, TB], F32, kind="ExternalInput").ap()
    w1th = nc.dram_tensor("w1th", [1024, HF], F32, kind="ExternalInput").ap()
    w2th = nc.dram_tensor("w2th", [1024, HF], F32, kind="ExternalInput").ap()
    w3th = nc.dram_tensor("w3th", [2816, C], F32, kind="ExternalInput").ap()
    wq = nc.dram_tensor("wq", [C, 2048], F32, kind="ExternalInput").ap()
    wk = nc.dram_tensor("wk", [C, 512], F32, kind="ExternalInput").ap()
    wv = nc.dram_tensor("wv", [C, 512], F32, kind="ExternalInput").ap()
    wo = nc.dram_tensor("wo", [2048, C], F32, kind="ExternalInput").ap()
    yT = nc.dram_tensor("yT", [C, TB], F32, kind="ExternalOutput").ap()

    sh_w1T = nc.dram_tensor("sh_w1T", [C, HF], BF16, addr_space="Shared").ap()
    sh_w2T = nc.dram_tensor("sh_w2T", [C, HF], BF16, addr_space="Shared").ap()
    sh_w3T = nc.dram_tensor("sh_w3T", [HF, C], BF16, addr_space="Shared").ap()
    sh_kT = nc.dram_tensor("sh_kT", [NKV * 128, SEQ], BF16, addr_space="Shared").ap()
    sh_v = nc.dram_tensor("sh_v", [SEQ, NKV * 128], BF16, addr_space="Shared").ap()
    sh_gm = nc.dram_tensor("sh_gm", [2, 16], F32, addr_space="Shared").ap()
    sh_fl = nc.dram_tensor("sh_fl", [2, 16], I32, addr_space="Shared").ap()
    xsp = nc.dram_tensor("xsp", [C, TB], F32).ap()
    qsp = nc.dram_tensor("qsp", [NQH * 128, TB], BF16).ap()

    es = ExitStack()

    def sb(name, shape, dt):
        return es.enter_context(nc.sbuf_tensor(name, shape, dt))

    with es:
        mt = sb("mt", [1, 16], I32)
        ident = sb("ident", [128, 128], BF16)
        ones_l = sb("ones_l", [128, 1], F32)
        ones_lb = sb("ones_lb", [128, 1], BF16)
        ones_b = sb("ones_b", [1, 128], F32)
        b_eln = sb("b_eln", [1, 1], F32)
        b_shift = sb("b_shift", [128, 1], F32)
        b_e8 = sb("b_e8", [1, 1], F32)
        b_eqk = sb("b_eqk", [1, 1], F32)

        xT_a = sb("xT_a", [128, CT * TB], F32)
        lnT_a = sb("lnT_a", [128, CT * TB], BF16)
        arena_t = sb("arena", [128, 33792], BF16)
        arena = arena_t[:, :]
        w12b = sb("w12b", [128, 4 * C], BF16)
        etb = sb("etb", [128, ETS * 512], BF16)
        sgb = sb("sgb", [128, TB], BF16)
        sqb = sb("sqb", [128, TB], F32)
        tmpf = sb("tmpf", [128, TB], F32)
        rowsA = sb("rowsA", [1, TB], F32)
        rowsB = sb("rowsB", [1, TB], F32)
        g_acc = sb("g_acc", [128, 80], F32)
        g3 = sb("g3", [128, 4], F32)
        gpart = sb("gpart", [1, 4], F32)
        gm2 = sb("gm2", [1, 8], F32)
        gmrow = sb("gmrow", [1, 4], F32)
        rinv = sb("rinv", [1, 4], F32)
        rcol = sb("rcol", [128, 4], F32)
        o_sb = sb("o_sb", [128, 128], BF16)
        rc1 = sb("rc1", [128, 1], F32)
        flg = sb("flg", [1, 32], I32)

        w3ch = arena[:, 0:CHUNK * C]
        hact = arena[:, CHUNK * C: CHUNK * C + CHUNK * TB]
        qT_a = arena[:, 0:NQH * TB]
        kTg = arena[:, 16384:16384 + SEQ]
        vg = arena[:, 24576:24576 + KT * 128]
        wob = arena[:, 0:2 * 4 * C]
        wkvb = arena[:, 16384:16384 + CT * 512]
        wqb = arena[:, 24576:24576 + 2 * C]
        vstage = arena[:, 28672:28672 + 8 * 512]
        arena_f32 = arena_t[:, :].bitcast(F32)
        kv_sq = [arena_f32[:, 0:1024], arena_f32[:, 1024:2048]]
        kv_ev = [arena_f32[:, 2048:3072], arena_f32[:, 3072:4096]]
        q_sq = [arena_f32[:, 8192:9216], arena_f32[:, 9216:10240]]
        q_ev = [arena_f32[:, 10240:11264], arena_f32[:, 11264:12288]]
        kout2 = vstage[:, 0:1024]
        qin0 = arena_f32[:, 0:2816]
        qin1 = arena_f32[:, 2816:5632]
        qpos = arena[:, 11264:11264 + 2816]
        qneg = arena[:, 14080:14080 + 2816]
        qout0 = arena[:, 16896:16896 + 2816]
        qout1 = arena[:, 19712:19712 + 2816]
        gdump = arena[:, 16896:16896 + 2816]

        def psum(name):
            return es.enter_context(nc.psum_tensor(name, [128, 1024], F32))
        p0 = psum("p0")
        p1 = psum("p1")
        p2 = psum("p2")
        p3 = psum("p3")

        sems = {}
        for s in ["s_meta", "s_x", "s_g", "s_g2", "s_q", "s_scr", "s_lw", "s_kv", "s_sp",
                  "s_out", "s_aw", "s_pl",
                  "c_po", "c_pe", "c_dv", "c_ac", "c_ev", "c_rdy"]:
            sems[s] = es.enter_context(nc.semaphore(s))
        r.sems = sems
        regs = {}

        r.emit_inc("sync", "s_meta", 16, lambda e: e.dma_start(out=mt[:], in_=meta))

        def init_sync(e):
            regs["s_par"] = e.alloc_register("s_par")
            e.wait_ge(sems["s_meta"], 16)
            e.reg_mov(regs["s_par"], e.value_load(mt[0:1, 0:1]))
        r.on("sync", init_sync)

        def init_gp(e):
            for nm in ("g_par", "g_nonce", "g_cnt", "g_w16", "g_t0", "g_go"):
                regs[nm] = e.alloc_register(nm)
            e.wait_ge(sems["s_meta"], 16)
            e.reg_mov(regs["g_par"], e.value_load(mt[0:1, 0:1]))
            e.reg_mov(regs["g_nonce"], e.value_load(mt[0:1, 1:2]))
            e.reg_mov(regs["g_cnt"], 0)
        r.on("gpsimd", init_gp)

        def setup_pool(e):
            e.memset(ident[:], 0.0)
            e.affine_select(out=ident[:], in_=ident[:], compare_op=ALU.not_equal,
                            fill=1.0, base=0, pattern=[[-1, 128]], channel_multiplier=1)
            e.memset(ones_l[:], 1.0)
            e.memset(ones_lb[:], 1.0)
            e.memset(ones_b[:], 1.0)
            e.memset(b_eln[:], EPS_LN)
            e.memset(b_shift[:], -SHIFT_C)
            e.memset(b_e8[:], EPS_BIT)
            e.memset(b_eqk[:], EPS_QK)
            e.memset(g_acc[:], 0.0)
        r.on("gpsimd", setup_pool)
        SETUP_PO = r.emit_inc("gpsimd", "c_po", 1, lambda e: e.engine_nop())

        def par_dma(out_even, out_odd, in_ap, sem):
            v = r.inc(sem, 16)
            def f(e, oe=out_even, oo=out_odd, ia=in_ap, s=sem):
                with e.If_eq(regs["s_par"], 0):
                    e.dma_start(out=oe, in_=ia).then_inc(r.sems[s], 16)
                with e.Else():
                    e.dma_start(out=oo, in_=ia).then_inc(r.sems[s], 16)
                e.end_ifs()
            r.on("sync", f)
            return v

        def flag_write(col):
            return par_dma(sh_fl[0:1, col:col + 1], sh_fl[1:2, col:col + 1],
                           mt[0:1, 1:2], "s_kv")

        def poll_both(col):
            v = r.inc("c_rdy", 1)
            def f(e, col=col):
                for row in (0, 1):
                    e.reg_mov(regs["g_go"], 1)
                    with e.While(regs["g_go"]):
                        e.reg_add(regs["g_cnt"], regs["g_cnt"], 1)
                        e.dma_start(out=flg.ap().rearrange("o (r c) -> o r c", r=2),
                                    in_=sh_fl.rearrange("(o r) c -> o r c", o=1)).then_inc(r.sems["s_pl"], 16)
                        e.reg_mov(regs["g_w16"], regs["g_cnt"])
                        e.reg_mul(regs["g_w16"], regs["g_w16"], 16)
                        e.wait_ge(r.sems["s_pl"], regs["g_w16"])
                        e.reg_mov(regs["g_t0"], e.value_load(flg[0:1, row * 16 + col: row * 16 + col + 1]))
                        e.reg_alu(regs["g_go"], regs["g_t0"], regs["g_nonce"], ALU.not_equal)
                e.engine_nop().then_inc(r.sems["c_rdy"], 1)
            r.on("gpsimd", f)
            return v

        def phase_barrier(engs=("tensor", "vector", "scalar")):
            for eng in engs:
                for s in ("c_pe", "c_dv", "c_ac", "c_ev", "s_kv", "s_sp", "s_scr"):
                    r.wait(eng, s)

        for ct in range(CT):
            r.emit_inc("sync", "s_x", 16,
                       lambda e, ct=ct: e.dma_start(out=xT_a[:, ct * TB:(ct + 1) * TB],
                                                    in_=xT[0, ct * 128:(ct + 1) * 128, :]))
        X_OWN = r.cur("s_x")

        def gamma_jobs(jobs, bufs, dump, queue_fn, pre_gates=None):
            ac_base = r.cur("c_ac")
            gated = set()
            for i, (dram, r0, c0, width, col) in enumerate(jobs):
                qb = bufs[i % 2]
                eng, sem = queue_fn(i)
                if pre_gates and eng not in gated:
                    for (gs, gv) in pre_gates:
                        r.wait(eng, gs, gv)
                    gated.add(eng)
                if i >= 2:
                    r.wait(eng, "c_ac", ac_base + (i - 2) + 1)
                v = r.emit_inc(eng, sem, 16,
                               lambda e, r0=r0, c0=c0, d=dram, w=width, qb=qb:
                               e.dma_start(out=qb[:, 0:w], in_=d[r0:r0 + 128, c0:c0 + w]))
                r.wait("scalar", sem, v)
                if i == 0:
                    r.wait("scalar", "c_po", SETUP_PO)
                r.emit_inc("scalar", "c_ac", 1,
                           lambda e, w=width, col=col, qb=qb, dump=dump:
                           e.activation(out=dump[:, 0:w], in_=qb[:, 0:w],
                                        func=AF.Abs, accum_out=g_acc[:, col:col + 1]))
            return r.cur("c_ac")

        def gamma_finish(ac_done, gacc_lo, gacc_hi, g3_cols, gm_cols, flag_col, nw_scales):
            r.wait("vector", "c_ac", ac_done)
            for ci, (lo, hi) in enumerate(zip(gacc_lo, gacc_hi)):
                r.emit_inc("vector", "c_dv", 1,
                           lambda e, lo=lo, hi=hi, gc=g3_cols[ci]:
                           e.reduce_sum(g3[:, gc:gc + 1], g_acc[:, lo:hi], axis=mybir.AxisListType.X))
            GS = r.cur("c_dv")
            r.wait("tensor", "c_dv", GS)
            r.wait("tensor", "c_po", SETUP_PO)
            nclz = len(g3_cols)
            gc0 = g3_cols[0]
            v = r.emit_inc("tensor", "c_pe", 1,
                           lambda e, gc0=gc0, n=nclz: e.matmul(out=p0[0:1, gc0:gc0 + n], lhsT=ones_l[:],
                                                               rhs=g3[:, gc0:gc0 + n], start=True, stop=True))
            r.wait("vector", "c_pe", v)
            GP = r.emit_inc("vector", "c_dv", 1,
                            lambda e, gc0=gc0, n=nclz: e.tensor_copy(out=gpart[0:1, 0:n], in_=p0[0:1, gc0:gc0 + n]))
            r.wait("sync", "c_dv", GP)
            par_dma(sh_gm[0:1, gm_cols[0]:gm_cols[1]], sh_gm[1:2, gm_cols[0]:gm_cols[1]],
                    gpart[0:1, 0:nclz], "s_kv")
            r.wait("sync", "s_kv")
            flag_write(flag_col)
            RDY = poll_both(flag_col)
            r.wait("sync", "c_rdy", RDY)
            GM2 = r.emit_inc("sync", "s_q", 16,
                             lambda e, a=gm_cols[0]:
                             e.dma_start(out=gm2.ap().rearrange("o (r c) -> o r c", r=2),
                                         in_=sh_gm[0:2, a:a + 4].rearrange("(o r) c -> o r c", o=1)))
            r.wait("vector", "s_q", GM2)
            GT = r.emit_inc("vector", "c_dv", 1,
                            lambda e, n=nclz: e.tensor_add(gmrow[0:1, 0:n], gm2[0:1, 0:n], gm2[0:1, 4:4 + n]))
            r.wait("scalar", "c_dv", GT)
            for ci, sc in enumerate(nw_scales):
                r.emit_inc("scalar", "c_ac", 1,
                           lambda e, ci=ci, sc=sc: e.activation(out=rinv[0:1, ci:ci + 1], in_=gmrow[0:1, ci:ci + 1],
                                                                func=AF.Identity, scale=sc, bias=b_e8[:, :]))
            r.emit_inc("scalar", "c_ac", 1,
                       lambda e, n=nclz: e.activation(out=rinv[0:1, 0:n], in_=rinv[0:1, 0:n], func=AF.Ln))
            RI = r.emit_inc("scalar", "c_ac", 1,
                            lambda e, n=nclz: e.activation(out=rinv[0:1, 0:n], in_=rinv[0:1, 0:n],
                                                           func=AF.Exp, scale=-1.0))
            r.wait("tensor", "c_ac", RI)
            v = r.emit_inc("tensor", "c_pe", 1,
                           lambda e, n=nclz: e.matmul(out=p0[:, 8:8 + n], lhsT=ones_b[:, :],
                                                      rhs=rinv[0:1, 0:n], start=True, stop=True))
            r.wait("vector", "c_pe", v)
            RC = r.emit_inc("vector", "c_dv", 1,
                            lambda e, gc0=g3_cols[0], n=nclz:
                            e.tensor_copy(out=rcol[:, gc0:gc0 + n], in_=p0[:, 8:8 + n]))
            return RC

        def quant_stage(jobs, rc_ready, flag_map, ibase, queue_fn,
                        qin_gate_ac=None, prev_dv=None, prev_scr=None):
            dv_done = {}
            scr_done = {}
            r.wait("vector", "c_dv", rc_ready)
            if prev_scr is not None:
                r.wait("vector", "s_scr", prev_scr)
            gated = set()
            for i0, (wn, dram, b, hh, width, shdst, half) in enumerate(jobs):
                i = ibase + i0
                qbuf = qin0 if i % 2 == 0 else qin1
                obuf = qout0 if i % 2 == 0 else qout1
                wi = {"w1": 0, "w2": 1, "w3": 2}[wn]
                eng, sem = queue_fn(i0)
                if eng not in gated:
                    if qin_gate_ac is not None:
                        r.wait(eng, "c_ac", qin_gate_ac)
                    if prev_dv is not None:
                        r.wait(eng, "c_dv", prev_dv)
                    gated.add(eng)
                if i0 >= 2:
                    r.wait(eng, "c_dv", dv_done[i0 - 2])
                v = r.emit_inc(eng, sem, 16,
                               lambda e, b=b, hh=hh, d=dram, w=width, qb=qbuf:
                               e.dma_start(out=qb[:, 0:w],
                                           in_=d[b * 128:(b + 1) * 128, hh * 2816: hh * 2816 + w]))
                r.wait("vector", sem, v)
                if i0 >= 2:
                    r.wait("vector", "s_scr", scr_done[i0 - 2])
                r.emit_inc("vector", "c_dv", 1,
                           lambda e, w=width, wi=wi, qb=qbuf:
                           e.tensor_scalar(out=qpos[:, 0:w], in0=qb[:, 0:w],
                                           scalar1=rcol[:, wi:wi + 1], scalar2=0.5,
                                           op0=ALU.mult, op1=ALU.is_ge))
                r.emit_inc("vector", "c_dv", 1,
                           lambda e, w=width, wi=wi, qb=qbuf:
                           e.tensor_scalar(out=qneg[:, 0:w], in0=qb[:, 0:w],
                                           scalar1=rcol[:, wi:wi + 1], scalar2=-0.5,
                                           op0=ALU.mult, op1=ALU.is_le))
                r.emit_inc("vector", "c_dv", 1,
                           lambda e, w=width, ob=obuf:
                           e.tensor_sub(ob[:, 0:w], qpos[:, 0:w], qneg[:, 0:w]))
                if wn == "w3":
                    r.emit_inc("vector", "c_dv", 1,
                               lambda e, w=width, ob=obuf:
                               e.tensor_scalar_mul(ob[:, 0:w], ob[:, 0:w], 0.5))
                dv_done[i0] = r.cur("c_dv")
                r.wait("sync", "c_dv", dv_done[i0])
                if wn == "w3":
                    scr_done[i0] = par_dma(shdst[b * 128:(b + 1) * 128, :],
                                           shdst[half + b * 128: half + (b + 1) * 128, :],
                                           obuf[:, 0:width], "s_scr")
                else:
                    scr_done[i0] = par_dma(
                        shdst[b * 128:(b + 1) * 128, hh * 2816: (hh + 1) * 2816],
                        shdst[half + b * 128: half + (b + 1) * 128, hh * 2816: (hh + 1) * 2816],
                        obuf[:, 0:width], "s_scr")
                if (wn, b, hh) in flag_map:
                    r.wait("sync", "s_scr")
                    flag_write(flag_map[(wn, b, hh)])
            return ibase + len(jobs)

        g12jobs = [(w1th, b * 128, hh * 2816, 2816, b * 2 + hh) for b in range(8) for hh in range(2)] + \
                  [(w2th, b * 128, hh * 2816, 2816, 16 + b * 2 + hh) for b in range(8) for hh in range(2)]
        g3jobs = [(w3th, b * 128, hh * 1024, 1024, 32 + b * 2 + hh)
                  for b in range(22) for hh in range(2)]
        q12jobs = [("w1", w1th, b, hh, 2816, sh_w1T, 1024) for b in range(8) for hh in range(2)] + \
                  [("w2", w2th, b, hh, 2816, sh_w2T, 1024) for b in range(8) for hh in range(2)]
        q3jobs = [("w3", w3th, b, 0, C, sh_w3T, 2816) for b in range(22)]

        def emit_ln():
            phase_barrier()
            pe_hist = {}
            for ct in range(CT):
                sq_t = sqb if ct % 2 == 0 else tmpf
                if ct >= 2:
                    r.wait("scalar", "c_pe", pe_hist[ct - 2])
                va = r.emit_inc("scalar", "c_ac", 1,
                                lambda e, ct=ct, sq_t=sq_t: e.activation(out=sq_t[:, :],
                                                              in_=xT_a[:, ct * TB:(ct + 1) * TB],
                                                              func=AF.Square))
                r.wait("tensor", "c_ac", va)
                for th in range(2):
                    r.emit_inc("tensor", "c_pe", 1,
                               lambda e, ct=ct, th=th:
                               e.matmul(out=p2[0:1, th * 512:(th + 1) * 512], lhsT=ones_l[:],
                                        rhs=xT_a[:, ct * TB + th * 512: ct * TB + (th + 1) * 512],
                                        start=(ct == 0), stop=(ct == CT - 1)))
                    r.emit_inc("tensor", "c_pe", 1,
                               lambda e, th=th, ct=ct, sq_t=sq_t:
                               e.matmul(out=p3[0:1, th * 512:(th + 1) * 512], lhsT=ones_l[:],
                                        rhs=sq_t[:, th * 512:(th + 1) * 512],
                                        start=(ct == 0), stop=(ct == CT - 1)))
                pe_hist[ct] = r.cur("c_pe")
            r.wait("vector", "c_pe")
            r.emit_inc("vector", "c_dv", 1, lambda e: e.tensor_scalar_mul(rowsA[:, :], p2[0:1, :], 1.0 / C))
            r.emit_inc("vector", "c_dv", 1, lambda e: e.tensor_scalar_mul(rowsB[:, :], p3[0:1, :], 1.0 / C))
            r.emit_inc("vector", "c_dv", 1, lambda e: e.tensor_mul(sqb[0:1, 0:TB], rowsA[:, :], rowsA[:, :]))
            v_dv = r.emit_inc("vector", "c_dv", 1,
                              lambda e: e.tensor_sub(rowsB[:, :], rowsB[:, :], sqb[0:1, 0:TB]))
            r.wait("scalar", "c_dv", v_dv)
            r.emit_inc("scalar", "c_ac", 1,
                       lambda e: e.activation(out=rowsB[:, :], in_=rowsB[:, :], func=AF.Ln, bias=b_eln[:, :]))
            v_ac = r.emit_inc("scalar", "c_ac", 1,
                              lambda e: e.activation(out=rowsB[:, :], in_=rowsB[:, :], func=AF.Exp, scale=-0.5))
            r.wait("tensor", "c_ac", v_ac)
            for th in range(2):
                r.emit_inc("tensor", "c_pe", 1,
                           lambda e, th=th: e.matmul(out=p2[:, th * 512:(th + 1) * 512], lhsT=ones_b[:, :],
                                                     rhs=rowsA[0:1, th * 512:(th + 1) * 512], start=True, stop=True))
                r.emit_inc("tensor", "c_pe", 1,
                           lambda e, th=th: e.matmul(out=p3[:, th * 512:(th + 1) * 512], lhsT=ones_b[:, :],
                                                     rhs=rowsB[0:1, th * 512:(th + 1) * 512], start=True, stop=True))
            r.wait("vector", "c_pe")
            for ct in range(CT):
                for th in range(2):
                    r.emit_inc("vector", "c_dv", 1,
                               lambda e, ct=ct, th=th:
                               e.tensor_sub(tmpf[:, th * 512:(th + 1) * 512],
                                            xT_a[:, ct * TB + th * 512: ct * TB + (th + 1) * 512],
                                            p2[:, th * 512:(th + 1) * 512]))
                    r.emit_inc("vector", "c_dv", 1,
                               lambda e, ct=ct, th=th:
                               e.tensor_mul(lnT_a[:, ct * TB + th * 512: ct * TB + (th + 1) * 512],
                                            tmpf[:, th * 512:(th + 1) * 512],
                                            p3[:, th * 512:(th + 1) * 512]))
            return r.cur("c_dv")

        def emit_ffn_pass(ln_dv, pending=None):
            pending = list(pending or [])
            defer_w3 = bool(pending)
            phase_barrier(("tensor",))
            r.wait("sync", "c_rdy")
            r.wait("sync", "c_pe")
            r.wait("sync", "c_dv")
            hj_pe = {}
            sig_ac = {}
            hct_dv = {}
            y_done_prev = 0
            for chn in range(NCHUNK):
                for j in range(CHUNK):
                    jj = chn * CHUNK + j
                    buf = jj % 2
                    if jj >= 2:
                        r.wait("sync", "c_pe", hj_pe[jj - 2])
                    r.emit_inc("sync", "s_lw", 16,
                               lambda e, jj=jj, buf=buf:
                               e.dma_start(out=w12b[:, buf * C:(buf + 1) * C]
                                           .rearrange("p (ct h) -> p ct h", ct=CT),
                                           in_=sh_w1T[:, jj * 128:(jj + 1) * 128]
                                           .rearrange("(ct p) h -> p ct h", p=128)))
                    v12 = r.emit_inc("sync", "s_lw", 16,
                                     lambda e, jj=jj, buf=buf:
                                     e.dma_start(out=w12b[:, (2 + buf) * C:(3 + buf) * C]
                                                 .rearrange("p (ct h) -> p ct h", ct=CT),
                                                 in_=sh_w2T[:, jj * 128:(jj + 1) * 128]
                                                 .rearrange("(ct p) h -> p ct h", p=128)))
                    if chn == 0 and defer_w3:
                        pass
                    else:
                        if chn >= 1:
                            r.wait("sync", "c_pe", y_done_prev)
                        r.emit_inc("sync", "s_lw", 16,
                                   lambda e, jj=jj, j=j:
                                   e.dma_start(out=w3ch[:, j * C:(j + 1) * C],
                                               in_=sh_w3T[jj * 128:(jj + 1) * 128, :]))
                    r.wait("tensor", "s_lw", v12)
                    if jj == 0:
                        r.wait("tensor", "c_dv", ln_dv)
                    if jj >= 1:
                        r.wait("tensor", "c_ac", sig_ac[jj - 1])
                    for th in range(2):
                        for ct in range(CT):
                            r.emit_inc("tensor", "c_pe", 1,
                                       lambda e, th=th, ct=ct, buf=buf:
                                       e.matmul(out=p0[:, th * 512:(th + 1) * 512],
                                                lhsT=w12b[:, buf * C + ct * 128: buf * C + (ct + 1) * 128],
                                                rhs=lnT_a[:, ct * TB + th * 512: ct * TB + (th + 1) * 512],
                                                start=(ct == 0), stop=(ct == CT - 1)))
                    v_x1 = r.cur("c_pe")
                    if jj >= 1:
                        r.wait("tensor", "c_dv", hct_dv[jj - 1])
                    for th in range(2):
                        for ct in range(CT):
                            r.emit_inc("tensor", "c_pe", 1,
                                       lambda e, th=th, ct=ct, buf=buf:
                                       e.matmul(out=p1[:, th * 512:(th + 1) * 512],
                                                lhsT=w12b[:, (2 + buf) * C + ct * 128: (2 + buf) * C + (ct + 1) * 128],
                                                rhs=lnT_a[:, ct * TB + th * 512: ct * TB + (th + 1) * 512],
                                                start=(ct == 0), stop=(ct == CT - 1)))
                    hj_pe[jj] = r.cur("c_pe")
                    r.wait("scalar", "c_pe", v_x1)
                    if jj >= 1:
                        r.wait("scalar", "c_dv", hct_dv[jj - 1])
                    sig_ac[jj] = r.emit_inc("scalar", "c_ac", 1,
                                            lambda e: e.activation(out=sgb[:, :], in_=p0[:, :], func=AF.Sigmoid))
                    r.wait("vector", "c_ac", sig_ac[jj])
                    r.wait("vector", "c_pe", hj_pe[jj])
                    hct_dv[jj] = r.emit_inc("vector", "c_dv", 1,
                                            lambda e, j=j: e.tensor_mul(hact[:, j * TB:(j + 1) * TB],
                                                                        sgb[:, :], p1[:, :]))
                    if pending and chn == 0:
                        if j == 6:
                            r.wait("vector", "s_kv")
                        for _ in range(3):
                            if pending:
                                pending.pop(0)()
                while pending:
                    pending.pop(0)()
                if chn == 0 and defer_w3:
                    r.wait("sync", "c_dv")
                    r.wait("sync", "c_ac")
                    for j2 in range(CHUNK):
                        jj2 = chn * CHUNK + j2
                        r.emit_inc("sync", "s_lw", 16,
                                   lambda e, jj2=jj2, j2=j2:
                                   e.dma_start(out=w3ch[:, j2 * C:(j2 + 1) * C],
                                               in_=sh_w3T[jj2 * 128:(jj2 + 1) * 128, :]))
                if chn == 0:
                    r.wait("tensor", "c_dv")
                    r.wait("tensor", "c_ac")
                r.wait("tensor", "c_dv", hct_dv[chn * CHUNK + CHUNK - 1])
                r.wait("tensor", "s_lw")
                yev = {}
                for ct in range(CT):
                    py = p2 if ct % 2 == 0 else p3
                    if ct >= 2:
                        r.wait("tensor", "c_ev", yev[ct - 2])
                    for j in range(CHUNK):
                        for th in range(2):
                            r.emit_inc("tensor", "c_pe", 1,
                                       lambda e, j=j, th=th, ct=ct, py=py:
                                       e.matmul(out=py[:, th * 512:(th + 1) * 512],
                                                lhsT=w3ch[:, j * C + ct * 128: j * C + (ct + 1) * 128],
                                                rhs=hact[:, j * TB + th * 512: j * TB + (th + 1) * 512],
                                                start=(j == 0), stop=(j == CHUNK - 1)))
                    v_y = r.cur("c_pe")
                    r.wait("vector", "c_pe", v_y)
                    for th in range(2):
                        r.emit_inc("vector", "c_dv", 1,
                                   lambda e, ct=ct, th=th, py=py:
                                   e.tensor_add(xT_a[:, ct * TB + th * 512: ct * TB + (th + 1) * 512],
                                                xT_a[:, ct * TB + th * 512: ct * TB + (th + 1) * 512],
                                                py[:, th * 512:(th + 1) * 512]))
                    yev[ct] = r.emit_inc("vector", "c_ev", 1, lambda e: e.engine_nop())
                y_done_prev = r.cur("c_pe")
            return r.cur("c_ev")

        kraw = [arena[:, 16384 + i * TB: 16384 + (i + 1) * TB] for i in range(NKV)]
        wkv2 = arena[:, 24576:24576 + CT * 512]

        def emit_qk(slot, ln_dv, do_q):
            phase_barrier()
            r.wait("gpsimd", "c_pe")
            r.wait("gpsimd", "c_dv")
            tiles = []
            if do_q:
                tiles += [("q", h) for h in range(NQH)]
            tiles += [("k", g) for g in range(NKV)]
            loads = {}
            evac_v = {}
            kload_v = None
            for i, (kind, idx) in enumerate(tiles):
                pp = (p0, p1)[i % 2]
                if kind == "q":
                    buf = i % 2
                    if i >= 2:
                        r.wait("gpsimd", "c_pe", evac_v.get(i - 2, (0, 0))[1])
                    loads[i] = r.emit_inc("gpsimd", "s_aw", 16,
                                          lambda e, idx=idx, buf=buf:
                                          e.dma_start(out=wqb[:, buf * C:(buf + 1) * C]
                                                      .rearrange("p (ct d) -> p ct d", ct=CT),
                                                      in_=wq[:, idx * 128:(idx + 1) * 128]
                                                      .rearrange("(ct p) d -> p ct d", p=128)))
                elif kload_v is None:
                    if do_q:
                        r.wait("gpsimd", "c_pe", evac_v[i - 1][1])
                    kload_v = r.emit_inc("gpsimd", "s_aw", 16,
                                         lambda e: e.dma_start(
                                             out=wkv2.rearrange("p (ct d) -> p ct d", ct=CT),
                                             in_=wk.rearrange("(ct p) d -> p ct d", p=128)))
                    loads[i] = kload_v
                else:
                    loads[i] = kload_v
                r.wait("tensor", "s_aw", loads[i])
                if i == 0:
                    r.wait("tensor", "c_dv", ln_dv)
                if i >= 2:
                    r.wait("tensor", "c_ac", evac_v[i - 2][0])
                for th in range(2):
                    for ct in range(CT):
                        def mk(e, kind=kind, idx=idx, th=th, ct=ct, pp=pp, buf=(i % 2)):
                            if kind == "q":
                                lh = wqb[:, buf * C + ct * 128: buf * C + (ct + 1) * 128]
                            else:
                                lh = wkv2[:, ct * 512 + idx * 128: ct * 512 + idx * 128 + 128]
                            return e.matmul(out=pp[:, th * 512:(th + 1) * 512], lhsT=lh,
                                            rhs=lnT_a[:, ct * TB + th * 512: ct * TB + (th + 1) * 512],
                                            start=(ct == 0), stop=(ct == CT - 1))
                        r.emit_inc("tensor", "c_pe", 1, mk)
                proj_v = r.cur("c_pe")
                r.wait("scalar", "c_pe", proj_v)
                dst = qT_a[:, idx * TB:(idx + 1) * TB] if kind == "q" else kraw[idx]
                va = r.emit_inc("scalar", "c_ac", 1,
                                lambda e, pp=pp, dst=dst: e.activation(out=dst, in_=pp[:, :], func=AF.Identity))
                evac_v[i] = (va, proj_v)
            EV_ALL = r.cur("c_ac")
            r.wait("gpsimd", "c_pe")
            v_wv = r.emit_inc("gpsimd", "s_aw", 16,
                              lambda e: e.dma_start(
                                  out=wkvb.rearrange("p (ct d) -> p ct d", ct=CT),
                                  in_=wv.rearrange("(ct p) d -> p ct d", p=128)))
            vdv = {}
            for tt in range(8):
                pv = p2 if tt % 2 == 0 else p3
                r.wait("tensor", "s_aw", v_wv)
                r.wait("tensor", "c_dv", ln_dv)
                if tt >= 2:
                    r.wait("tensor", "c_dv", vdv[tt - 2])
                for ct in range(CT):
                    r.emit_inc("tensor", "c_pe", 1,
                               lambda e, tt=tt, ct=ct, pv=pv:
                               e.matmul(out=pv[:, 0:512],
                                        lhsT=lnT_a[:, ct * TB + tt * 128: ct * TB + tt * 128 + 128],
                                        rhs=wkvb[:, ct * 512:(ct + 1) * 512],
                                        start=(ct == 0), stop=(ct == CT - 1)))
                v_v = r.cur("c_pe")
                r.wait("vector", "c_pe", v_v)
                if tt == 0:
                    r.wait("vector", "s_kv")
                vdv[tt] = r.emit_inc("vector", "c_dv", 1,
                                     lambda e, tt=tt, pv=pv:
                                     e.tensor_copy(out=vstage[:, tt * 512:(tt + 1) * 512], in_=pv[:, 0:512]))
            def v_write_cl():
                r.wait("sync", "c_dv", vdv[7])
                par_dma(sh_v[slot * TB:(slot + 1) * TB, :].rearrange("(a p) d -> p a d", p=128),
                        sh_v[4096 + slot * TB: 4096 + (slot + 1) * TB, :].rearrange("(a p) d -> p a d", p=128),
                        vstage.rearrange("p (a d) -> p a d", a=8), "s_kv")
            pending = [v_write_cl]
            sq_v = {}
            ss_v = {}
            le_v = {}
            bc_v = {}
            mu_v = {}
            wr_v = {}

            def tile_chain(i, kind, idx):
                pn = (p2, p3)[i % 2]
                sq_t = (sqb, tmpf)[i % 2]
                row_t = (rowsA, rowsB)[i % 2]
                src_t = qT_a[:, idx * TB:(idx + 1) * TB] if kind == "q" else kraw[idx]
                r.wait("scalar", "c_ac", evac_v[i][0])
                if i >= 2:
                    r.wait("scalar", "c_pe", ss_v[i - 2])
                sq_v[i] = r.emit_inc("scalar", "c_ac", 1,
                                     lambda e, src_t=src_t, sq_t=sq_t:
                                     e.activation(out=sq_t[:, :], in_=src_t, func=AF.Square))
                r.wait("tensor", "c_ac", sq_v[i])
                if i >= 2:
                    r.wait("tensor", "c_dv", mu_v[i - 2])
                elif i < 2:
                    r.wait("tensor", "c_dv", vdv[7] if (6 + i) not in vdv else vdv[6 + i])
                for th in range(2):
                    r.emit_inc("tensor", "c_pe", 1,
                               lambda e, th=th, sq_t=sq_t, pn=pn:
                               e.matmul(out=pn[0:1, th * 512:(th + 1) * 512], lhsT=ones_l[:],
                                        rhs=sq_t[:, th * 512:(th + 1) * 512], start=True, stop=True))
                ss_v[i] = r.cur("c_pe")
                r.wait("scalar", "c_pe", ss_v[i])
                if i >= 2:
                    r.wait("scalar", "c_pe", bc_v[i - 2])
                r.emit_inc("scalar", "c_ac", 1,
                           lambda e, pn=pn, row_t=row_t: e.activation(out=row_t[:, :], in_=pn[0:1, :], func=AF.Ln,
                                                                      scale=1.0 / D, bias=b_eqk[:, :]))
                le_v[i] = r.emit_inc("scalar", "c_ac", 1,
                                     lambda e, row_t=row_t: e.activation(out=row_t[:, :], in_=row_t[:, :],
                                                                         func=AF.Exp, scale=-0.5))
                r.wait("tensor", "c_ac", le_v[i])
                for th in range(2):
                    r.emit_inc("tensor", "c_pe", 1,
                               lambda e, th=th, pn=pn, row_t=row_t:
                               e.matmul(out=pn[:, th * 512:(th + 1) * 512], lhsT=ones_b[:, :],
                                        rhs=row_t[0:1, th * 512:(th + 1) * 512], start=True, stop=True))
                bc_v[i] = r.cur("c_pe")
                r.wait("vector", "c_pe", bc_v[i])
                if kind == "k" and i >= 2 and (i - 2) in wr_v:
                    r.wait("vector", "s_kv", wr_v[i - 2])
                mu_v[i] = r.emit_inc("vector", "c_dv", 1,
                                     lambda e, src_t=src_t, pn=pn:
                                     e.tensor_mul(src_t, src_t, pn[:, :]))
                if kind == "k":
                    r.wait("sync", "c_dv", mu_v[i])
                    wr_v[i] = par_dma(
                        sh_kT[idx * 128:(idx + 1) * 128, slot * TB:(slot + 1) * TB],
                        sh_kT[idx * 128:(idx + 1) * 128, 4096 + slot * TB: 4096 + (slot + 1) * TB],
                        src_t, "s_kv")

            for i, (kind, idx) in enumerate(tiles):
                pending.append(lambda i=i, kind=kind, idx=idx: tile_chain(i, kind, idx))

            def flag_cl():
                r.wait("sync", "s_kv")
                flag_write(4 + slot)
            pending.append(flag_cl)
            if do_q:
                def spill_q_cl():
                    r.wait("sync", "c_dv", mu_v[NQH - 1] if (NQH - 1) in mu_v else r.cur("c_dv"))
                    for h in range(NQH):
                        r.wait("sync", "c_dv", mu_v.get(h, 0))
                        r.emit_inc("sync", "s_sp", 16,
                                   lambda e, h=h: e.dma_start(out=qsp[h * 128:(h + 1) * 128, :],
                                                              in_=qT_a[:, h * TB:(h + 1) * TB]))
                pending.append(spill_q_cl)
            return pending

        for eng in ("scalar", "tensor", "vector"):
            r.wait(eng, "s_x", X_OWN)
        ln1 = emit_ln()
        ln1_pe = r.cur("c_pe")
        q12_alt = lambda i: ("sync", "s_g") if i % 2 == 0 else ("gpsimd", "s_g2")
        gp_only = lambda i: ("gpsimd", "s_g2")
        G12_AC = gamma_jobs(g12jobs, (qin0, qin1), gdump, q12_alt)
        RC12 = gamma_finish(G12_AC, [0, 16], [16, 32], [0, 1], (0, 2), 0,
                            [1.0 / NW_12, 1.0 / NW_12])
        G3_AC = gamma_jobs(g3jobs, (sqb, tmpf), sgb, gp_only,
                           pre_gates=[("c_dv", ln1), ("c_pe", ln1_pe)])
        nxt = quant_stage(q12jobs, RC12, {("w2", 7, 1): 1}, 0, q12_alt,
                          qin_gate_ac=G12_AC)
        DV12 = r.cur("c_dv")
        RC3 = gamma_finish(G3_AC, [32], [76], [2], (4, 5), 8, [1.0 / NW_3])
        quant_stage(q3jobs, RC3, {("w3", 21, 0): 3}, nxt, gp_only, prev_dv=DV12,
                    prev_scr=r.cur("s_scr"))
        poll_both(1)
        poll_both(3)
        emit_ffn_pass(ln1)
        ln2 = emit_ln()
        ln2_marks = (r.cur("c_dv"), r.cur("c_ac"), r.cur("c_pe"))
        own_pending = emit_qk(0, ln2, do_q=True)
        r.wait("sync", "c_dv", ln2_marks[0])
        r.wait("sync", "c_ac", ln2_marks[1])
        r.wait("sync", "c_pe", ln2_marks[2])
        for ct in range(CT):
            r.emit_inc("sync", "s_sp", 16,
                       lambda e, ct=ct: e.dma_start(out=xsp[ct * 128:(ct + 1) * 128, :],
                                                    in_=xT_a[:, ct * TB:(ct + 1) * TB]))
        SPILL_V = None

        prev_marks = ln2_marks
        pend = own_pending
        for blk in range(1, NBLK):
            r.wait("sync", "c_dv", prev_marks[0])
            r.wait("sync", "c_ac", prev_marks[1])
            r.wait("sync", "c_pe", prev_marks[2])
            for ct in range(CT):
                r.emit_inc("sync", "s_x", 16,
                           lambda e, ct=ct, blk=blk: e.dma_start(out=xT_a[:, ct * TB:(ct + 1) * TB],
                                                                 in_=xT[blk, ct * 128:(ct + 1) * 128, :]))
            vx = r.cur("s_x")
            for eng in ("scalar", "tensor", "vector"):
                r.wait(eng, "s_x", vx)
            lnb = emit_ln()
            emit_ffn_pass(lnb, pending=pend)
            ln2b = emit_ln()
            prev_marks = (r.cur("c_dv"), r.cur("c_ac"), r.cur("c_pe"))
            pend = emit_qk(blk, ln2b, do_q=False)
        while pend:
            pend.pop(0)()

        for s_ in range(4):
            poll_both(4 + s_)
        KV_RDY = r.cur("c_rdy")
        r.wait("sync", "c_rdy", KV_RDY)
        for s_ in ("c_dv", "c_ev", "c_pe", "c_ac"):
            r.wait("sync", s_)
        for h in range(NQH):
            r.emit_inc("sync", "s_q", 16,
                       lambda e, h=h: e.dma_start(out=qT_a[:, h * TB:(h + 1) * TB],
                                                  in_=qsp[h * 128:(h + 1) * 128, :]))
        QT_V = r.cur("s_q")
        for ct in range(CT):
            r.emit_inc("sync", "s_sp", 16,
                       lambda e, ct=ct: e.dma_start(out=xT_a[:, ct * TB:(ct + 1) * TB],
                                                    in_=xsp[ct * 128:(ct + 1) * 128, :]))
        XRL_V = r.cur("s_sp")

        ot_dv = 0
        exp_vals = []
        otm_vals = {}
        it = 0
        for g in range(NKV):
            if g >= 1:
                r.wait("sync", "c_pe")
            r.emit_inc("sync", "s_q", 16,
                       lambda e, g=g: e.dma_start(out=kTg, in_=sh_kT[g * 128:(g + 1) * 128, :]))
            KV_V = r.emit_inc("sync", "s_q", 16,
                              lambda e, g=g: e.dma_start(
                                  out=vg.rearrange("p (kt d) -> p kt d", kt=KT),
                                  in_=sh_v[:, g * 128:(g + 1) * 128].rearrange("(kt p) d -> p kt d", p=128)))
            for h4 in range(4):
                h = g * 4 + h4
                for qb in range(2):
                    pX = (p2, p3)[it % 2]
                    if it == 0:
                        r.wait("tensor", "s_q", max(KV_V, QT_V))
                        r.wait("tensor", "c_ac")
                        r.wait("tensor", "c_dv")
                    elif qb == 0 and h4 == 0:
                        r.wait("tensor", "s_q", KV_V)

                    def emit_pv(p_, first, pX=pX):
                        r.wait("tensor", "c_ac", exp_vals[-1 - ((len(exp_vals) - 1) - p_ - it * 32)]
                               if False else exp_vals[it * 32 + p_])
                        if first and it >= 2:
                            r.wait("tensor", "c_dv", otm_vals[it - 2])
                        for k2 in (2 * p_, 2 * p_ + 1):
                            r.emit_inc("tensor", "c_pe", 1,
                                       lambda e, k2=k2, pX=pX:
                                       e.matmul(out=pX[:, 0:512],
                                                lhsT=vg[:, k2 * 128:(k2 + 1) * 128],
                                                rhs=etb[:, (k2 % ETS) * 512:((k2 % ETS) + 1) * 512],
                                                start=(k2 == 0), stop=(k2 == KT - 1)))
                            r.emit_inc("tensor", "c_pe", 1,
                                       lambda e, k2=k2, pX=pX:
                                       e.matmul(out=pX[0:1, 512:1024],
                                                lhsT=ones_lb[:],
                                                rhs=etb[:, (k2 % ETS) * 512:((k2 % ETS) + 1) * 512],
                                                start=(k2 == 0), stop=(k2 == KT - 1)))

                    for p_ in range(KT // 2):
                        for half in (0, 1):
                            kt = 2 * p_ + half
                            gidx = it * KT + kt
                            stg = p0 if p_ % 2 == 0 else p1
                            if gidx >= 4 and (gidx - 4) // 2 < len(exp_vals):
                                r.wait("tensor", "c_ac", exp_vals[(gidx - 4) // 2])
                            r.emit_inc("tensor", "c_pe", 1,
                                       lambda e, kt=kt, h=h, qb=qb, stg=stg, half=half:
                                       e.matmul(out=stg[:, half * 512:(half + 1) * 512],
                                                lhsT=kTg[:, kt * 128:(kt + 1) * 128],
                                                rhs=qT_a[:, h * TB + qb * 512: h * TB + (qb + 1) * 512],
                                                start=True, stop=True))
                        v_st = r.cur("c_pe")
                        slot = (2 * p_) % ETS
                        stg = p0 if p_ % 2 == 0 else p1
                        r.wait("scalar", "c_pe", v_st)
                        va = r.emit_inc("scalar", "c_ac", 1,
                                        lambda e, slot=slot, stg=stg:
                                        e.activation(out=etb[:, slot * 512: (slot + 2) * 512],
                                                     in_=stg[:, :], func=AF.Exp,
                                                     scale=INV_SQRT_D, bias=b_shift[:, :]))
                        exp_vals.append(va)
                        if p_ >= 1:
                            emit_pv(p_ - 1, first=(p_ == 1))
                    emit_pv(KT // 2 - 1, first=False)
                    v_pv = r.cur("c_pe")
                    r.wait("scalar", "c_pe", v_pv)
                    r.emit_inc("scalar", "c_ac", 1,
                               lambda e, pX=pX: e.activation(out=rowsB[0:1, 0:512], in_=pX[0:1, 512:1024],
                                                             func=AF.Ln))
                    v_rc = r.emit_inc("scalar", "c_ac", 1,
                                      lambda e: e.activation(out=rowsB[0:1, 0:512], in_=rowsB[0:1, 0:512],
                                                             func=AF.Exp, scale=-1.0))
                    r.wait("tensor", "c_ac", v_rc)
                    v_bc = r.emit_inc("tensor", "c_pe", 1,
                                      lambda e, pX=pX: e.matmul(out=pX[:, 512:1024], lhsT=ones_b[:, :],
                                                                rhs=rowsB[0:1, 0:512], start=True, stop=True))
                    r.wait("vector", "c_pe", v_bc)
                    r.emit_inc("vector", "c_dv", 1,
                               lambda e, pX=pX: e.tensor_copy(out=tmpf[:, 0:512], in_=pX[:, 0:512]))
                    otm_vals[it] = r.emit_inc("vector", "c_dv", 1,
                                              lambda e, h=h, qb=qb, pX=pX:
                                              e.tensor_mul(lnT_a[:, h * TB + qb * 512: h * TB + (qb + 1) * 512],
                                                           tmpf[:, 0:512], pX[:, 512:1024]))
                    it += 1
        ot_dv = otm_vals[it - 1]

        wo_pe = {}
        for chw in range(4):
            buf = chw % 2
            if chw <= 1:
                r.wait("gpsimd", "c_dv", ot_dv)
            else:
                r.wait("gpsimd", "c_pe", wo_pe[chw - 2])
            v_wo = r.emit_inc("gpsimd", "s_aw", 16,
                              lambda e, chw=chw, buf=buf:
                              e.dma_start(out=wob[:, buf * 4 * C:(buf + 1) * 4 * C]
                                          .rearrange("p (b c) -> p b c", b=4),
                                          in_=wo[chw * 512:(chw + 1) * 512, :]
                                          .rearrange("(b p) c -> p b c", p=128)))
            r.wait("tensor", "s_aw", v_wo)
            r.wait("tensor", "c_dv", ot_dv)
            if chw == 0:
                r.wait("tensor", "s_sp", XRL_V)
                r.wait("vector", "s_sp", XRL_V)
            yev = {}
            for ct in range(CT):
                pyw = p2 if ct % 2 == 0 else p3
                if ct >= 2:
                    r.wait("tensor", "c_ev", yev[ct - 2])
                for b4 in range(4):
                    hd = chw * 4 + b4
                    for th in range(2):
                        r.emit_inc("tensor", "c_pe", 1,
                                   lambda e, b4=b4, hd=hd, th=th, ct=ct, pyw=pyw, buf=buf:
                                   e.matmul(out=pyw[:, th * 512:(th + 1) * 512],
                                            lhsT=wob[:, buf * 4 * C + b4 * C + ct * 128:
                                                     buf * 4 * C + b4 * C + (ct + 1) * 128],
                                            rhs=lnT_a[:, hd * TB + th * 512: hd * TB + (th + 1) * 512],
                                            start=(b4 == 0), stop=(b4 == 3)))
                v_y = r.cur("c_pe")
                r.wait("vector", "c_pe", v_y)
                for th in range(2):
                    r.emit_inc("vector", "c_dv", 1,
                               lambda e, ct=ct, th=th, pyw=pyw:
                               e.tensor_add(xT_a[:, ct * TB + th * 512: ct * TB + (th + 1) * 512],
                                            xT_a[:, ct * TB + th * 512: ct * TB + (th + 1) * 512],
                                            pyw[:, th * 512:(th + 1) * 512]))
                yev[ct] = r.emit_inc("vector", "c_ev", 1, lambda e: e.engine_nop())
            wo_pe[chw] = r.cur("c_pe")

        ln3 = emit_ln()
        ffn3 = emit_ffn_pass(ln3)
        r.wait("sync", "c_ev", ffn3)
        for ct in range(CT):
            r.emit_inc("sync", "s_out", 16,
                       lambda e, ct=ct: e.dma_start(out=yT[ct * 128:(ct + 1) * 128, :],
                                                    in_=xT_a[:, ct * TB:(ct + 1) * TB]))
        r.wait("sync", "s_out")

        with nc.Block() as block:
            @block.sync
            def _(e):
                for f in r.ops["sync"]:
                    f(e)

            @block.gpsimd
            def _(e):
                for f in r.ops["gpsimd"]:
                    f(e)

            @block.vector
            def _(e):
                for f in r.ops["vector"]:
                    f(e)

            @block.scalar
            def _(e):
                for f in r.ops["scalar"]:
                    f(e)

            @block.tensor
            def _(e):
                for f in r.ops["tensor"]:
                    f(e)

    return nc


_NC_CACHE = {}


def kernel(L0_attn_x_q, L0_attn_Wq, L0_attn_Wk, L0_attn_Wv, L0_attn_Wo,
           L0_ffn_w1, L0_ffn_w2, L0_ffn_w3):
    x = np.ascontiguousarray(np.asarray(L0_attn_x_q, dtype=np.float32)[0])
    wq_ = np.ascontiguousarray(np.asarray(L0_attn_Wq, dtype=np.float32).reshape(C, 2048))
    wk_ = np.ascontiguousarray(np.asarray(L0_attn_Wk, dtype=np.float32).reshape(C, 512))
    wv_ = np.ascontiguousarray(np.asarray(L0_attn_Wv, dtype=np.float32).reshape(C, 512))
    wo_ = np.ascontiguousarray(np.asarray(L0_attn_Wo, dtype=np.float32))
    w1T = np.ascontiguousarray(np.asarray(L0_ffn_w1, dtype=np.float32).T)
    w2T = np.ascontiguousarray(np.asarray(L0_ffn_w2, dtype=np.float32).T)
    w3T = np.ascontiguousarray(np.asarray(L0_ffn_w3, dtype=np.float32).T)

    xTb = np.ascontiguousarray(x.reshape(8, TB, C).transpose(0, 2, 1))

    nonce = int(np.random.default_rng().integers(1 << 20, (1 << 31) - 1))
    in_maps = []
    for core in range(8):
        par = core % 2
        blocks = [core] + [b for b in range(par, 8, 2) if b != core]
        m = np.zeros((1, 16), np.int32)
        m[0, 0] = par
        m[0, 1] = nonce
        in_maps.append({
            "meta": m,
            "xT": np.ascontiguousarray(xTb[blocks]),
            "w1th": np.ascontiguousarray(w1T[par * 1024:(par + 1) * 1024, :]),
            "w2th": np.ascontiguousarray(w2T[par * 1024:(par + 1) * 1024, :]),
            "w3th": np.ascontiguousarray(w3T[par * 2816:(par + 1) * 2816, :]),
            "wq": wq_, "wk": wk_, "wv": wv_, "wo": wo_,
        })
    if "nc" not in _NC_CACHE:
        _NC_CACHE["nc"] = build_program()
    res = run_bass_kernel_spmd(_NC_CACHE["nc"], in_maps, core_ids=list(range(8)))
    out = np.empty((SEQ, C), np.float32)
    for core in range(8):
        out[core * TB:(core + 1) * TB, :] = res.results[core]["yT"].T
    return out[None, :, :]


# version 13
# speedup vs baseline: 1.3419x; 1.0047x over previous
import numpy as np
from contextlib import ExitStack

import concourse.bass as bass
import concourse.mybir as mybir
from concourse.bass_utils import run_bass_kernel_spmd

F32 = mybir.dt.float32
BF16 = mybir.dt.bfloat16
I32 = mybir.dt.int32
AF = mybir.ActivationFunctionType
ALU = mybir.AluOpType

C = 2048
CT = 16
TB = 1024
HF = 5632
NJ = 44
CHUNK = 11
NCHUNK = 4
D = 128
NQH = 16
NKV = 4
SEQ = 8192
KT = 64
NBLK = 4
NW_12 = HF * C
NW_3 = C * HF
EPS_BIT = 1e-8
EPS_LN = 1e-5
EPS_QK = 1e-6
INV_SQRT_D = float(1.0 / np.sqrt(128.0))
SHIFT_C = float(np.sqrt(128.0) + 1e-3)
ETS = 6

ENGINES = ("sync", "gpsimd", "vector", "scalar", "tensor")


class Rec:
    def __init__(self):
        self.ops = {e: [] for e in ENGINES}
        self.cnt = {}
        self.sems = None

    def inc(self, sem, amt):
        self.cnt[sem] = self.cnt.get(sem, 0) + amt
        return self.cnt[sem]

    def cur(self, sem):
        return self.cnt.get(sem, 0)

    def on(self, eng, fn):
        self.ops[eng].append(fn)

    def wait(self, eng, sem, val=None):
        v = self.cur(sem) if val is None else val
        if v > 0:
            self.on(eng, lambda e, s=sem, vv=v: e.wait_ge(self.sems[s], vv))

    def emit_inc(self, eng, sem, amt, builder):
        v = self.inc(sem, amt)
        self.on(eng, lambda e, b=builder, s=sem, a=amt: b(e).then_inc(self.sems[s], a))
        return v


def build_program():
    nc = bass.Bass("TRN2", target_bir_lowering=False, debug=False)
    r = Rec()

    meta = nc.dram_tensor("meta", [1, 16], I32, kind="ExternalInput").ap()
    xT = nc.dram_tensor("xT", [NBLK, C, TB], F32, kind="ExternalInput").ap()
    w1th = nc.dram_tensor("w1th", [1024, HF], F32, kind="ExternalInput").ap()
    w2th = nc.dram_tensor("w2th", [1024, HF], F32, kind="ExternalInput").ap()
    w3th = nc.dram_tensor("w3th", [2816, C], F32, kind="ExternalInput").ap()
    wq = nc.dram_tensor("wq", [C, 2048], F32, kind="ExternalInput").ap()
    wk = nc.dram_tensor("wk", [C, 512], F32, kind="ExternalInput").ap()
    wv = nc.dram_tensor("wv", [C, 512], F32, kind="ExternalInput").ap()
    wo = nc.dram_tensor("wo", [2048, C], F32, kind="ExternalInput").ap()
    yT = nc.dram_tensor("yT", [C, TB], F32, kind="ExternalOutput").ap()

    sh_w1T = nc.dram_tensor("sh_w1T", [C, HF], BF16, addr_space="Shared").ap()
    sh_w2T = nc.dram_tensor("sh_w2T", [C, HF], BF16, addr_space="Shared").ap()
    sh_w3T = nc.dram_tensor("sh_w3T", [HF, C], BF16, addr_space="Shared").ap()
    sh_kT = nc.dram_tensor("sh_kT", [NKV * 128, SEQ], BF16, addr_space="Shared").ap()
    sh_v = nc.dram_tensor("sh_v", [SEQ, NKV * 128], BF16, addr_space="Shared").ap()
    sh_gm = nc.dram_tensor("sh_gm", [2, 16], F32, addr_space="Shared").ap()
    sh_fl = nc.dram_tensor("sh_fl", [2, 16], I32, addr_space="Shared").ap()
    xsp = nc.dram_tensor("xsp", [C, TB], F32).ap()
    qsp = nc.dram_tensor("qsp", [NQH * 128, TB], BF16).ap()

    es = ExitStack()

    def sb(name, shape, dt):
        return es.enter_context(nc.sbuf_tensor(name, shape, dt))

    with es:
        mt = sb("mt", [1, 16], I32)
        ident = sb("ident", [128, 128], BF16)
        ones_l = sb("ones_l", [128, 1], F32)
        ones_lb = sb("ones_lb", [128, 1], BF16)
        ones_b = sb("ones_b", [1, 128], F32)
        b_eln = sb("b_eln", [1, 1], F32)
        b_shift = sb("b_shift", [128, 1], F32)
        b_e8 = sb("b_e8", [1, 1], F32)
        b_eqk = sb("b_eqk", [1, 1], F32)

        xT_a = sb("xT_a", [128, CT * TB], F32)
        lnT_a = sb("lnT_a", [128, CT * TB], BF16)
        arena_t = sb("arena", [128, 33792], BF16)
        arena = arena_t[:, :]
        w12b = sb("w12b", [128, 4 * C], BF16)
        etb = sb("etb", [128, ETS * 512], BF16)
        sgb = sb("sgb", [128, TB], BF16)
        sqb = sb("sqb", [128, TB], F32)
        tmpf = sb("tmpf", [128, TB], F32)
        rowsA = sb("rowsA", [1, TB], F32)
        rowsB = sb("rowsB", [1, TB], F32)
        g_acc = sb("g_acc", [128, 80], F32)
        g3 = sb("g3", [128, 4], F32)
        gpart = sb("gpart", [1, 4], F32)
        gm2 = sb("gm2", [1, 8], F32)
        gmrow = sb("gmrow", [1, 4], F32)
        rinv = sb("rinv", [1, 4], F32)
        rcol = sb("rcol", [128, 4], F32)
        o_sb = sb("o_sb", [128, 128], BF16)
        rc1 = sb("rc1", [128, 1], F32)
        flg = sb("flg", [1, 32], I32)

        w3ch = arena[:, 0:CHUNK * C]
        hact = arena[:, CHUNK * C: CHUNK * C + CHUNK * TB]
        qT_a = arena[:, 0:NQH * TB]
        kTg = arena[:, 16384:16384 + SEQ]
        vg = arena[:, 24576:24576 + KT * 128]
        wob = arena[:, 0:2 * 4 * C]
        wkvb = arena[:, 16384:16384 + CT * 512]
        wqb = arena[:, 24576:24576 + 2 * C]
        vstage = arena[:, 28672:28672 + 8 * 512]
        arena_f32 = arena_t[:, :].bitcast(F32)
        kv_sq = [arena_f32[:, 0:1024], arena_f32[:, 1024:2048]]
        kv_ev = [arena_f32[:, 2048:3072], arena_f32[:, 3072:4096]]
        q_sq = [arena_f32[:, 8192:9216], arena_f32[:, 9216:10240]]
        q_ev = [arena_f32[:, 10240:11264], arena_f32[:, 11264:12288]]
        kout2 = vstage[:, 0:1024]
        qin0 = arena_f32[:, 0:2816]
        qin1 = arena_f32[:, 2816:5632]
        qpos = arena[:, 11264:11264 + 2816]
        qneg = arena[:, 14080:14080 + 2816]
        qout0 = arena[:, 16896:16896 + 2816]
        qout1 = arena[:, 19712:19712 + 2816]
        gdump = arena[:, 16896:16896 + 2816]

        def psum(name):
            return es.enter_context(nc.psum_tensor(name, [128, 1024], F32))
        p0 = psum("p0")
        p1 = psum("p1")
        p2 = psum("p2")
        p3 = psum("p3")

        sems = {}
        for s in ["s_meta", "s_x", "s_g", "s_g2", "s_q", "s_scr", "s_lw", "s_kv", "s_sp",
                  "s_out", "s_aw", "s_pl",
                  "c_po", "c_pe", "c_dv", "c_ac", "c_ev", "c_rdy"]:
            sems[s] = es.enter_context(nc.semaphore(s))
        r.sems = sems
        regs = {}

        r.emit_inc("sync", "s_meta", 16, lambda e: e.dma_start(out=mt[:], in_=meta))

        def init_sync(e):
            regs["s_par"] = e.alloc_register("s_par")
            e.wait_ge(sems["s_meta"], 16)
            e.reg_mov(regs["s_par"], e.value_load(mt[0:1, 0:1]))
        r.on("sync", init_sync)

        def init_gp(e):
            for nm in ("g_par", "g_nonce", "g_cnt", "g_w16", "g_t0", "g_go"):
                regs[nm] = e.alloc_register(nm)
            e.wait_ge(sems["s_meta"], 16)
            e.reg_mov(regs["g_par"], e.value_load(mt[0:1, 0:1]))
            e.reg_mov(regs["g_nonce"], e.value_load(mt[0:1, 1:2]))
            e.reg_mov(regs["g_cnt"], 0)
        r.on("gpsimd", init_gp)

        def setup_pool(e):
            e.memset(ident[:], 0.0).then_inc(sems["c_po"], 1)
        r.on("gpsimd", setup_pool)
        ID0 = r.inc("c_po", 1)

        def setup_pool2(e):
            e.wait_ge(sems["c_po"], ID0)
            e.affine_select(out=ident[:], in_=ident[:], compare_op=ALU.not_equal,
                            fill=1.0, base=0, pattern=[[-1, 128]], channel_multiplier=1)
            e.memset(ones_l[:], 1.0)
            e.memset(ones_lb[:], 1.0)
            e.memset(ones_b[:], 1.0)
            e.memset(b_eln[:], EPS_LN)
            e.memset(b_shift[:], -SHIFT_C)
            e.memset(b_e8[:], EPS_BIT)
            e.memset(b_eqk[:], EPS_QK)
            e.memset(g_acc[:], 0.0)
        r.on("gpsimd", setup_pool2)
        SETUP_PO = r.emit_inc("gpsimd", "c_po", 1, lambda e: e.engine_nop())

        def par_dma(out_even, out_odd, in_ap, sem):
            v = r.inc(sem, 16)
            def f(e, oe=out_even, oo=out_odd, ia=in_ap, s=sem):
                with e.If_eq(regs["s_par"], 0):
                    e.dma_start(out=oe, in_=ia).then_inc(r.sems[s], 16)
                with e.Else():
                    e.dma_start(out=oo, in_=ia).then_inc(r.sems[s], 16)
                e.end_ifs()
            r.on("sync", f)
            return v

        def flag_write(col):
            return par_dma(sh_fl[0:1, col:col + 1], sh_fl[1:2, col:col + 1],
                           mt[0:1, 1:2], "s_kv")

        def poll_both(col):
            v = r.inc("c_rdy", 1)
            def f(e, col=col):
                for row in (0, 1):
                    e.reg_mov(regs["g_go"], 1)
                    with e.While(regs["g_go"]):
                        e.reg_add(regs["g_cnt"], regs["g_cnt"], 1)
                        e.dma_start(out=flg.ap().rearrange("o (r c) -> o r c", r=2),
                                    in_=sh_fl.rearrange("(o r) c -> o r c", o=1)).then_inc(r.sems["s_pl"], 16)
                        e.reg_mov(regs["g_w16"], regs["g_cnt"])
                        e.reg_mul(regs["g_w16"], regs["g_w16"], 16)
                        e.wait_ge(r.sems["s_pl"], regs["g_w16"])
                        e.reg_mov(regs["g_t0"], e.value_load(flg[0:1, row * 16 + col: row * 16 + col + 1]))
                        e.reg_alu(regs["g_go"], regs["g_t0"], regs["g_nonce"], ALU.not_equal)
                e.engine_nop().then_inc(r.sems["c_rdy"], 1)
            r.on("gpsimd", f)
            return v

        def phase_barrier(engs=("tensor", "vector", "scalar")):
            for eng in engs:
                for s in ("c_pe", "c_dv", "c_ac", "c_ev", "s_kv", "s_sp", "s_scr"):
                    r.wait(eng, s)

        for ct in range(CT):
            r.emit_inc("sync", "s_x", 16,
                       lambda e, ct=ct: e.dma_start(out=xT_a[:, ct * TB:(ct + 1) * TB],
                                                    in_=xT[0, ct * 128:(ct + 1) * 128, :]))
        X_OWN = r.cur("s_x")

        def gamma_jobs(jobs, bufs, dump, queue_fn, pre_gates=None):
            ac_base = r.cur("c_ac")
            gated = set()
            for i, (dram, r0, c0, width, col) in enumerate(jobs):
                qb = bufs[i % 2]
                eng, sem = queue_fn(i)
                if pre_gates and eng not in gated:
                    for (gs, gv) in pre_gates:
                        r.wait(eng, gs, gv)
                    gated.add(eng)
                if i >= 2:
                    r.wait(eng, "c_ac", ac_base + (i - 2) + 1)
                v = r.emit_inc(eng, sem, 16,
                               lambda e, r0=r0, c0=c0, d=dram, w=width, qb=qb:
                               e.dma_start(out=qb[:, 0:w], in_=d[r0:r0 + 128, c0:c0 + w]))
                r.wait("scalar", sem, v)
                if i == 0:
                    r.wait("scalar", "c_po", SETUP_PO)
                r.emit_inc("scalar", "c_ac", 1,
                           lambda e, w=width, col=col, qb=qb, dump=dump:
                           e.activation(out=dump[:, 0:w], in_=qb[:, 0:w],
                                        func=AF.Abs, accum_out=g_acc[:, col:col + 1]))
            return r.cur("c_ac")

        def gamma_finish(ac_done, gacc_lo, gacc_hi, g3_cols, gm_cols, flag_col, nw_scales):
            r.wait("vector", "c_ac", ac_done)
            for ci, (lo, hi) in enumerate(zip(gacc_lo, gacc_hi)):
                r.emit_inc("vector", "c_dv", 1,
                           lambda e, lo=lo, hi=hi, gc=g3_cols[ci]:
                           e.reduce_sum(g3[:, gc:gc + 1], g_acc[:, lo:hi], axis=mybir.AxisListType.X))
            GS = r.cur("c_dv")
            r.wait("tensor", "c_dv", GS)
            r.wait("tensor", "c_po", SETUP_PO)
            nclz = len(g3_cols)
            gc0 = g3_cols[0]
            v = r.emit_inc("tensor", "c_pe", 1,
                           lambda e, gc0=gc0, n=nclz: e.matmul(out=p0[0:1, gc0:gc0 + n], lhsT=ones_l[:],
                                                               rhs=g3[:, gc0:gc0 + n], start=True, stop=True))
            r.wait("vector", "c_pe", v)
            GP = r.emit_inc("vector", "c_dv", 1,
                            lambda e, gc0=gc0, n=nclz: e.tensor_copy(out=gpart[0:1, 0:n], in_=p0[0:1, gc0:gc0 + n]))
            r.wait("sync", "c_dv", GP)
            par_dma(sh_gm[0:1, gm_cols[0]:gm_cols[1]], sh_gm[1:2, gm_cols[0]:gm_cols[1]],
                    gpart[0:1, 0:nclz], "s_kv")
            r.wait("sync", "s_kv")
            flag_write(flag_col)
            RDY = poll_both(flag_col)
            r.wait("sync", "c_rdy", RDY)
            GM2 = r.emit_inc("sync", "s_q", 16,
                             lambda e, a=gm_cols[0]:
                             e.dma_start(out=gm2.ap().rearrange("o (r c) -> o r c", r=2),
                                         in_=sh_gm[0:2, a:a + 4].rearrange("(o r) c -> o r c", o=1)))
            r.wait("vector", "s_q", GM2)
            GT = r.emit_inc("vector", "c_dv", 1,
                            lambda e, n=nclz: e.tensor_add(gmrow[0:1, 0:n], gm2[0:1, 0:n], gm2[0:1, 4:4 + n]))
            r.wait("scalar", "c_dv", GT)
            for ci, sc in enumerate(nw_scales):
                r.emit_inc("scalar", "c_ac", 1,
                           lambda e, ci=ci, sc=sc: e.activation(out=rinv[0:1, ci:ci + 1], in_=gmrow[0:1, ci:ci + 1],
                                                                func=AF.Identity, scale=sc, bias=b_e8[:, :]))
            r.emit_inc("scalar", "c_ac", 1,
                       lambda e, n=nclz: e.activation(out=rinv[0:1, 0:n], in_=rinv[0:1, 0:n], func=AF.Ln))
            RI = r.emit_inc("scalar", "c_ac", 1,
                            lambda e, n=nclz: e.activation(out=rinv[0:1, 0:n], in_=rinv[0:1, 0:n],
                                                           func=AF.Exp, scale=-1.0))
            r.wait("tensor", "c_ac", RI)
            v = r.emit_inc("tensor", "c_pe", 1,
                           lambda e, n=nclz: e.matmul(out=p0[:, 8:8 + n], lhsT=ones_b[:, :],
                                                      rhs=rinv[0:1, 0:n], start=True, stop=True))
            r.wait("vector", "c_pe", v)
            RC = r.emit_inc("vector", "c_dv", 1,
                            lambda e, gc0=g3_cols[0], n=nclz:
                            e.tensor_copy(out=rcol[:, gc0:gc0 + n], in_=p0[:, 8:8 + n]))
            return RC

        def quant_stage(jobs, rc_ready, flag_map, ibase, queue_fn,
                        qin_gate_ac=None, prev_dv=None, prev_scr=None):
            dv_done = {}
            scr_done = {}
            r.wait("vector", "c_dv", rc_ready)
            if prev_scr is not None:
                r.wait("vector", "s_scr", prev_scr)
            gated = set()
            for i0, (wn, dram, b, hh, width, shdst, half) in enumerate(jobs):
                i = ibase + i0
                qbuf = qin0 if i % 2 == 0 else qin1
                obuf = qout0 if i % 2 == 0 else qout1
                wi = {"w1": 0, "w2": 1, "w3": 2}[wn]
                eng, sem = queue_fn(i0)
                if eng not in gated:
                    if qin_gate_ac is not None:
                        r.wait(eng, "c_ac", qin_gate_ac)
                    if prev_dv is not None:
                        r.wait(eng, "c_dv", prev_dv)
                    gated.add(eng)
                if i0 >= 2:
                    r.wait(eng, "c_dv", dv_done[i0 - 2])
                v = r.emit_inc(eng, sem, 16,
                               lambda e, b=b, hh=hh, d=dram, w=width, qb=qbuf:
                               e.dma_start(out=qb[:, 0:w],
                                           in_=d[b * 128:(b + 1) * 128, hh * 2816: hh * 2816 + w]))
                r.wait("vector", sem, v)
                if i0 >= 2:
                    r.wait("vector", "s_scr", scr_done[i0 - 2])
                r.emit_inc("vector", "c_dv", 1,
                           lambda e, w=width, wi=wi, qb=qbuf:
                           e.tensor_scalar(out=qpos[:, 0:w], in0=qb[:, 0:w],
                                           scalar1=rcol[:, wi:wi + 1], scalar2=0.5,
                                           op0=ALU.mult, op1=ALU.is_ge))
                r.emit_inc("vector", "c_dv", 1,
                           lambda e, w=width, wi=wi, qb=qbuf:
                           e.tensor_scalar(out=qneg[:, 0:w], in0=qb[:, 0:w],
                                           scalar1=rcol[:, wi:wi + 1], scalar2=-0.5,
                                           op0=ALU.mult, op1=ALU.is_le))
                r.emit_inc("vector", "c_dv", 1,
                           lambda e, w=width, ob=obuf:
                           e.tensor_sub(ob[:, 0:w], qpos[:, 0:w], qneg[:, 0:w]))
                if wn == "w3":
                    r.emit_inc("vector", "c_dv", 1,
                               lambda e, w=width, ob=obuf:
                               e.tensor_scalar_mul(ob[:, 0:w], ob[:, 0:w], 0.5))
                dv_done[i0] = r.cur("c_dv")
                r.wait("sync", "c_dv", dv_done[i0])
                if wn == "w3":
                    scr_done[i0] = par_dma(shdst[b * 128:(b + 1) * 128, :],
                                           shdst[half + b * 128: half + (b + 1) * 128, :],
                                           obuf[:, 0:width], "s_scr")
                else:
                    scr_done[i0] = par_dma(
                        shdst[b * 128:(b + 1) * 128, hh * 2816: (hh + 1) * 2816],
                        shdst[half + b * 128: half + (b + 1) * 128, hh * 2816: (hh + 1) * 2816],
                        obuf[:, 0:width], "s_scr")
                if (wn, b, hh) in flag_map:
                    r.wait("sync", "s_scr")
                    flag_write(flag_map[(wn, b, hh)])
            return ibase + len(jobs)

        g12jobs = [(w1th, b * 128, hh * 2816, 2816, b * 2 + hh) for b in range(8) for hh in range(2)] + \
                  [(w2th, b * 128, hh * 2816, 2816, 16 + b * 2 + hh) for b in range(8) for hh in range(2)]
        g3jobs = [(w3th, b * 128, hh * 1024, 1024, 32 + b * 2 + hh)
                  for b in range(22) for hh in range(2)]
        q12jobs = [("w1", w1th, b, hh, 2816, sh_w1T, 1024) for b in range(8) for hh in range(2)] + \
                  [("w2", w2th, b, hh, 2816, sh_w2T, 1024) for b in range(8) for hh in range(2)]
        q3jobs = [("w3", w3th, b, 0, C, sh_w3T, 2816) for b in range(22)]

        def emit_ln():
            phase_barrier()
            pe_hist = {}
            for ct in range(CT):
                sq_t = sqb if ct % 2 == 0 else tmpf
                if ct >= 2:
                    r.wait("scalar", "c_pe", pe_hist[ct - 2])
                va = r.emit_inc("scalar", "c_ac", 1,
                                lambda e, ct=ct, sq_t=sq_t: e.activation(out=sq_t[:, :],
                                                              in_=xT_a[:, ct * TB:(ct + 1) * TB],
                                                              func=AF.Square))
                r.wait("tensor", "c_ac", va)
                for th in range(2):
                    r.emit_inc("tensor", "c_pe", 1,
                               lambda e, ct=ct, th=th:
                               e.matmul(out=p2[0:1, th * 512:(th + 1) * 512], lhsT=ones_l[:],
                                        rhs=xT_a[:, ct * TB + th * 512: ct * TB + (th + 1) * 512],
                                        start=(ct == 0), stop=(ct == CT - 1)))
                    r.emit_inc("tensor", "c_pe", 1,
                               lambda e, th=th, ct=ct, sq_t=sq_t:
                               e.matmul(out=p3[0:1, th * 512:(th + 1) * 512], lhsT=ones_l[:],
                                        rhs=sq_t[:, th * 512:(th + 1) * 512],
                                        start=(ct == 0), stop=(ct == CT - 1)))
                pe_hist[ct] = r.cur("c_pe")
            r.wait("vector", "c_pe")
            r.emit_inc("vector", "c_dv", 1, lambda e: e.tensor_scalar_mul(rowsA[:, :], p2[0:1, :], 1.0 / C))
            r.emit_inc("vector", "c_dv", 1, lambda e: e.tensor_scalar_mul(rowsB[:, :], p3[0:1, :], 1.0 / C))
            r.emit_inc("vector", "c_dv", 1, lambda e: e.tensor_mul(sqb[0:1, 0:TB], rowsA[:, :], rowsA[:, :]))
            v_dv = r.emit_inc("vector", "c_dv", 1,
                              lambda e: e.tensor_sub(rowsB[:, :], rowsB[:, :], sqb[0:1, 0:TB]))
            r.wait("scalar", "c_dv", v_dv)
            r.emit_inc("scalar", "c_ac", 1,
                       lambda e: e.activation(out=rowsB[:, :], in_=rowsB[:, :], func=AF.Ln, bias=b_eln[:, :]))
            v_ac = r.emit_inc("scalar", "c_ac", 1,
                              lambda e: e.activation(out=rowsB[:, :], in_=rowsB[:, :], func=AF.Exp, scale=-0.5))
            r.wait("tensor", "c_ac", v_ac)
            for th in range(2):
                r.emit_inc("tensor", "c_pe", 1,
                           lambda e, th=th: e.matmul(out=p2[:, th * 512:(th + 1) * 512], lhsT=ones_b[:, :],
                                                     rhs=rowsA[0:1, th * 512:(th + 1) * 512], start=True, stop=True))
                r.emit_inc("tensor", "c_pe", 1,
                           lambda e, th=th: e.matmul(out=p3[:, th * 512:(th + 1) * 512], lhsT=ones_b[:, :],
                                                     rhs=rowsB[0:1, th * 512:(th + 1) * 512], start=True, stop=True))
            r.wait("vector", "c_pe")
            for ct in range(CT):
                for th in range(2):
                    r.emit_inc("vector", "c_dv", 1,
                               lambda e, ct=ct, th=th:
                               e.tensor_sub(tmpf[:, th * 512:(th + 1) * 512],
                                            xT_a[:, ct * TB + th * 512: ct * TB + (th + 1) * 512],
                                            p2[:, th * 512:(th + 1) * 512]))
                    r.emit_inc("vector", "c_dv", 1,
                               lambda e, ct=ct, th=th:
                               e.tensor_mul(lnT_a[:, ct * TB + th * 512: ct * TB + (th + 1) * 512],
                                            tmpf[:, th * 512:(th + 1) * 512],
                                            p3[:, th * 512:(th + 1) * 512]))
            return r.cur("c_dv")

        def emit_ffn_pass(ln_dv, pending=None):
            pending = list(pending or [])
            defer_w3 = bool(pending)
            phase_barrier(("tensor",))
            r.wait("sync", "c_rdy")
            r.wait("sync", "c_pe")
            r.wait("sync", "c_dv")
            hj_pe = {}
            sig_ac = {}
            hct_dv = {}
            y_done_prev = 0
            for chn in range(NCHUNK):
                for j in range(CHUNK):
                    jj = chn * CHUNK + j
                    buf = jj % 2
                    if jj >= 2:
                        r.wait("sync", "c_pe", hj_pe[jj - 2])
                    r.emit_inc("sync", "s_lw", 16,
                               lambda e, jj=jj, buf=buf:
                               e.dma_start(out=w12b[:, buf * C:(buf + 1) * C]
                                           .rearrange("p (ct h) -> p ct h", ct=CT),
                                           in_=sh_w1T[:, jj * 128:(jj + 1) * 128]
                                           .rearrange("(ct p) h -> p ct h", p=128)))
                    v12 = r.emit_inc("sync", "s_lw", 16,
                                     lambda e, jj=jj, buf=buf:
                                     e.dma_start(out=w12b[:, (2 + buf) * C:(3 + buf) * C]
                                                 .rearrange("p (ct h) -> p ct h", ct=CT),
                                                 in_=sh_w2T[:, jj * 128:(jj + 1) * 128]
                                                 .rearrange("(ct p) h -> p ct h", p=128)))
                    if chn == 0 and defer_w3:
                        pass
                    else:
                        if chn >= 1:
                            r.wait("sync", "c_pe", y_done_prev)
                        r.emit_inc("sync", "s_lw", 16,
                                   lambda e, jj=jj, j=j:
                                   e.dma_start(out=w3ch[:, j * C:(j + 1) * C],
                                               in_=sh_w3T[jj * 128:(jj + 1) * 128, :]))
                    r.wait("tensor", "s_lw", v12)
                    if jj == 0:
                        r.wait("tensor", "c_dv", ln_dv)
                    if jj >= 1:
                        r.wait("tensor", "c_ac", sig_ac[jj - 1])
                    for th in range(2):
                        for ct in range(CT):
                            r.emit_inc("tensor", "c_pe", 1,
                                       lambda e, th=th, ct=ct, buf=buf:
                                       e.matmul(out=p0[:, th * 512:(th + 1) * 512],
                                                lhsT=w12b[:, buf * C + ct * 128: buf * C + (ct + 1) * 128],
                                                rhs=lnT_a[:, ct * TB + th * 512: ct * TB + (th + 1) * 512],
                                                start=(ct == 0), stop=(ct == CT - 1)))
                    v_x1 = r.cur("c_pe")
                    if jj >= 1:
                        r.wait("tensor", "c_dv", hct_dv[jj - 1])
                    for th in range(2):
                        for ct in range(CT):
                            r.emit_inc("tensor", "c_pe", 1,
                                       lambda e, th=th, ct=ct, buf=buf:
                                       e.matmul(out=p1[:, th * 512:(th + 1) * 512],
                                                lhsT=w12b[:, (2 + buf) * C + ct * 128: (2 + buf) * C + (ct + 1) * 128],
                                                rhs=lnT_a[:, ct * TB + th * 512: ct * TB + (th + 1) * 512],
                                                start=(ct == 0), stop=(ct == CT - 1)))
                    hj_pe[jj] = r.cur("c_pe")
                    r.wait("scalar", "c_pe", v_x1)
                    if jj >= 1:
                        r.wait("scalar", "c_dv", hct_dv[jj - 1])
                    sig_ac[jj] = r.emit_inc("scalar", "c_ac", 1,
                                            lambda e: e.activation(out=sgb[:, :], in_=p0[:, :], func=AF.Sigmoid))
                    r.wait("vector", "c_ac", sig_ac[jj])
                    r.wait("vector", "c_pe", hj_pe[jj])
                    hct_dv[jj] = r.emit_inc("vector", "c_dv", 1,
                                            lambda e, j=j: e.tensor_mul(hact[:, j * TB:(j + 1) * TB],
                                                                        sgb[:, :], p1[:, :]))
                    if pending and chn == 0:
                        if j == 6:
                            r.wait("vector", "s_kv")
                        for _ in range(3):
                            if pending:
                                pending.pop(0)()
                while pending:
                    pending.pop(0)()
                if chn == 0 and defer_w3:
                    r.wait("sync", "c_dv")
                    r.wait("sync", "c_ac")
                    for j2 in range(CHUNK):
                        jj2 = chn * CHUNK + j2
                        r.emit_inc("sync", "s_lw", 16,
                                   lambda e, jj2=jj2, j2=j2:
                                   e.dma_start(out=w3ch[:, j2 * C:(j2 + 1) * C],
                                               in_=sh_w3T[jj2 * 128:(jj2 + 1) * 128, :]))
                if chn == 0:
                    r.wait("tensor", "c_dv")
                    r.wait("tensor", "c_ac")
                r.wait("tensor", "c_dv", hct_dv[chn * CHUNK + CHUNK - 1])
                r.wait("tensor", "s_lw")
                yev = {}
                for ct in range(CT):
                    py = p2 if ct % 2 == 0 else p3
                    if ct >= 2:
                        r.wait("tensor", "c_ev", yev[ct - 2])
                    for j in range(CHUNK):
                        for th in range(2):
                            r.emit_inc("tensor", "c_pe", 1,
                                       lambda e, j=j, th=th, ct=ct, py=py:
                                       e.matmul(out=py[:, th * 512:(th + 1) * 512],
                                                lhsT=w3ch[:, j * C + ct * 128: j * C + (ct + 1) * 128],
                                                rhs=hact[:, j * TB + th * 512: j * TB + (th + 1) * 512],
                                                start=(j == 0), stop=(j == CHUNK - 1)))
                    v_y = r.cur("c_pe")
                    r.wait("vector", "c_pe", v_y)
                    for th in range(2):
                        r.emit_inc("vector", "c_dv", 1,
                                   lambda e, ct=ct, th=th, py=py:
                                   e.tensor_add(xT_a[:, ct * TB + th * 512: ct * TB + (th + 1) * 512],
                                                xT_a[:, ct * TB + th * 512: ct * TB + (th + 1) * 512],
                                                py[:, th * 512:(th + 1) * 512]))
                    yev[ct] = r.emit_inc("vector", "c_ev", 1, lambda e: e.engine_nop())
                y_done_prev = r.cur("c_pe")
            return r.cur("c_ev")

        kraw = [arena[:, 16384 + i * TB: 16384 + (i + 1) * TB] for i in range(NKV)]
        wkv2 = arena[:, 24576:24576 + CT * 512]

        def emit_qk(slot, ln_dv, do_q):
            phase_barrier()
            r.wait("gpsimd", "c_pe")
            r.wait("gpsimd", "c_dv")
            tiles = []
            if do_q:
                tiles += [("q", h) for h in range(NQH)]
            tiles += [("k", g) for g in range(NKV)]
            loads = {}
            evac_v = {}
            kload_v = None
            for i, (kind, idx) in enumerate(tiles):
                pp = (p0, p1)[i % 2]
                if kind == "q":
                    buf = i % 2
                    if i >= 2:
                        r.wait("gpsimd", "c_pe", evac_v.get(i - 2, (0, 0))[1])
                    loads[i] = r.emit_inc("gpsimd", "s_aw", 16,
                                          lambda e, idx=idx, buf=buf:
                                          e.dma_start(out=wqb[:, buf * C:(buf + 1) * C]
                                                      .rearrange("p (ct d) -> p ct d", ct=CT),
                                                      in_=wq[:, idx * 128:(idx + 1) * 128]
                                                      .rearrange("(ct p) d -> p ct d", p=128)))
                elif kload_v is None:
                    if do_q:
                        r.wait("gpsimd", "c_pe", evac_v[i - 1][1])
                    kload_v = r.emit_inc("gpsimd", "s_aw", 16,
                                         lambda e: e.dma_start(
                                             out=wkv2.rearrange("p (ct d) -> p ct d", ct=CT),
                                             in_=wk.rearrange("(ct p) d -> p ct d", p=128)))
                    loads[i] = kload_v
                else:
                    loads[i] = kload_v
                r.wait("tensor", "s_aw", loads[i])
                if i == 0:
                    r.wait("tensor", "c_dv", ln_dv)
                if i >= 2:
                    r.wait("tensor", "c_ac", evac_v[i - 2][0])
                for th in range(2):
                    for ct in range(CT):
                        def mk(e, kind=kind, idx=idx, th=th, ct=ct, pp=pp, buf=(i % 2)):
                            if kind == "q":
                                lh = wqb[:, buf * C + ct * 128: buf * C + (ct + 1) * 128]
                            else:
                                lh = wkv2[:, ct * 512 + idx * 128: ct * 512 + idx * 128 + 128]
                            return e.matmul(out=pp[:, th * 512:(th + 1) * 512], lhsT=lh,
                                            rhs=lnT_a[:, ct * TB + th * 512: ct * TB + (th + 1) * 512],
                                            start=(ct == 0), stop=(ct == CT - 1))
                        r.emit_inc("tensor", "c_pe", 1, mk)
                proj_v = r.cur("c_pe")
                r.wait("scalar", "c_pe", proj_v)
                dst = qT_a[:, idx * TB:(idx + 1) * TB] if kind == "q" else kraw[idx]
                va = r.emit_inc("scalar", "c_ac", 1,
                                lambda e, pp=pp, dst=dst: e.activation(out=dst, in_=pp[:, :], func=AF.Identity))
                evac_v[i] = (va, proj_v)
            EV_ALL = r.cur("c_ac")
            r.wait("gpsimd", "c_pe")
            v_wv = r.emit_inc("gpsimd", "s_aw", 16,
                              lambda e: e.dma_start(
                                  out=wkvb.rearrange("p (ct d) -> p ct d", ct=CT),
                                  in_=wv.rearrange("(ct p) d -> p ct d", p=128)))
            vdv = {}
            for tt in range(8):
                pv = p2 if tt % 2 == 0 else p3
                r.wait("tensor", "s_aw", v_wv)
                r.wait("tensor", "c_dv", ln_dv)
                if tt >= 2:
                    r.wait("tensor", "c_dv", vdv[tt - 2])
                for ct in range(CT):
                    r.emit_inc("tensor", "c_pe", 1,
                               lambda e, tt=tt, ct=ct, pv=pv:
                               e.matmul(out=pv[:, 0:512],
                                        lhsT=lnT_a[:, ct * TB + tt * 128: ct * TB + tt * 128 + 128],
                                        rhs=wkvb[:, ct * 512:(ct + 1) * 512],
                                        start=(ct == 0), stop=(ct == CT - 1)))
                v_v = r.cur("c_pe")
                r.wait("vector", "c_pe", v_v)
                if tt == 0:
                    r.wait("vector", "s_kv")
                vdv[tt] = r.emit_inc("vector", "c_dv", 1,
                                     lambda e, tt=tt, pv=pv:
                                     e.tensor_copy(out=vstage[:, tt * 512:(tt + 1) * 512], in_=pv[:, 0:512]))
            def v_write_cl():
                r.wait("sync", "c_dv", vdv[7])
                par_dma(sh_v[slot * TB:(slot + 1) * TB, :].rearrange("(a p) d -> p a d", p=128),
                        sh_v[4096 + slot * TB: 4096 + (slot + 1) * TB, :].rearrange("(a p) d -> p a d", p=128),
                        vstage.rearrange("p (a d) -> p a d", a=8), "s_kv")
            pending = [v_write_cl]
            sq_v = {}
            ss_v = {}
            le_v = {}
            bc_v = {}
            mu_v = {}
            wr_v = {}

            def tile_chain(i, kind, idx):
                pn = (p2, p3)[i % 2]
                sq_t = (sqb, tmpf)[i % 2]
                row_t = (rowsA, rowsB)[i % 2]
                src_t = qT_a[:, idx * TB:(idx + 1) * TB] if kind == "q" else kraw[idx]
                r.wait("scalar", "c_ac", evac_v[i][0])
                if i >= 2:
                    r.wait("scalar", "c_pe", ss_v[i - 2])
                sq_v[i] = r.emit_inc("scalar", "c_ac", 1,
                                     lambda e, src_t=src_t, sq_t=sq_t:
                                     e.activation(out=sq_t[:, :], in_=src_t, func=AF.Square))
                r.wait("tensor", "c_ac", sq_v[i])
                if i >= 2:
                    r.wait("tensor", "c_dv", mu_v[i - 2])
                elif i < 2:
                    r.wait("tensor", "c_dv", vdv[7] if (6 + i) not in vdv else vdv[6 + i])
                for th in range(2):
                    r.emit_inc("tensor", "c_pe", 1,
                               lambda e, th=th, sq_t=sq_t, pn=pn:
                               e.matmul(out=pn[0:1, th * 512:(th + 1) * 512], lhsT=ones_l[:],
                                        rhs=sq_t[:, th * 512:(th + 1) * 512], start=True, stop=True))
                ss_v[i] = r.cur("c_pe")
                r.wait("scalar", "c_pe", ss_v[i])
                if i >= 2:
                    r.wait("scalar", "c_pe", bc_v[i - 2])
                r.emit_inc("scalar", "c_ac", 1,
                           lambda e, pn=pn, row_t=row_t: e.activation(out=row_t[:, :], in_=pn[0:1, :], func=AF.Ln,
                                                                      scale=1.0 / D, bias=b_eqk[:, :]))
                le_v[i] = r.emit_inc("scalar", "c_ac", 1,
                                     lambda e, row_t=row_t: e.activation(out=row_t[:, :], in_=row_t[:, :],
                                                                         func=AF.Exp, scale=-0.5))
                r.wait("tensor", "c_ac", le_v[i])
                for th in range(2):
                    r.emit_inc("tensor", "c_pe", 1,
                               lambda e, th=th, pn=pn, row_t=row_t:
                               e.matmul(out=pn[:, th * 512:(th + 1) * 512], lhsT=ones_b[:, :],
                                        rhs=row_t[0:1, th * 512:(th + 1) * 512], start=True, stop=True))
                bc_v[i] = r.cur("c_pe")
                r.wait("vector", "c_pe", bc_v[i])
                if kind == "k" and i >= 2 and (i - 2) in wr_v:
                    r.wait("vector", "s_kv", wr_v[i - 2])
                mu_v[i] = r.emit_inc("vector", "c_dv", 1,
                                     lambda e, src_t=src_t, pn=pn:
                                     e.tensor_mul(src_t, src_t, pn[:, :]))
                if kind == "k":
                    r.wait("sync", "c_dv", mu_v[i])
                    wr_v[i] = par_dma(
                        sh_kT[idx * 128:(idx + 1) * 128, slot * TB:(slot + 1) * TB],
                        sh_kT[idx * 128:(idx + 1) * 128, 4096 + slot * TB: 4096 + (slot + 1) * TB],
                        src_t, "s_kv")

            for i, (kind, idx) in enumerate(tiles):
                pending.append(lambda i=i, kind=kind, idx=idx: tile_chain(i, kind, idx))

            def flag_cl():
                r.wait("sync", "s_kv")
                flag_write(4 + slot)
            pending.append(flag_cl)
            if do_q:
                def spill_q_cl():
                    r.wait("sync", "c_dv", mu_v[NQH - 1] if (NQH - 1) in mu_v else r.cur("c_dv"))
                    for h in range(NQH):
                        r.wait("sync", "c_dv", mu_v.get(h, 0))
                        r.emit_inc("sync", "s_sp", 16,
                                   lambda e, h=h: e.dma_start(out=qsp[h * 128:(h + 1) * 128, :],
                                                              in_=qT_a[:, h * TB:(h + 1) * TB]))
                pending.append(spill_q_cl)
            return pending

        for eng in ("scalar", "tensor", "vector"):
            r.wait(eng, "s_x", X_OWN)
        ln1 = emit_ln()
        ln1_pe = r.cur("c_pe")
        q12_alt = lambda i: ("sync", "s_g") if i % 2 == 0 else ("gpsimd", "s_g2")
        gp_only = lambda i: ("gpsimd", "s_g2")
        G12_AC = gamma_jobs(g12jobs, (qin0, qin1), gdump, q12_alt)
        RC12 = gamma_finish(G12_AC, [0, 16], [16, 32], [0, 1], (0, 2), 0,
                            [1.0 / NW_12, 1.0 / NW_12])
        G3_AC = gamma_jobs(g3jobs, (sqb, tmpf), sgb, gp_only,
                           pre_gates=[("c_dv", ln1), ("c_pe", ln1_pe)])
        nxt = quant_stage(q12jobs, RC12, {("w2", 7, 1): 1}, 0, q12_alt,
                          qin_gate_ac=G12_AC)
        DV12 = r.cur("c_dv")
        RC3 = gamma_finish(G3_AC, [32], [76], [2], (4, 5), 8, [1.0 / NW_3])
        quant_stage(q3jobs, RC3, {("w3", 21, 0): 3}, nxt, gp_only, prev_dv=DV12,
                    prev_scr=r.cur("s_scr"))
        poll_both(1)
        poll_both(3)
        emit_ffn_pass(ln1)
        ln2 = emit_ln()
        ln2_marks = (r.cur("c_dv"), r.cur("c_ac"), r.cur("c_pe"))
        own_pending = emit_qk(0, ln2, do_q=True)
        r.wait("sync", "c_dv", ln2_marks[0])
        r.wait("sync", "c_ac", ln2_marks[1])
        r.wait("sync", "c_pe", ln2_marks[2])
        for ct in range(CT):
            r.emit_inc("sync", "s_sp", 16,
                       lambda e, ct=ct: e.dma_start(out=xsp[ct * 128:(ct + 1) * 128, :],
                                                    in_=xT_a[:, ct * TB:(ct + 1) * TB]))
        SPILL_V = None

        prev_marks = ln2_marks
        pend = own_pending
        for blk in range(1, NBLK):
            r.wait("sync", "c_dv", prev_marks[0])
            r.wait("sync", "c_ac", prev_marks[1])
            r.wait("sync", "c_pe", prev_marks[2])
            for ct in range(CT):
                r.emit_inc("sync", "s_x", 16,
                           lambda e, ct=ct, blk=blk: e.dma_start(out=xT_a[:, ct * TB:(ct + 1) * TB],
                                                                 in_=xT[blk, ct * 128:(ct + 1) * 128, :]))
            vx = r.cur("s_x")
            for eng in ("scalar", "tensor", "vector"):
                r.wait(eng, "s_x", vx)
            lnb = emit_ln()
            emit_ffn_pass(lnb, pending=pend)
            ln2b = emit_ln()
            prev_marks = (r.cur("c_dv"), r.cur("c_ac"), r.cur("c_pe"))
            pend = emit_qk(blk, ln2b, do_q=False)
        while pend:
            pend.pop(0)()

        for s_ in range(4):
            poll_both(4 + s_)
        KV_RDY = r.cur("c_rdy")
        r.wait("sync", "c_rdy", KV_RDY)
        for s_ in ("c_dv", "c_ev", "c_pe", "c_ac"):
            r.wait("sync", s_)
        for h in range(NQH):
            r.emit_inc("sync", "s_q", 16,
                       lambda e, h=h: e.dma_start(out=qT_a[:, h * TB:(h + 1) * TB],
                                                  in_=qsp[h * 128:(h + 1) * 128, :]))
        QT_V = r.cur("s_q")
        for ct in range(CT):
            r.emit_inc("sync", "s_sp", 16,
                       lambda e, ct=ct: e.dma_start(out=xT_a[:, ct * TB:(ct + 1) * TB],
                                                    in_=xsp[ct * 128:(ct + 1) * 128, :]))
        XRL_V = r.cur("s_sp")

        ot_dv = 0
        exp_vals = []
        otm_vals = {}
        it = 0
        for g in range(NKV):
            if g >= 1:
                r.wait("sync", "c_pe")
            r.emit_inc("sync", "s_q", 16,
                       lambda e, g=g: e.dma_start(out=kTg, in_=sh_kT[g * 128:(g + 1) * 128, :]))
            KV_V = r.emit_inc("sync", "s_q", 16,
                              lambda e, g=g: e.dma_start(
                                  out=vg.rearrange("p (kt d) -> p kt d", kt=KT),
                                  in_=sh_v[:, g * 128:(g + 1) * 128].rearrange("(kt p) d -> p kt d", p=128)))
            for h4 in range(4):
                h = g * 4 + h4
                for qb in range(2):
                    pX = (p2, p3)[it % 2]
                    if it == 0:
                        r.wait("tensor", "s_q", max(KV_V, QT_V))
                        r.wait("tensor", "c_ac")
                        r.wait("tensor", "c_dv")
                    elif qb == 0 and h4 == 0:
                        r.wait("tensor", "s_q", KV_V)

                    def emit_pv(p_, first, pX=pX):
                        r.wait("tensor", "c_ac", exp_vals[-1 - ((len(exp_vals) - 1) - p_ - it * 32)]
                               if False else exp_vals[it * 32 + p_])
                        if first and it >= 2:
                            r.wait("tensor", "c_dv", otm_vals[it - 2])
                        for k2 in (2 * p_, 2 * p_ + 1):
                            r.emit_inc("tensor", "c_pe", 1,
                                       lambda e, k2=k2, pX=pX:
                                       e.matmul(out=pX[:, 0:512],
                                                lhsT=vg[:, k2 * 128:(k2 + 1) * 128],
                                                rhs=etb[:, (k2 % ETS) * 512:((k2 % ETS) + 1) * 512],
                                                start=(k2 == 0), stop=(k2 == KT - 1)))
                            r.emit_inc("tensor", "c_pe", 1,
                                       lambda e, k2=k2, pX=pX:
                                       e.matmul(out=pX[0:1, 512:1024],
                                                lhsT=ones_lb[:],
                                                rhs=etb[:, (k2 % ETS) * 512:((k2 % ETS) + 1) * 512],
                                                start=(k2 == 0), stop=(k2 == KT - 1)))

                    for p_ in range(KT // 2):
                        for half in (0, 1):
                            kt = 2 * p_ + half
                            gidx = it * KT + kt
                            stg = p0 if p_ % 2 == 0 else p1
                            if gidx >= 4 and (gidx - 4) // 2 < len(exp_vals):
                                r.wait("tensor", "c_ac", exp_vals[(gidx - 4) // 2])
                            r.emit_inc("tensor", "c_pe", 1,
                                       lambda e, kt=kt, h=h, qb=qb, stg=stg, half=half:
                                       e.matmul(out=stg[:, half * 512:(half + 1) * 512],
                                                lhsT=kTg[:, kt * 128:(kt + 1) * 128],
                                                rhs=qT_a[:, h * TB + qb * 512: h * TB + (qb + 1) * 512],
                                                start=True, stop=True))
                        v_st = r.cur("c_pe")
                        slot = (2 * p_) % ETS
                        stg = p0 if p_ % 2 == 0 else p1
                        r.wait("scalar", "c_pe", v_st)
                        va = r.emit_inc("scalar", "c_ac", 1,
                                        lambda e, slot=slot, stg=stg:
                                        e.activation(out=etb[:, slot * 512: (slot + 2) * 512],
                                                     in_=stg[:, :], func=AF.Exp,
                                                     scale=INV_SQRT_D, bias=b_shift[:, :]))
                        exp_vals.append(va)
                        if p_ >= 1:
                            emit_pv(p_ - 1, first=(p_ == 1))
                    emit_pv(KT // 2 - 1, first=False)
                    v_pv = r.cur("c_pe")
                    r.wait("scalar", "c_pe", v_pv)
                    r.emit_inc("scalar", "c_ac", 1,
                               lambda e, pX=pX: e.activation(out=rowsB[0:1, 0:512], in_=pX[0:1, 512:1024],
                                                             func=AF.Ln))
                    v_rc = r.emit_inc("scalar", "c_ac", 1,
                                      lambda e: e.activation(out=rowsB[0:1, 0:512], in_=rowsB[0:1, 0:512],
                                                             func=AF.Exp, scale=-1.0))
                    r.wait("tensor", "c_ac", v_rc)
                    v_bc = r.emit_inc("tensor", "c_pe", 1,
                                      lambda e, pX=pX: e.matmul(out=pX[:, 512:1024], lhsT=ones_b[:, :],
                                                                rhs=rowsB[0:1, 0:512], start=True, stop=True))
                    r.wait("vector", "c_pe", v_bc)
                    r.emit_inc("vector", "c_dv", 1,
                               lambda e, pX=pX: e.tensor_copy(out=tmpf[:, 0:512], in_=pX[:, 0:512]))
                    otm_vals[it] = r.emit_inc("vector", "c_dv", 1,
                                              lambda e, h=h, qb=qb, pX=pX:
                                              e.tensor_mul(lnT_a[:, h * TB + qb * 512: h * TB + (qb + 1) * 512],
                                                           tmpf[:, 0:512], pX[:, 512:1024]))
                    it += 1
        ot_dv = otm_vals[it - 1]

        wo_pe = {}
        for chw in range(4):
            buf = chw % 2
            if chw <= 1:
                r.wait("gpsimd", "c_dv", ot_dv)
            else:
                r.wait("gpsimd", "c_pe", wo_pe[chw - 2])
            v_wo = r.emit_inc("gpsimd", "s_aw", 16,
                              lambda e, chw=chw, buf=buf:
                              e.dma_start(out=wob[:, buf * 4 * C:(buf + 1) * 4 * C]
                                          .rearrange("p (b c) -> p b c", b=4),
                                          in_=wo[chw * 512:(chw + 1) * 512, :]
                                          .rearrange("(b p) c -> p b c", p=128)))
            r.wait("tensor", "s_aw", v_wo)
            r.wait("tensor", "c_dv", ot_dv)
            if chw == 0:
                r.wait("tensor", "s_sp", XRL_V)
                r.wait("vector", "s_sp", XRL_V)
            yev = {}
            for ct in range(CT):
                pyw = p2 if ct % 2 == 0 else p3
                if ct >= 2:
                    r.wait("tensor", "c_ev", yev[ct - 2])
                for b4 in range(4):
                    hd = chw * 4 + b4
                    for th in range(2):
                        r.emit_inc("tensor", "c_pe", 1,
                                   lambda e, b4=b4, hd=hd, th=th, ct=ct, pyw=pyw, buf=buf:
                                   e.matmul(out=pyw[:, th * 512:(th + 1) * 512],
                                            lhsT=wob[:, buf * 4 * C + b4 * C + ct * 128:
                                                     buf * 4 * C + b4 * C + (ct + 1) * 128],
                                            rhs=lnT_a[:, hd * TB + th * 512: hd * TB + (th + 1) * 512],
                                            start=(b4 == 0), stop=(b4 == 3)))
                v_y = r.cur("c_pe")
                r.wait("vector", "c_pe", v_y)
                for th in range(2):
                    r.emit_inc("vector", "c_dv", 1,
                               lambda e, ct=ct, th=th, pyw=pyw:
                               e.tensor_add(xT_a[:, ct * TB + th * 512: ct * TB + (th + 1) * 512],
                                            xT_a[:, ct * TB + th * 512: ct * TB + (th + 1) * 512],
                                            pyw[:, th * 512:(th + 1) * 512]))
                yev[ct] = r.emit_inc("vector", "c_ev", 1, lambda e: e.engine_nop())
            wo_pe[chw] = r.cur("c_pe")

        ln3 = emit_ln()
        ffn3 = emit_ffn_pass(ln3)
        r.wait("sync", "c_ev", ffn3)
        for ct in range(CT):
            r.emit_inc("sync", "s_out", 16,
                       lambda e, ct=ct: e.dma_start(out=yT[ct * 128:(ct + 1) * 128, :],
                                                    in_=xT_a[:, ct * TB:(ct + 1) * TB]))
        r.wait("sync", "s_out")

        with nc.Block() as block:
            @block.sync
            def _(e):
                for f in r.ops["sync"]:
                    f(e)

            @block.gpsimd
            def _(e):
                for f in r.ops["gpsimd"]:
                    f(e)

            @block.vector
            def _(e):
                for f in r.ops["vector"]:
                    f(e)

            @block.scalar
            def _(e):
                for f in r.ops["scalar"]:
                    f(e)

            @block.tensor
            def _(e):
                for f in r.ops["tensor"]:
                    f(e)

    return nc


_NC_CACHE = {}


def kernel(L0_attn_x_q, L0_attn_Wq, L0_attn_Wk, L0_attn_Wv, L0_attn_Wo,
           L0_ffn_w1, L0_ffn_w2, L0_ffn_w3):
    x = np.ascontiguousarray(np.asarray(L0_attn_x_q, dtype=np.float32)[0])
    wq_ = np.ascontiguousarray(np.asarray(L0_attn_Wq, dtype=np.float32).reshape(C, 2048))
    wk_ = np.ascontiguousarray(np.asarray(L0_attn_Wk, dtype=np.float32).reshape(C, 512))
    wv_ = np.ascontiguousarray(np.asarray(L0_attn_Wv, dtype=np.float32).reshape(C, 512))
    wo_ = np.ascontiguousarray(np.asarray(L0_attn_Wo, dtype=np.float32))
    w1T = np.ascontiguousarray(np.asarray(L0_ffn_w1, dtype=np.float32).T)
    w2T = np.ascontiguousarray(np.asarray(L0_ffn_w2, dtype=np.float32).T)
    w3T = np.ascontiguousarray(np.asarray(L0_ffn_w3, dtype=np.float32).T)

    xTb = np.ascontiguousarray(x.reshape(8, TB, C).transpose(0, 2, 1))

    nonce = int(np.random.default_rng().integers(1 << 20, (1 << 31) - 1))
    in_maps = []
    for core in range(8):
        par = core % 2
        blocks = [core] + [b for b in range(par, 8, 2) if b != core]
        m = np.zeros((1, 16), np.int32)
        m[0, 0] = par
        m[0, 1] = nonce
        in_maps.append({
            "meta": m,
            "xT": np.ascontiguousarray(xTb[blocks]),
            "w1th": np.ascontiguousarray(w1T[par * 1024:(par + 1) * 1024, :]),
            "w2th": np.ascontiguousarray(w2T[par * 1024:(par + 1) * 1024, :]),
            "w3th": np.ascontiguousarray(w3T[par * 2816:(par + 1) * 2816, :]),
            "wq": wq_, "wk": wk_, "wv": wv_, "wo": wo_,
        })
    if "nc" not in _NC_CACHE:
        _NC_CACHE["nc"] = build_program()
    res = run_bass_kernel_spmd(_NC_CACHE["nc"], in_maps, core_ids=list(range(8)))
    out = np.empty((SEQ, C), np.float32)
    for core in range(8):
        out[core * TB:(core + 1) * TB, :] = res.results[core]["yT"].T
    return out[None, :, :]


# version 14
# speedup vs baseline: 1.3445x; 1.0020x over previous
import numpy as np
from contextlib import ExitStack

import concourse.bass as bass
import concourse.mybir as mybir
from concourse.bass_utils import run_bass_kernel_spmd

F32 = mybir.dt.float32
BF16 = mybir.dt.bfloat16
I32 = mybir.dt.int32
AF = mybir.ActivationFunctionType
ALU = mybir.AluOpType

C = 2048
CT = 16
TB = 1024
HF = 5632
NJ = 44
CHUNK = 11
NCHUNK = 4
D = 128
NQH = 16
NKV = 4
SEQ = 8192
KT = 64
NBLK = 4
NW_12 = HF * C
NW_3 = C * HF
EPS_BIT = 1e-8
EPS_LN = 1e-5
EPS_QK = 1e-6
INV_SQRT_D = float(1.0 / np.sqrt(128.0))
SHIFT_C = float(np.sqrt(128.0) + 1e-3)
ETS = 6

ENGINES = ("sync", "gpsimd", "vector", "scalar", "tensor")


class Rec:
    def __init__(self):
        self.ops = {e: [] for e in ENGINES}
        self.cnt = {}
        self.sems = None

    def inc(self, sem, amt):
        self.cnt[sem] = self.cnt.get(sem, 0) + amt
        return self.cnt[sem]

    def cur(self, sem):
        return self.cnt.get(sem, 0)

    def on(self, eng, fn):
        self.ops[eng].append(fn)

    def wait(self, eng, sem, val=None):
        v = self.cur(sem) if val is None else val
        if v > 0:
            self.on(eng, lambda e, s=sem, vv=v: e.wait_ge(self.sems[s], vv))

    def emit_inc(self, eng, sem, amt, builder):
        v = self.inc(sem, amt)
        self.on(eng, lambda e, b=builder, s=sem, a=amt: b(e).then_inc(self.sems[s], a))
        return v


def build_program():
    nc = bass.Bass("TRN2", target_bir_lowering=False, debug=False)
    r = Rec()

    meta = nc.dram_tensor("meta", [1, 16], I32, kind="ExternalInput").ap()
    xT = nc.dram_tensor("xT", [NBLK, C, TB], F32, kind="ExternalInput").ap()
    w1th = nc.dram_tensor("w1th", [1024, HF], F32, kind="ExternalInput").ap()
    w2th = nc.dram_tensor("w2th", [1024, HF], F32, kind="ExternalInput").ap()
    w3th = nc.dram_tensor("w3th", [2816, C], F32, kind="ExternalInput").ap()
    wq = nc.dram_tensor("wq", [C, 2048], F32, kind="ExternalInput").ap()
    wk = nc.dram_tensor("wk", [C, 512], F32, kind="ExternalInput").ap()
    wv = nc.dram_tensor("wv", [C, 512], F32, kind="ExternalInput").ap()
    wo = nc.dram_tensor("wo", [2048, C], F32, kind="ExternalInput").ap()
    yT = nc.dram_tensor("yT", [C, TB], F32, kind="ExternalOutput").ap()

    sh_w1T = nc.dram_tensor("sh_w1T", [C, HF], BF16, addr_space="Shared").ap()
    sh_w2T = nc.dram_tensor("sh_w2T", [C, HF], BF16, addr_space="Shared").ap()
    sh_w3T = nc.dram_tensor("sh_w3T", [HF, C], BF16, addr_space="Shared").ap()
    sh_kT = nc.dram_tensor("sh_kT", [NKV * 128, SEQ], BF16, addr_space="Shared").ap()
    sh_v = nc.dram_tensor("sh_v", [SEQ, NKV * 128], BF16, addr_space="Shared").ap()
    sh_gm = nc.dram_tensor("sh_gm", [2, 16], F32, addr_space="Shared").ap()
    sh_fl = nc.dram_tensor("sh_fl", [2, 16], I32, addr_space="Shared").ap()
    xsp = nc.dram_tensor("xsp", [C, TB], F32).ap()
    qsp = nc.dram_tensor("qsp", [NQH * 128, TB], BF16).ap()

    es = ExitStack()

    def sb(name, shape, dt):
        return es.enter_context(nc.sbuf_tensor(name, shape, dt))

    with es:
        mt = sb("mt", [1, 16], I32)
        ident = sb("ident", [128, 128], BF16)
        ones_l = sb("ones_l", [128, 1], F32)
        ones_lb = sb("ones_lb", [128, 1], BF16)
        ones_b = sb("ones_b", [1, 128], F32)
        b_eln = sb("b_eln", [1, 1], F32)
        b_shift = sb("b_shift", [128, 1], F32)
        b_e8 = sb("b_e8", [1, 1], F32)
        b_eqk = sb("b_eqk", [1, 1], F32)

        xT_a = sb("xT_a", [128, CT * TB], F32)
        lnT_a = sb("lnT_a", [128, CT * TB], BF16)
        arena_t = sb("arena", [128, 33792], BF16)
        arena = arena_t[:, :]
        w12b = sb("w12b", [128, 4 * C], BF16)
        etb = sb("etb", [128, ETS * 512], BF16)
        sgb = sb("sgb", [128, TB], BF16)
        sqb = sb("sqb", [128, TB], F32)
        tmpf = sb("tmpf", [128, TB], F32)
        rowsA = sb("rowsA", [1, TB], F32)
        rowsB = sb("rowsB", [1, TB], F32)
        g_acc = sb("g_acc", [128, 80], F32)
        g3 = sb("g3", [128, 4], F32)
        gpart = sb("gpart", [1, 4], F32)
        gm2 = sb("gm2", [1, 8], F32)
        gmrow = sb("gmrow", [1, 4], F32)
        rinv = sb("rinv", [1, 4], F32)
        rcol = sb("rcol", [128, 4], F32)
        o_sb = sb("o_sb", [128, 128], BF16)
        rc1 = sb("rc1", [128, 1], F32)
        flg = sb("flg", [1, 32], I32)

        w3ch = arena[:, 0:CHUNK * C]
        hact = arena[:, CHUNK * C: CHUNK * C + CHUNK * TB]
        qT_a = arena[:, 0:NQH * TB]
        kTg = arena[:, 16384:16384 + SEQ]
        vg = arena[:, 24576:24576 + KT * 128]
        wob = arena[:, 0:2 * 4 * C]
        wkvb = arena[:, 16384:16384 + CT * 512]
        wqb = arena[:, 24576:24576 + 2 * C]
        vstage = arena[:, 28672:28672 + 8 * 512]
        arena_f32 = arena_t[:, :].bitcast(F32)
        kv_sq = [arena_f32[:, 0:1024], arena_f32[:, 1024:2048]]
        kv_ev = [arena_f32[:, 2048:3072], arena_f32[:, 3072:4096]]
        q_sq = [arena_f32[:, 8192:9216], arena_f32[:, 9216:10240]]
        q_ev = [arena_f32[:, 10240:11264], arena_f32[:, 11264:12288]]
        kout2 = vstage[:, 0:1024]
        qin0 = arena_f32[:, 0:2816]
        qin1 = arena_f32[:, 2816:5632]
        qpos = arena[:, 11264:11264 + 2816]
        qneg = arena[:, 14080:14080 + 2816]
        qout0 = arena[:, 16896:16896 + 2816]
        qout1 = arena[:, 19712:19712 + 2816]
        gdump = arena[:, 16896:16896 + 2816]

        def psum(name):
            return es.enter_context(nc.psum_tensor(name, [128, 1024], F32))
        p0 = psum("p0")
        p1 = psum("p1")
        p2 = psum("p2")
        p3 = psum("p3")

        sems = {}
        for s in ["s_meta", "s_x", "s_g", "s_g2", "s_q", "s_scr", "s_lw", "s_kv", "s_sp",
                  "s_out", "s_aw", "s_pl",
                  "c_po", "c_pe", "c_dv", "c_ac", "c_ev", "c_rdy"]:
            sems[s] = es.enter_context(nc.semaphore(s))
        r.sems = sems
        regs = {}

        r.emit_inc("sync", "s_meta", 16, lambda e: e.dma_start(out=mt[:], in_=meta))

        def init_sync(e):
            regs["s_par"] = e.alloc_register("s_par")
            e.wait_ge(sems["s_meta"], 16)
            e.reg_mov(regs["s_par"], e.value_load(mt[0:1, 0:1]))
        r.on("sync", init_sync)

        def init_gp(e):
            for nm in ("g_par", "g_nonce", "g_cnt", "g_w16", "g_t0", "g_go"):
                regs[nm] = e.alloc_register(nm)
            e.wait_ge(sems["s_meta"], 16)
            e.reg_mov(regs["g_par"], e.value_load(mt[0:1, 0:1]))
            e.reg_mov(regs["g_nonce"], e.value_load(mt[0:1, 1:2]))
            e.reg_mov(regs["g_cnt"], 0)
        r.on("gpsimd", init_gp)

        def setup_pool(e):
            e.memset(ident[:], 0.0).then_inc(sems["c_po"], 1)
        r.on("gpsimd", setup_pool)
        ID0 = r.inc("c_po", 1)

        def setup_pool2(e):
            e.wait_ge(sems["c_po"], ID0)
            e.affine_select(out=ident[:], in_=ident[:], compare_op=ALU.not_equal,
                            fill=1.0, base=0, pattern=[[-1, 128]], channel_multiplier=1)
            e.memset(ones_l[:], 1.0)
            e.memset(ones_lb[:], 1.0)
            e.memset(ones_b[:], 1.0)
            e.memset(b_eln[:], EPS_LN)
            e.memset(b_shift[:], -SHIFT_C)
            e.memset(b_e8[:], EPS_BIT)
            e.memset(b_eqk[:], EPS_QK)
            e.memset(g_acc[:], 0.0)
        r.on("gpsimd", setup_pool2)
        SETUP_PO = r.emit_inc("gpsimd", "c_po", 1, lambda e: e.engine_nop())

        def par_dma(out_even, out_odd, in_ap, sem):
            v = r.inc(sem, 16)
            def f(e, oe=out_even, oo=out_odd, ia=in_ap, s=sem):
                with e.If_eq(regs["s_par"], 0):
                    e.dma_start(out=oe, in_=ia).then_inc(r.sems[s], 16)
                with e.Else():
                    e.dma_start(out=oo, in_=ia).then_inc(r.sems[s], 16)
                e.end_ifs()
            r.on("sync", f)
            return v

        def flag_write(col):
            return par_dma(sh_fl[0:1, col:col + 1], sh_fl[1:2, col:col + 1],
                           mt[0:1, 1:2], "s_kv")

        def poll_both(col):
            v = r.inc("c_rdy", 1)
            def f(e, col=col):
                for row in (0, 1):
                    e.reg_mov(regs["g_go"], 1)
                    with e.While(regs["g_go"]):
                        e.reg_add(regs["g_cnt"], regs["g_cnt"], 1)
                        e.dma_start(out=flg.ap().rearrange("o (r c) -> o r c", r=2),
                                    in_=sh_fl.rearrange("(o r) c -> o r c", o=1)).then_inc(r.sems["s_pl"], 16)
                        e.reg_mov(regs["g_w16"], regs["g_cnt"])
                        e.reg_mul(regs["g_w16"], regs["g_w16"], 16)
                        e.wait_ge(r.sems["s_pl"], regs["g_w16"])
                        e.reg_mov(regs["g_t0"], e.value_load(flg[0:1, row * 16 + col: row * 16 + col + 1]))
                        e.reg_alu(regs["g_go"], regs["g_t0"], regs["g_nonce"], ALU.not_equal)
                e.engine_nop().then_inc(r.sems["c_rdy"], 1)
            r.on("gpsimd", f)
            return v

        def phase_barrier(engs=("tensor", "vector", "scalar")):
            for eng in engs:
                for s in ("c_pe", "c_dv", "c_ac", "c_ev", "s_kv", "s_sp", "s_scr"):
                    r.wait(eng, s)

        for ct in range(CT):
            r.emit_inc("sync", "s_x", 16,
                       lambda e, ct=ct: e.dma_start(out=xT_a[:, ct * TB:(ct + 1) * TB],
                                                    in_=xT[0, ct * 128:(ct + 1) * 128, :]))
        X_OWN = r.cur("s_x")

        def gamma_jobs(jobs, bufs, dump, queue_fn, pre_gates=None):
            ac_base = r.cur("c_ac")
            gated = set()
            for i, (dram, r0, c0, width, col) in enumerate(jobs):
                qb = bufs[i % 2]
                eng, sem = queue_fn(i)
                if pre_gates and eng not in gated:
                    for (gs, gv) in pre_gates:
                        r.wait(eng, gs, gv)
                    gated.add(eng)
                if i >= 2:
                    r.wait(eng, "c_ac", ac_base + (i - 2) + 1)
                v = r.emit_inc(eng, sem, 16,
                               lambda e, r0=r0, c0=c0, d=dram, w=width, qb=qb:
                               e.dma_start(out=qb[:, 0:w], in_=d[r0:r0 + 128, c0:c0 + w]))
                r.wait("scalar", sem, v)
                if i == 0:
                    r.wait("scalar", "c_po", SETUP_PO)
                r.emit_inc("scalar", "c_ac", 1,
                           lambda e, w=width, col=col, qb=qb, dump=dump:
                           e.activation(out=dump[:, 0:w], in_=qb[:, 0:w],
                                        func=AF.Abs, accum_out=g_acc[:, col:col + 1]))
            return r.cur("c_ac")

        def gamma_finish(ac_done, gacc_lo, gacc_hi, g3_cols, gm_cols, flag_col, nw_scales):
            r.wait("vector", "c_ac", ac_done)
            for ci, (lo, hi) in enumerate(zip(gacc_lo, gacc_hi)):
                r.emit_inc("vector", "c_dv", 1,
                           lambda e, lo=lo, hi=hi, gc=g3_cols[ci]:
                           e.reduce_sum(g3[:, gc:gc + 1], g_acc[:, lo:hi], axis=mybir.AxisListType.X))
            GS = r.cur("c_dv")
            r.wait("tensor", "c_dv", GS)
            r.wait("tensor", "c_po", SETUP_PO)
            nclz = len(g3_cols)
            gc0 = g3_cols[0]
            v = r.emit_inc("tensor", "c_pe", 1,
                           lambda e, gc0=gc0, n=nclz: e.matmul(out=p0[0:1, gc0:gc0 + n], lhsT=ones_l[:],
                                                               rhs=g3[:, gc0:gc0 + n], start=True, stop=True))
            r.wait("vector", "c_pe", v)
            GP = r.emit_inc("vector", "c_dv", 1,
                            lambda e, gc0=gc0, n=nclz: e.tensor_copy(out=gpart[0:1, 0:n], in_=p0[0:1, gc0:gc0 + n]))
            r.wait("sync", "c_dv", GP)
            par_dma(sh_gm[0:1, gm_cols[0]:gm_cols[1]], sh_gm[1:2, gm_cols[0]:gm_cols[1]],
                    gpart[0:1, 0:nclz], "s_kv")
            r.wait("sync", "s_kv")
            flag_write(flag_col)
            RDY = poll_both(flag_col)
            r.wait("sync", "c_rdy", RDY)
            GM2 = r.emit_inc("sync", "s_q", 16,
                             lambda e, a=gm_cols[0]:
                             e.dma_start(out=gm2.ap().rearrange("o (r c) -> o r c", r=2),
                                         in_=sh_gm[0:2, a:a + 4].rearrange("(o r) c -> o r c", o=1)))
            r.wait("vector", "s_q", GM2)
            GT = r.emit_inc("vector", "c_dv", 1,
                            lambda e, n=nclz: e.tensor_add(gmrow[0:1, 0:n], gm2[0:1, 0:n], gm2[0:1, 4:4 + n]))
            r.wait("scalar", "c_dv", GT)
            for ci, sc in enumerate(nw_scales):
                r.emit_inc("scalar", "c_ac", 1,
                           lambda e, ci=ci, sc=sc: e.activation(out=rinv[0:1, ci:ci + 1], in_=gmrow[0:1, ci:ci + 1],
                                                                func=AF.Identity, scale=sc, bias=b_e8[:, :]))
            r.emit_inc("scalar", "c_ac", 1,
                       lambda e, n=nclz: e.activation(out=rinv[0:1, 0:n], in_=rinv[0:1, 0:n], func=AF.Ln))
            RI = r.emit_inc("scalar", "c_ac", 1,
                            lambda e, n=nclz: e.activation(out=rinv[0:1, 0:n], in_=rinv[0:1, 0:n],
                                                           func=AF.Exp, scale=-1.0))
            r.wait("tensor", "c_ac", RI)
            v = r.emit_inc("tensor", "c_pe", 1,
                           lambda e, n=nclz: e.matmul(out=p0[:, 8:8 + n], lhsT=ones_b[:, :],
                                                      rhs=rinv[0:1, 0:n], start=True, stop=True))
            r.wait("vector", "c_pe", v)
            RC = r.emit_inc("vector", "c_dv", 1,
                            lambda e, gc0=g3_cols[0], n=nclz:
                            e.tensor_copy(out=rcol[:, gc0:gc0 + n], in_=p0[:, 8:8 + n]))
            return RC

        def quant_stage(jobs, rc_ready, flag_map, ibase, queue_fn,
                        qin_gate_ac=None, prev_dv=None, prev_scr=None):
            dv_done = {}
            scr_done = {}
            r.wait("vector", "c_dv", rc_ready)
            if prev_scr is not None:
                r.wait("vector", "s_scr", prev_scr)
            gated = set()
            for i0, (wn, dram, b, hh, width, shdst, half) in enumerate(jobs):
                i = ibase + i0
                qbuf = qin0 if i % 2 == 0 else qin1
                obuf = qout0 if i % 2 == 0 else qout1
                wi = {"w1": 0, "w2": 1, "w3": 2}[wn]
                eng, sem = queue_fn(i0)
                if eng not in gated:
                    if qin_gate_ac is not None:
                        r.wait(eng, "c_ac", qin_gate_ac)
                    if prev_dv is not None:
                        r.wait(eng, "c_dv", prev_dv)
                    gated.add(eng)
                if i0 >= 2:
                    r.wait(eng, "c_dv", dv_done[i0 - 2])
                v = r.emit_inc(eng, sem, 16,
                               lambda e, b=b, hh=hh, d=dram, w=width, qb=qbuf:
                               e.dma_start(out=qb[:, 0:w],
                                           in_=d[b * 128:(b + 1) * 128, hh * 2816: hh * 2816 + w]))
                r.wait("vector", sem, v)
                if i0 >= 2:
                    r.wait("vector", "s_scr", scr_done[i0 - 2])
                r.emit_inc("vector", "c_dv", 1,
                           lambda e, w=width, wi=wi, qb=qbuf:
                           e.tensor_scalar(out=qpos[:, 0:w], in0=qb[:, 0:w],
                                           scalar1=rcol[:, wi:wi + 1], scalar2=0.5,
                                           op0=ALU.mult, op1=ALU.is_ge))
                r.emit_inc("vector", "c_dv", 1,
                           lambda e, w=width, wi=wi, qb=qbuf:
                           e.tensor_scalar(out=qneg[:, 0:w], in0=qb[:, 0:w],
                                           scalar1=rcol[:, wi:wi + 1], scalar2=-0.5,
                                           op0=ALU.mult, op1=ALU.is_le))
                r.emit_inc("vector", "c_dv", 1,
                           lambda e, w=width, ob=obuf:
                           e.tensor_sub(ob[:, 0:w], qpos[:, 0:w], qneg[:, 0:w]))
                if wn == "w3":
                    r.emit_inc("vector", "c_dv", 1,
                               lambda e, w=width, ob=obuf:
                               e.tensor_scalar_mul(ob[:, 0:w], ob[:, 0:w], 0.5))
                dv_done[i0] = r.cur("c_dv")
                r.wait("sync", "c_dv", dv_done[i0])
                if wn == "w3":
                    scr_done[i0] = par_dma(shdst[b * 128:(b + 1) * 128, :],
                                           shdst[half + b * 128: half + (b + 1) * 128, :],
                                           obuf[:, 0:width], "s_scr")
                else:
                    scr_done[i0] = par_dma(
                        shdst[b * 128:(b + 1) * 128, hh * 2816: (hh + 1) * 2816],
                        shdst[half + b * 128: half + (b + 1) * 128, hh * 2816: (hh + 1) * 2816],
                        obuf[:, 0:width], "s_scr")
                if (wn, b, hh) in flag_map:
                    r.wait("sync", "s_scr")
                    flag_write(flag_map[(wn, b, hh)])
            return ibase + len(jobs)

        g12jobs = [(w1th, b * 128, hh * 2816, 2816, b * 2 + hh) for b in range(8) for hh in range(2)] + \
                  [(w2th, b * 128, hh * 2816, 2816, 16 + b * 2 + hh) for b in range(8) for hh in range(2)]
        g3jobs = [(w3th, b * 128, hh * 1024, 1024, 32 + b * 2 + hh)
                  for b in range(22) for hh in range(2)]
        q12jobs = [("w1", w1th, b, hh, 2816, sh_w1T, 1024) for b in range(8) for hh in range(2)] + \
                  [("w2", w2th, b, hh, 2816, sh_w2T, 1024) for b in range(8) for hh in range(2)]
        q3jobs = [("w3", w3th, b, 0, C, sh_w3T, 2816) for b in range(22)]

        def emit_ln():
            phase_barrier()
            for eng in ("tensor", "scalar", "vector"):
                r.wait(eng, "c_po", SETUP_PO)
            pe_hist = {}
            for ct in range(CT):
                sq_t = sqb if ct % 2 == 0 else tmpf
                if ct >= 2:
                    r.wait("scalar", "c_pe", pe_hist[ct - 2])
                va = r.emit_inc("scalar", "c_ac", 1,
                                lambda e, ct=ct, sq_t=sq_t: e.activation(out=sq_t[:, :],
                                                              in_=xT_a[:, ct * TB:(ct + 1) * TB],
                                                              func=AF.Square))
                r.wait("tensor", "c_ac", va)
                for th in range(2):
                    r.emit_inc("tensor", "c_pe", 1,
                               lambda e, ct=ct, th=th:
                               e.matmul(out=p2[0:1, th * 512:(th + 1) * 512], lhsT=ones_l[:],
                                        rhs=xT_a[:, ct * TB + th * 512: ct * TB + (th + 1) * 512],
                                        start=(ct == 0), stop=(ct == CT - 1)))
                    r.emit_inc("tensor", "c_pe", 1,
                               lambda e, th=th, ct=ct, sq_t=sq_t:
                               e.matmul(out=p3[0:1, th * 512:(th + 1) * 512], lhsT=ones_l[:],
                                        rhs=sq_t[:, th * 512:(th + 1) * 512],
                                        start=(ct == 0), stop=(ct == CT - 1)))
                pe_hist[ct] = r.cur("c_pe")
            r.wait("vector", "c_pe")
            r.emit_inc("vector", "c_dv", 1, lambda e: e.tensor_scalar_mul(rowsA[:, :], p2[0:1, :], 1.0 / C))
            r.emit_inc("vector", "c_dv", 1, lambda e: e.tensor_scalar_mul(rowsB[:, :], p3[0:1, :], 1.0 / C))
            r.emit_inc("vector", "c_dv", 1, lambda e: e.tensor_mul(sqb[0:1, 0:TB], rowsA[:, :], rowsA[:, :]))
            v_dv = r.emit_inc("vector", "c_dv", 1,
                              lambda e: e.tensor_sub(rowsB[:, :], rowsB[:, :], sqb[0:1, 0:TB]))
            r.wait("scalar", "c_dv", v_dv)
            r.emit_inc("scalar", "c_ac", 1,
                       lambda e: e.activation(out=rowsB[:, :], in_=rowsB[:, :], func=AF.Ln, bias=b_eln[:, :]))
            v_ac = r.emit_inc("scalar", "c_ac", 1,
                              lambda e: e.activation(out=rowsB[:, :], in_=rowsB[:, :], func=AF.Exp, scale=-0.5))
            r.wait("tensor", "c_ac", v_ac)
            for th in range(2):
                r.emit_inc("tensor", "c_pe", 1,
                           lambda e, th=th: e.matmul(out=p2[:, th * 512:(th + 1) * 512], lhsT=ones_b[:, :],
                                                     rhs=rowsA[0:1, th * 512:(th + 1) * 512], start=True, stop=True))
                r.emit_inc("tensor", "c_pe", 1,
                           lambda e, th=th: e.matmul(out=p3[:, th * 512:(th + 1) * 512], lhsT=ones_b[:, :],
                                                     rhs=rowsB[0:1, th * 512:(th + 1) * 512], start=True, stop=True))
            r.wait("vector", "c_pe")
            for ct in range(CT):
                for th in range(2):
                    r.emit_inc("vector", "c_dv", 1,
                               lambda e, ct=ct, th=th:
                               e.tensor_sub(tmpf[:, th * 512:(th + 1) * 512],
                                            xT_a[:, ct * TB + th * 512: ct * TB + (th + 1) * 512],
                                            p2[:, th * 512:(th + 1) * 512]))
                    r.emit_inc("vector", "c_dv", 1,
                               lambda e, ct=ct, th=th:
                               e.tensor_mul(lnT_a[:, ct * TB + th * 512: ct * TB + (th + 1) * 512],
                                            tmpf[:, th * 512:(th + 1) * 512],
                                            p3[:, th * 512:(th + 1) * 512]))
            return r.cur("c_dv")

        def emit_ffn_pass(ln_dv, pending=None):
            pending = list(pending or [])
            defer_w3 = bool(pending)
            phase_barrier(("tensor",))
            r.wait("sync", "c_rdy")
            r.wait("sync", "c_pe")
            r.wait("sync", "c_dv")
            hj_pe = {}
            sig_ac = {}
            hct_dv = {}
            y_done_prev = 0
            for chn in range(NCHUNK):
                for j in range(CHUNK):
                    jj = chn * CHUNK + j
                    buf = jj % 2
                    if jj >= 2:
                        r.wait("sync", "c_pe", hj_pe[jj - 2])
                    r.emit_inc("sync", "s_lw", 16,
                               lambda e, jj=jj, buf=buf:
                               e.dma_start(out=w12b[:, buf * C:(buf + 1) * C]
                                           .rearrange("p (ct h) -> p ct h", ct=CT),
                                           in_=sh_w1T[:, jj * 128:(jj + 1) * 128]
                                           .rearrange("(ct p) h -> p ct h", p=128)))
                    v12 = r.emit_inc("sync", "s_lw", 16,
                                     lambda e, jj=jj, buf=buf:
                                     e.dma_start(out=w12b[:, (2 + buf) * C:(3 + buf) * C]
                                                 .rearrange("p (ct h) -> p ct h", ct=CT),
                                                 in_=sh_w2T[:, jj * 128:(jj + 1) * 128]
                                                 .rearrange("(ct p) h -> p ct h", p=128)))
                    if chn == 0 and defer_w3:
                        pass
                    else:
                        if chn >= 1:
                            r.wait("sync", "c_pe", y_done_prev)
                        r.emit_inc("sync", "s_lw", 16,
                                   lambda e, jj=jj, j=j:
                                   e.dma_start(out=w3ch[:, j * C:(j + 1) * C],
                                               in_=sh_w3T[jj * 128:(jj + 1) * 128, :]))
                    r.wait("tensor", "s_lw", v12)
                    if jj == 0:
                        r.wait("tensor", "c_dv", ln_dv)
                    if jj >= 1:
                        r.wait("tensor", "c_ac", sig_ac[jj - 1])
                    for th in range(2):
                        for ct in range(CT):
                            r.emit_inc("tensor", "c_pe", 1,
                                       lambda e, th=th, ct=ct, buf=buf:
                                       e.matmul(out=p0[:, th * 512:(th + 1) * 512],
                                                lhsT=w12b[:, buf * C + ct * 128: buf * C + (ct + 1) * 128],
                                                rhs=lnT_a[:, ct * TB + th * 512: ct * TB + (th + 1) * 512],
                                                start=(ct == 0), stop=(ct == CT - 1)))
                    v_x1 = r.cur("c_pe")
                    if jj >= 1:
                        r.wait("tensor", "c_dv", hct_dv[jj - 1])
                    for th in range(2):
                        for ct in range(CT):
                            r.emit_inc("tensor", "c_pe", 1,
                                       lambda e, th=th, ct=ct, buf=buf:
                                       e.matmul(out=p1[:, th * 512:(th + 1) * 512],
                                                lhsT=w12b[:, (2 + buf) * C + ct * 128: (2 + buf) * C + (ct + 1) * 128],
                                                rhs=lnT_a[:, ct * TB + th * 512: ct * TB + (th + 1) * 512],
                                                start=(ct == 0), stop=(ct == CT - 1)))
                    hj_pe[jj] = r.cur("c_pe")
                    r.wait("scalar", "c_pe", v_x1)
                    if jj >= 1:
                        r.wait("scalar", "c_dv", hct_dv[jj - 1])
                    sig_ac[jj] = r.emit_inc("scalar", "c_ac", 1,
                                            lambda e: e.activation(out=sgb[:, :], in_=p0[:, :], func=AF.Sigmoid))
                    r.wait("vector", "c_ac", sig_ac[jj])
                    r.wait("vector", "c_pe", hj_pe[jj])
                    hct_dv[jj] = r.emit_inc("vector", "c_dv", 1,
                                            lambda e, j=j: e.tensor_mul(hact[:, j * TB:(j + 1) * TB],
                                                                        sgb[:, :], p1[:, :]))
                    if pending and chn == 0:
                        if j == 6:
                            r.wait("vector", "s_kv")
                        for _ in range(3):
                            if pending:
                                pending.pop(0)()
                while pending:
                    pending.pop(0)()
                if chn == 0 and defer_w3:
                    r.wait("sync", "c_dv")
                    r.wait("sync", "c_ac")
                    for j2 in range(CHUNK):
                        jj2 = chn * CHUNK + j2
                        r.emit_inc("sync", "s_lw", 16,
                                   lambda e, jj2=jj2, j2=j2:
                                   e.dma_start(out=w3ch[:, j2 * C:(j2 + 1) * C],
                                               in_=sh_w3T[jj2 * 128:(jj2 + 1) * 128, :]))
                if chn == 0:
                    r.wait("tensor", "c_dv")
                    r.wait("tensor", "c_ac")
                r.wait("tensor", "c_dv", hct_dv[chn * CHUNK + CHUNK - 1])
                r.wait("tensor", "s_lw")
                yev = {}
                for ct in range(CT):
                    py = p2 if ct % 2 == 0 else p3
                    if ct >= 2:
                        r.wait("tensor", "c_ev", yev[ct - 2])
                    for j in range(CHUNK):
                        for th in range(2):
                            r.emit_inc("tensor", "c_pe", 1,
                                       lambda e, j=j, th=th, ct=ct, py=py:
                                       e.matmul(out=py[:, th * 512:(th + 1) * 512],
                                                lhsT=w3ch[:, j * C + ct * 128: j * C + (ct + 1) * 128],
                                                rhs=hact[:, j * TB + th * 512: j * TB + (th + 1) * 512],
                                                start=(j == 0), stop=(j == CHUNK - 1)))
                    v_y = r.cur("c_pe")
                    r.wait("vector", "c_pe", v_y)
                    for th in range(2):
                        r.emit_inc("vector", "c_dv", 1,
                                   lambda e, ct=ct, th=th, py=py:
                                   e.tensor_add(xT_a[:, ct * TB + th * 512: ct * TB + (th + 1) * 512],
                                                xT_a[:, ct * TB + th * 512: ct * TB + (th + 1) * 512],
                                                py[:, th * 512:(th + 1) * 512]))
                    yev[ct] = r.emit_inc("vector", "c_ev", 1, lambda e: e.engine_nop())
                y_done_prev = r.cur("c_pe")
            return r.cur("c_ev")

        kraw = [arena[:, 16384 + i * TB: 16384 + (i + 1) * TB] for i in range(NKV)]
        wkv2 = arena[:, 24576:24576 + CT * 512]

        def emit_qk(slot, ln_dv, do_q):
            phase_barrier()
            r.wait("gpsimd", "c_pe")
            r.wait("gpsimd", "c_dv")
            tiles = []
            if do_q:
                tiles += [("q", h) for h in range(NQH)]
            tiles += [("k", g) for g in range(NKV)]
            loads = {}
            evac_v = {}
            kload_v = None
            for i, (kind, idx) in enumerate(tiles):
                pp = (p0, p1)[i % 2]
                if kind == "q":
                    buf = i % 2
                    if i >= 2:
                        r.wait("gpsimd", "c_pe", evac_v.get(i - 2, (0, 0))[1])
                    loads[i] = r.emit_inc("gpsimd", "s_aw", 16,
                                          lambda e, idx=idx, buf=buf:
                                          e.dma_start(out=wqb[:, buf * C:(buf + 1) * C]
                                                      .rearrange("p (ct d) -> p ct d", ct=CT),
                                                      in_=wq[:, idx * 128:(idx + 1) * 128]
                                                      .rearrange("(ct p) d -> p ct d", p=128)))
                elif kload_v is None:
                    if do_q:
                        r.wait("gpsimd", "c_pe", evac_v[i - 1][1])
                    kload_v = r.emit_inc("gpsimd", "s_aw", 16,
                                         lambda e: e.dma_start(
                                             out=wkv2.rearrange("p (ct d) -> p ct d", ct=CT),
                                             in_=wk.rearrange("(ct p) d -> p ct d", p=128)))
                    loads[i] = kload_v
                else:
                    loads[i] = kload_v
                r.wait("tensor", "s_aw", loads[i])
                if i == 0:
                    r.wait("tensor", "c_dv", ln_dv)
                if i >= 2:
                    r.wait("tensor", "c_ac", evac_v[i - 2][0])
                for th in range(2):
                    for ct in range(CT):
                        def mk(e, kind=kind, idx=idx, th=th, ct=ct, pp=pp, buf=(i % 2)):
                            if kind == "q":
                                lh = wqb[:, buf * C + ct * 128: buf * C + (ct + 1) * 128]
                            else:
                                lh = wkv2[:, ct * 512 + idx * 128: ct * 512 + idx * 128 + 128]
                            return e.matmul(out=pp[:, th * 512:(th + 1) * 512], lhsT=lh,
                                            rhs=lnT_a[:, ct * TB + th * 512: ct * TB + (th + 1) * 512],
                                            start=(ct == 0), stop=(ct == CT - 1))
                        r.emit_inc("tensor", "c_pe", 1, mk)
                proj_v = r.cur("c_pe")
                r.wait("scalar", "c_pe", proj_v)
                dst = qT_a[:, idx * TB:(idx + 1) * TB] if kind == "q" else kraw[idx]
                va = r.emit_inc("scalar", "c_ac", 1,
                                lambda e, pp=pp, dst=dst: e.activation(out=dst, in_=pp[:, :], func=AF.Identity))
                evac_v[i] = (va, proj_v)
            EV_ALL = r.cur("c_ac")
            r.wait("gpsimd", "c_pe")
            v_wv = r.emit_inc("gpsimd", "s_aw", 16,
                              lambda e: e.dma_start(
                                  out=wkvb.rearrange("p (ct d) -> p ct d", ct=CT),
                                  in_=wv.rearrange("(ct p) d -> p ct d", p=128)))
            vdv = {}
            for tt in range(8):
                pv = p2 if tt % 2 == 0 else p3
                r.wait("tensor", "s_aw", v_wv)
                r.wait("tensor", "c_dv", ln_dv)
                if tt >= 2:
                    r.wait("tensor", "c_dv", vdv[tt - 2])
                for ct in range(CT):
                    r.emit_inc("tensor", "c_pe", 1,
                               lambda e, tt=tt, ct=ct, pv=pv:
                               e.matmul(out=pv[:, 0:512],
                                        lhsT=lnT_a[:, ct * TB + tt * 128: ct * TB + tt * 128 + 128],
                                        rhs=wkvb[:, ct * 512:(ct + 1) * 512],
                                        start=(ct == 0), stop=(ct == CT - 1)))
                v_v = r.cur("c_pe")
                r.wait("vector", "c_pe", v_v)
                if tt == 0:
                    r.wait("vector", "s_kv")
                vdv[tt] = r.emit_inc("vector", "c_dv", 1,
                                     lambda e, tt=tt, pv=pv:
                                     e.tensor_copy(out=vstage[:, tt * 512:(tt + 1) * 512], in_=pv[:, 0:512]))
            def v_write_cl():
                r.wait("sync", "c_dv", vdv[7])
                par_dma(sh_v[slot * TB:(slot + 1) * TB, :].rearrange("(a p) d -> p a d", p=128),
                        sh_v[4096 + slot * TB: 4096 + (slot + 1) * TB, :].rearrange("(a p) d -> p a d", p=128),
                        vstage.rearrange("p (a d) -> p a d", a=8), "s_kv")
            pending = [v_write_cl]
            sq_v = {}
            ss_v = {}
            le_v = {}
            bc_v = {}
            mu_v = {}
            wr_v = {}

            def tile_chain(i, kind, idx):
                pn = (p2, p3)[i % 2]
                sq_t = (sqb, tmpf)[i % 2]
                row_t = (rowsA, rowsB)[i % 2]
                src_t = qT_a[:, idx * TB:(idx + 1) * TB] if kind == "q" else kraw[idx]
                r.wait("scalar", "c_ac", evac_v[i][0])
                if i >= 2:
                    r.wait("scalar", "c_pe", ss_v[i - 2])
                sq_v[i] = r.emit_inc("scalar", "c_ac", 1,
                                     lambda e, src_t=src_t, sq_t=sq_t:
                                     e.activation(out=sq_t[:, :], in_=src_t, func=AF.Square))
                r.wait("tensor", "c_ac", sq_v[i])
                if i >= 2:
                    r.wait("tensor", "c_dv", mu_v[i - 2])
                elif i < 2:
                    r.wait("tensor", "c_dv", vdv[7] if (6 + i) not in vdv else vdv[6 + i])
                for th in range(2):
                    r.emit_inc("tensor", "c_pe", 1,
                               lambda e, th=th, sq_t=sq_t, pn=pn:
                               e.matmul(out=pn[0:1, th * 512:(th + 1) * 512], lhsT=ones_l[:],
                                        rhs=sq_t[:, th * 512:(th + 1) * 512], start=True, stop=True))
                ss_v[i] = r.cur("c_pe")
                r.wait("scalar", "c_pe", ss_v[i])
                if i >= 2:
                    r.wait("scalar", "c_pe", bc_v[i - 2])
                r.emit_inc("scalar", "c_ac", 1,
                           lambda e, pn=pn, row_t=row_t: e.activation(out=row_t[:, :], in_=pn[0:1, :], func=AF.Ln,
                                                                      scale=1.0 / D, bias=b_eqk[:, :]))
                le_v[i] = r.emit_inc("scalar", "c_ac", 1,
                                     lambda e, row_t=row_t: e.activation(out=row_t[:, :], in_=row_t[:, :],
                                                                         func=AF.Exp, scale=-0.5))
                r.wait("tensor", "c_ac", le_v[i])
                for th in range(2):
                    r.emit_inc("tensor", "c_pe", 1,
                               lambda e, th=th, pn=pn, row_t=row_t:
                               e.matmul(out=pn[:, th * 512:(th + 1) * 512], lhsT=ones_b[:, :],
                                        rhs=row_t[0:1, th * 512:(th + 1) * 512], start=True, stop=True))
                bc_v[i] = r.cur("c_pe")
                r.wait("vector", "c_pe", bc_v[i])
                if kind == "k" and i >= 2 and (i - 2) in wr_v:
                    r.wait("vector", "s_kv", wr_v[i - 2])
                mu_v[i] = r.emit_inc("vector", "c_dv", 1,
                                     lambda e, src_t=src_t, pn=pn:
                                     e.tensor_mul(src_t, src_t, pn[:, :]))
                if kind == "k":
                    r.wait("sync", "c_dv", mu_v[i])
                    wr_v[i] = par_dma(
                        sh_kT[idx * 128:(idx + 1) * 128, slot * TB:(slot + 1) * TB],
                        sh_kT[idx * 128:(idx + 1) * 128, 4096 + slot * TB: 4096 + (slot + 1) * TB],
                        src_t, "s_kv")

            for i, (kind, idx) in enumerate(tiles):
                pending.append(lambda i=i, kind=kind, idx=idx: tile_chain(i, kind, idx))

            def flag_cl():
                r.wait("sync", "s_kv")
                flag_write(4 + slot)
            pending.append(flag_cl)
            if do_q:
                def spill_q_cl():
                    r.wait("sync", "c_dv", mu_v[NQH - 1] if (NQH - 1) in mu_v else r.cur("c_dv"))
                    for h in range(NQH):
                        r.wait("sync", "c_dv", mu_v.get(h, 0))
                        r.emit_inc("sync", "s_sp", 16,
                                   lambda e, h=h: e.dma_start(out=qsp[h * 128:(h + 1) * 128, :],
                                                              in_=qT_a[:, h * TB:(h + 1) * TB]))
                pending.append(spill_q_cl)
            return pending

        for eng in ("scalar", "tensor", "vector"):
            r.wait(eng, "s_x", X_OWN)
        ln1 = emit_ln()
        ln1_pe = r.cur("c_pe")
        q12_alt = lambda i: ("sync", "s_g") if i % 2 == 0 else ("gpsimd", "s_g2")
        gp_only = lambda i: ("gpsimd", "s_g2")
        G12_AC = gamma_jobs(g12jobs, (qin0, qin1), gdump, q12_alt)
        RC12 = gamma_finish(G12_AC, [0, 16], [16, 32], [0, 1], (0, 2), 0,
                            [1.0 / NW_12, 1.0 / NW_12])
        G3_AC = gamma_jobs(g3jobs, (sqb, tmpf), sgb, gp_only,
                           pre_gates=[("c_dv", ln1), ("c_pe", ln1_pe)])
        nxt = quant_stage(q12jobs, RC12, {("w2", 7, 1): 1}, 0, q12_alt,
                          qin_gate_ac=G12_AC)
        DV12 = r.cur("c_dv")
        RC3 = gamma_finish(G3_AC, [32], [76], [2], (4, 5), 8, [1.0 / NW_3])
        quant_stage(q3jobs, RC3, {("w3", 21, 0): 3}, nxt, gp_only, prev_dv=DV12,
                    prev_scr=r.cur("s_scr"))
        poll_both(1)
        poll_both(3)
        emit_ffn_pass(ln1)
        ln2 = emit_ln()
        ln2_marks = (r.cur("c_dv"), r.cur("c_ac"), r.cur("c_pe"))
        own_pending = emit_qk(0, ln2, do_q=True)
        r.wait("sync", "c_dv", ln2_marks[0])
        r.wait("sync", "c_ac", ln2_marks[1])
        r.wait("sync", "c_pe", ln2_marks[2])
        for ct in range(CT):
            r.emit_inc("sync", "s_sp", 16,
                       lambda e, ct=ct: e.dma_start(out=xsp[ct * 128:(ct + 1) * 128, :],
                                                    in_=xT_a[:, ct * TB:(ct + 1) * TB]))
        SPILL_V = None

        prev_marks = ln2_marks
        pend = own_pending
        for blk in range(1, NBLK):
            r.wait("sync", "c_dv", prev_marks[0])
            r.wait("sync", "c_ac", prev_marks[1])
            r.wait("sync", "c_pe", prev_marks[2])
            for ct in range(CT):
                r.emit_inc("sync", "s_x", 16,
                           lambda e, ct=ct, blk=blk: e.dma_start(out=xT_a[:, ct * TB:(ct + 1) * TB],
                                                                 in_=xT[blk, ct * 128:(ct + 1) * 128, :]))
            vx = r.cur("s_x")
            for eng in ("scalar", "tensor", "vector"):
                r.wait(eng, "s_x", vx)
            lnb = emit_ln()
            emit_ffn_pass(lnb, pending=pend)
            ln2b = emit_ln()
            prev_marks = (r.cur("c_dv"), r.cur("c_ac"), r.cur("c_pe"))
            pend = emit_qk(blk, ln2b, do_q=False)
        while pend:
            pend.pop(0)()

        for s_ in range(4):
            poll_both(4 + s_)
        KV_RDY = r.cur("c_rdy")
        r.wait("sync", "c_rdy", KV_RDY)
        for s_ in ("c_dv", "c_ev", "c_pe", "c_ac"):
            r.wait("sync", s_)
        for h in range(NQH):
            r.emit_inc("sync", "s_q", 16,
                       lambda e, h=h: e.dma_start(out=qT_a[:, h * TB:(h + 1) * TB],
                                                  in_=qsp[h * 128:(h + 1) * 128, :]))
        QT_V = r.cur("s_q")
        for ct in range(CT):
            r.emit_inc("sync", "s_sp", 16,
                       lambda e, ct=ct: e.dma_start(out=xT_a[:, ct * TB:(ct + 1) * TB],
                                                    in_=xsp[ct * 128:(ct + 1) * 128, :]))
        XRL_V = r.cur("s_sp")

        ot_dv = 0
        exp_vals = []
        otm_vals = {}
        it = 0
        for g in range(NKV):
            if g >= 1:
                r.wait("sync", "c_pe")
            r.emit_inc("sync", "s_q", 16,
                       lambda e, g=g: e.dma_start(out=kTg, in_=sh_kT[g * 128:(g + 1) * 128, :]))
            KV_V = r.emit_inc("sync", "s_q", 16,
                              lambda e, g=g: e.dma_start(
                                  out=vg.rearrange("p (kt d) -> p kt d", kt=KT),
                                  in_=sh_v[:, g * 128:(g + 1) * 128].rearrange("(kt p) d -> p kt d", p=128)))
            for h4 in range(4):
                h = g * 4 + h4
                for qb in range(2):
                    pX = (p2, p3)[it % 2]
                    if it == 0:
                        r.wait("tensor", "s_q", max(KV_V, QT_V))
                        r.wait("tensor", "c_ac")
                        r.wait("tensor", "c_dv")
                    elif qb == 0 and h4 == 0:
                        r.wait("tensor", "s_q", KV_V)

                    def emit_pv(p_, first, pX=pX):
                        r.wait("tensor", "c_ac", exp_vals[-1 - ((len(exp_vals) - 1) - p_ - it * 32)]
                               if False else exp_vals[it * 32 + p_])
                        if first and it >= 2:
                            r.wait("tensor", "c_dv", otm_vals[it - 2])
                        for k2 in (2 * p_, 2 * p_ + 1):
                            r.emit_inc("tensor", "c_pe", 1,
                                       lambda e, k2=k2, pX=pX:
                                       e.matmul(out=pX[:, 0:512],
                                                lhsT=vg[:, k2 * 128:(k2 + 1) * 128],
                                                rhs=etb[:, (k2 % ETS) * 512:((k2 % ETS) + 1) * 512],
                                                start=(k2 == 0), stop=(k2 == KT - 1)))
                            r.emit_inc("tensor", "c_pe", 1,
                                       lambda e, k2=k2, pX=pX:
                                       e.matmul(out=pX[0:1, 512:1024],
                                                lhsT=ones_lb[:],
                                                rhs=etb[:, (k2 % ETS) * 512:((k2 % ETS) + 1) * 512],
                                                start=(k2 == 0), stop=(k2 == KT - 1)))

                    for p_ in range(KT // 2):
                        for half in (0, 1):
                            kt = 2 * p_ + half
                            gidx = it * KT + kt
                            stg = p0 if p_ % 2 == 0 else p1
                            if gidx >= 4 and (gidx - 4) // 2 < len(exp_vals):
                                r.wait("tensor", "c_ac", exp_vals[(gidx - 4) // 2])
                            r.emit_inc("tensor", "c_pe", 1,
                                       lambda e, kt=kt, h=h, qb=qb, stg=stg, half=half:
                                       e.matmul(out=stg[:, half * 512:(half + 1) * 512],
                                                lhsT=kTg[:, kt * 128:(kt + 1) * 128],
                                                rhs=qT_a[:, h * TB + qb * 512: h * TB + (qb + 1) * 512],
                                                start=True, stop=True))
                        v_st = r.cur("c_pe")
                        slot = (2 * p_) % ETS
                        stg = p0 if p_ % 2 == 0 else p1
                        r.wait("scalar", "c_pe", v_st)
                        va = r.emit_inc("scalar", "c_ac", 1,
                                        lambda e, slot=slot, stg=stg:
                                        e.activation(out=etb[:, slot * 512: (slot + 2) * 512],
                                                     in_=stg[:, :], func=AF.Exp,
                                                     scale=INV_SQRT_D, bias=b_shift[:, :]))
                        exp_vals.append(va)
                        if p_ >= 1:
                            emit_pv(p_ - 1, first=(p_ == 1))
                    emit_pv(KT // 2 - 1, first=False)
                    v_pv = r.cur("c_pe")
                    r.wait("scalar", "c_pe", v_pv)
                    r.emit_inc("scalar", "c_ac", 1,
                               lambda e, pX=pX: e.activation(out=rowsB[0:1, 0:512], in_=pX[0:1, 512:1024],
                                                             func=AF.Ln))
                    v_rc = r.emit_inc("scalar", "c_ac", 1,
                                      lambda e: e.activation(out=rowsB[0:1, 0:512], in_=rowsB[0:1, 0:512],
                                                             func=AF.Exp, scale=-1.0))
                    r.wait("tensor", "c_ac", v_rc)
                    v_bc = r.emit_inc("tensor", "c_pe", 1,
                                      lambda e, pX=pX: e.matmul(out=pX[:, 512:1024], lhsT=ones_b[:, :],
                                                                rhs=rowsB[0:1, 0:512], start=True, stop=True))
                    r.wait("vector", "c_pe", v_bc)
                    r.emit_inc("vector", "c_dv", 1,
                               lambda e, pX=pX: e.tensor_copy(out=tmpf[:, 0:512], in_=pX[:, 0:512]))
                    otm_vals[it] = r.emit_inc("vector", "c_dv", 1,
                                              lambda e, h=h, qb=qb, pX=pX:
                                              e.tensor_mul(lnT_a[:, h * TB + qb * 512: h * TB + (qb + 1) * 512],
                                                           tmpf[:, 0:512], pX[:, 512:1024]))
                    it += 1
        ot_dv = otm_vals[it - 1]

        wo_pe = {}
        for chw in range(4):
            buf = chw % 2
            if chw <= 1:
                r.wait("gpsimd", "c_dv", ot_dv)
            else:
                r.wait("gpsimd", "c_pe", wo_pe[chw - 2])
            v_wo = r.emit_inc("gpsimd", "s_aw", 16,
                              lambda e, chw=chw, buf=buf:
                              e.dma_start(out=wob[:, buf * 4 * C:(buf + 1) * 4 * C]
                                          .rearrange("p (b c) -> p b c", b=4),
                                          in_=wo[chw * 512:(chw + 1) * 512, :]
                                          .rearrange("(b p) c -> p b c", p=128)))
            r.wait("tensor", "s_aw", v_wo)
            r.wait("tensor", "c_dv", ot_dv)
            if chw == 0:
                r.wait("tensor", "s_sp", XRL_V)
                r.wait("vector", "s_sp", XRL_V)
            yev = {}
            for ct in range(CT):
                pyw = p2 if ct % 2 == 0 else p3
                if ct >= 2:
                    r.wait("tensor", "c_ev", yev[ct - 2])
                for b4 in range(4):
                    hd = chw * 4 + b4
                    for th in range(2):
                        r.emit_inc("tensor", "c_pe", 1,
                                   lambda e, b4=b4, hd=hd, th=th, ct=ct, pyw=pyw, buf=buf:
                                   e.matmul(out=pyw[:, th * 512:(th + 1) * 512],
                                            lhsT=wob[:, buf * 4 * C + b4 * C + ct * 128:
                                                     buf * 4 * C + b4 * C + (ct + 1) * 128],
                                            rhs=lnT_a[:, hd * TB + th * 512: hd * TB + (th + 1) * 512],
                                            start=(b4 == 0), stop=(b4 == 3)))
                v_y = r.cur("c_pe")
                r.wait("vector", "c_pe", v_y)
                for th in range(2):
                    r.emit_inc("vector", "c_dv", 1,
                               lambda e, ct=ct, th=th, pyw=pyw:
                               e.tensor_add(xT_a[:, ct * TB + th * 512: ct * TB + (th + 1) * 512],
                                            xT_a[:, ct * TB + th * 512: ct * TB + (th + 1) * 512],
                                            pyw[:, th * 512:(th + 1) * 512]))
                yev[ct] = r.emit_inc("vector", "c_ev", 1, lambda e: e.engine_nop())
            wo_pe[chw] = r.cur("c_pe")

        ln3 = emit_ln()
        ffn3 = emit_ffn_pass(ln3)
        r.wait("sync", "c_ev", ffn3)
        for ct in range(CT):
            r.emit_inc("sync", "s_out", 16,
                       lambda e, ct=ct: e.dma_start(out=yT[ct * 128:(ct + 1) * 128, :],
                                                    in_=xT_a[:, ct * TB:(ct + 1) * TB]))
        r.wait("sync", "s_out")

        with nc.Block() as block:
            @block.sync
            def _(e):
                for f in r.ops["sync"]:
                    f(e)

            @block.gpsimd
            def _(e):
                for f in r.ops["gpsimd"]:
                    f(e)

            @block.vector
            def _(e):
                for f in r.ops["vector"]:
                    f(e)

            @block.scalar
            def _(e):
                for f in r.ops["scalar"]:
                    f(e)

            @block.tensor
            def _(e):
                for f in r.ops["tensor"]:
                    f(e)

    return nc


_NC_CACHE = {}


def kernel(L0_attn_x_q, L0_attn_Wq, L0_attn_Wk, L0_attn_Wv, L0_attn_Wo,
           L0_ffn_w1, L0_ffn_w2, L0_ffn_w3):
    x = np.ascontiguousarray(np.asarray(L0_attn_x_q, dtype=np.float32)[0])
    wq_ = np.ascontiguousarray(np.asarray(L0_attn_Wq, dtype=np.float32).reshape(C, 2048))
    wk_ = np.ascontiguousarray(np.asarray(L0_attn_Wk, dtype=np.float32).reshape(C, 512))
    wv_ = np.ascontiguousarray(np.asarray(L0_attn_Wv, dtype=np.float32).reshape(C, 512))
    wo_ = np.ascontiguousarray(np.asarray(L0_attn_Wo, dtype=np.float32))
    w1T = np.ascontiguousarray(np.asarray(L0_ffn_w1, dtype=np.float32).T)
    w2T = np.ascontiguousarray(np.asarray(L0_ffn_w2, dtype=np.float32).T)
    w3T = np.ascontiguousarray(np.asarray(L0_ffn_w3, dtype=np.float32).T)

    xTb = np.ascontiguousarray(x.reshape(8, TB, C).transpose(0, 2, 1))

    nonce = int(np.random.default_rng().integers(1 << 20, (1 << 31) - 1))
    in_maps = []
    for core in range(8):
        par = core % 2
        blocks = [core] + [b for b in range(par, 8, 2) if b != core]
        m = np.zeros((1, 16), np.int32)
        m[0, 0] = par
        m[0, 1] = nonce
        in_maps.append({
            "meta": m,
            "xT": np.ascontiguousarray(xTb[blocks]),
            "w1th": np.ascontiguousarray(w1T[par * 1024:(par + 1) * 1024, :]),
            "w2th": np.ascontiguousarray(w2T[par * 1024:(par + 1) * 1024, :]),
            "w3th": np.ascontiguousarray(w3T[par * 2816:(par + 1) * 2816, :]),
            "wq": wq_, "wk": wk_, "wv": wv_, "wo": wo_,
        })
    if "nc" not in _NC_CACHE:
        _NC_CACHE["nc"] = build_program()
    res = run_bass_kernel_spmd(_NC_CACHE["nc"], in_maps, core_ids=list(range(8)))
    out = np.empty((SEQ, C), np.float32)
    for core in range(8):
        out[core * TB:(core + 1) * TB, :] = res.results[core]["yT"].T
    return out[None, :, :]


# version 15 (final)
# speedup vs baseline: 1.3521x; 1.0057x over previous
import numpy as np
from contextlib import ExitStack

import concourse.bass as bass
import concourse.mybir as mybir
from concourse.bass_utils import run_bass_kernel_spmd

F32 = mybir.dt.float32
BF16 = mybir.dt.bfloat16
I32 = mybir.dt.int32
AF = mybir.ActivationFunctionType
ALU = mybir.AluOpType

C = 2048
CT = 16
TB = 1024
HF = 5632
NJ = 44
CHUNK = 11
NCHUNK = 4
D = 128
NQH = 16
NKV = 4
SEQ = 8192
KT = 64
NBLK = 4
NW_12 = HF * C
NW_3 = C * HF
EPS_BIT = 1e-8
EPS_LN = 1e-5
EPS_QK = 1e-6
INV_SQRT_D = float(1.0 / np.sqrt(128.0))
SHIFT_C = float(np.sqrt(128.0) + 1e-3)
ETS = 6

ENGINES = ("sync", "gpsimd", "vector", "scalar", "tensor")


class Rec:
    def __init__(self):
        self.ops = {e: [] for e in ENGINES}
        self.cnt = {}
        self.sems = None

    def inc(self, sem, amt):
        self.cnt[sem] = self.cnt.get(sem, 0) + amt
        return self.cnt[sem]

    def cur(self, sem):
        return self.cnt.get(sem, 0)

    def on(self, eng, fn):
        self.ops[eng].append(fn)

    def wait(self, eng, sem, val=None):
        v = self.cur(sem) if val is None else val
        if v > 0:
            self.on(eng, lambda e, s=sem, vv=v: e.wait_ge(self.sems[s], vv))

    def emit_inc(self, eng, sem, amt, builder):
        v = self.inc(sem, amt)
        self.on(eng, lambda e, b=builder, s=sem, a=amt: b(e).then_inc(self.sems[s], a))
        return v


def build_program():
    nc = bass.Bass("TRN2", target_bir_lowering=False, debug=False)
    r = Rec()

    meta = nc.dram_tensor("meta", [1, 16], I32, kind="ExternalInput").ap()
    xT = nc.dram_tensor("xT", [NBLK, C, TB], F32, kind="ExternalInput").ap()
    w1th = nc.dram_tensor("w1th", [1024, HF], F32, kind="ExternalInput").ap()
    w2th = nc.dram_tensor("w2th", [1024, HF], F32, kind="ExternalInput").ap()
    w3th = nc.dram_tensor("w3th", [2816, C], F32, kind="ExternalInput").ap()
    wq = nc.dram_tensor("wq", [C, 2048], F32, kind="ExternalInput").ap()
    wk = nc.dram_tensor("wk", [C, 512], F32, kind="ExternalInput").ap()
    wv = nc.dram_tensor("wv", [C, 512], F32, kind="ExternalInput").ap()
    wo = nc.dram_tensor("wo", [2048, C], F32, kind="ExternalInput").ap()
    yT = nc.dram_tensor("yT", [C, TB], F32, kind="ExternalOutput").ap()

    sh_w1T = nc.dram_tensor("sh_w1T", [C, HF], BF16, addr_space="Shared").ap()
    sh_w2T = nc.dram_tensor("sh_w2T", [C, HF], BF16, addr_space="Shared").ap()
    sh_w3T = nc.dram_tensor("sh_w3T", [HF, C], BF16, addr_space="Shared").ap()
    sh_kT = nc.dram_tensor("sh_kT", [NKV * 128, SEQ], BF16, addr_space="Shared").ap()
    sh_v = nc.dram_tensor("sh_v", [SEQ, NKV * 128], BF16, addr_space="Shared").ap()
    sh_gm = nc.dram_tensor("sh_gm", [2, 16], F32, addr_space="Shared").ap()
    sh_fl = nc.dram_tensor("sh_fl", [2, 16], I32, addr_space="Shared").ap()
    xsp = nc.dram_tensor("xsp", [C, TB], F32).ap()
    qsp = nc.dram_tensor("qsp", [NQH * 128, TB], BF16).ap()

    es = ExitStack()

    def sb(name, shape, dt):
        return es.enter_context(nc.sbuf_tensor(name, shape, dt))

    with es:
        mt = sb("mt", [1, 16], I32)
        ident = sb("ident", [128, 128], BF16)
        ones_l = sb("ones_l", [128, 1], F32)
        ones_lb = sb("ones_lb", [128, 1], BF16)
        ones_b = sb("ones_b", [1, 128], F32)
        b_eln = sb("b_eln", [1, 1], F32)
        b_shift = sb("b_shift", [128, 1], F32)
        b_e8 = sb("b_e8", [1, 1], F32)
        b_eqk = sb("b_eqk", [1, 1], F32)

        xT_a = sb("xT_a", [128, CT * TB], F32)
        lnT_a = sb("lnT_a", [128, CT * TB], BF16)
        arena_t = sb("arena", [128, 33792], BF16)
        arena = arena_t[:, :]
        w12b = sb("w12b", [128, 4 * C], BF16)
        etb = sb("etb", [128, ETS * 512], BF16)
        sgb = sb("sgb", [128, TB], BF16)
        sqb = sb("sqb", [128, TB], F32)
        tmpf = sb("tmpf", [128, TB], F32)
        rowsA = sb("rowsA", [1, TB], F32)
        rowsB = sb("rowsB", [1, TB], F32)
        g_acc = sb("g_acc", [128, 80], F32)
        g3 = sb("g3", [128, 4], F32)
        gpart = sb("gpart", [1, 4], F32)
        gm2 = sb("gm2", [1, 8], F32)
        gmrow = sb("gmrow", [1, 4], F32)
        rinv = sb("rinv", [1, 4], F32)
        rcol = sb("rcol", [128, 4], F32)
        o_sb = sb("o_sb", [128, 128], BF16)
        rc1 = sb("rc1", [128, 1], F32)
        flg = sb("flg", [1, 32], I32)

        w3ch = arena[:, 0:CHUNK * C]
        hact = arena[:, CHUNK * C: CHUNK * C + CHUNK * TB]
        qT_a = arena[:, 0:NQH * TB]
        kTg = arena[:, 16384:16384 + SEQ]
        vg = arena[:, 24576:24576 + KT * 128]
        wob = arena[:, 0:2 * 4 * C]
        wkvb = arena[:, 16384:16384 + CT * 512]
        wqb = arena[:, 24576:24576 + 2 * C]
        vstage = arena[:, 28672:28672 + 8 * 512]
        arena_f32 = arena_t[:, :].bitcast(F32)
        kv_sq = [arena_f32[:, 0:1024], arena_f32[:, 1024:2048]]
        kv_ev = [arena_f32[:, 2048:3072], arena_f32[:, 3072:4096]]
        q_sq = [arena_f32[:, 8192:9216], arena_f32[:, 9216:10240]]
        q_ev = [arena_f32[:, 10240:11264], arena_f32[:, 11264:12288]]
        kout2 = vstage[:, 0:1024]
        qin0 = arena_f32[:, 0:2816]
        qin1 = arena_f32[:, 2816:5632]
        qpos = arena[:, 11264:11264 + 2816]
        qneg = arena[:, 14080:14080 + 2816]
        qout0 = arena[:, 16896:16896 + 2816]
        qout1 = arena[:, 19712:19712 + 2816]
        gdump = arena[:, 16896:16896 + 2816]

        def psum(name):
            return es.enter_context(nc.psum_tensor(name, [128, 1024], F32))
        p0 = psum("p0")
        p1 = psum("p1")
        p2 = psum("p2")
        p3 = psum("p3")

        sems = {}
        for s in ["s_meta", "s_x", "s_g", "s_g2", "s_q", "s_scr", "s_lw", "s_kv", "s_sp",
                  "s_out", "s_aw", "s_pl",
                  "c_po", "c_pe", "c_dv", "c_ac", "c_ev", "c_rdy"]:
            sems[s] = es.enter_context(nc.semaphore(s))
        r.sems = sems
        regs = {}

        r.emit_inc("sync", "s_meta", 16, lambda e: e.dma_start(out=mt[:], in_=meta))

        def init_sync(e):
            regs["s_par"] = e.alloc_register("s_par")
            e.wait_ge(sems["s_meta"], 16)
            e.reg_mov(regs["s_par"], e.value_load(mt[0:1, 0:1]))
        r.on("sync", init_sync)

        def init_gp(e):
            for nm in ("g_par", "g_nonce", "g_cnt", "g_w16", "g_t0", "g_go"):
                regs[nm] = e.alloc_register(nm)
            e.wait_ge(sems["s_meta"], 16)
            e.reg_mov(regs["g_par"], e.value_load(mt[0:1, 0:1]))
            e.reg_mov(regs["g_nonce"], e.value_load(mt[0:1, 1:2]))
            e.reg_mov(regs["g_cnt"], 0)
        r.on("gpsimd", init_gp)

        def setup_pool(e):
            e.memset(ident[:], 0.0).then_inc(sems["c_po"], 1)
        r.on("gpsimd", setup_pool)
        ID0 = r.inc("c_po", 1)

        def setup_pool2(e):
            e.wait_ge(sems["c_po"], ID0)
            e.affine_select(out=ident[:], in_=ident[:], compare_op=ALU.not_equal,
                            fill=1.0, base=0, pattern=[[-1, 128]], channel_multiplier=1)
            e.memset(ones_l[:], 1.0)
            e.memset(ones_lb[:], 1.0)
            e.memset(ones_b[:], 1.0)
            e.memset(b_eln[:], EPS_LN)
            e.memset(b_shift[:], -SHIFT_C)
            e.memset(b_e8[:], EPS_BIT)
            e.memset(b_eqk[:], EPS_QK)
            e.memset(g_acc[:], 0.0)
        r.on("gpsimd", setup_pool2)
        SETUP_PO = r.emit_inc("gpsimd", "c_po", 1, lambda e: e.engine_nop())

        def par_dma(out_even, out_odd, in_ap, sem):
            v = r.inc(sem, 16)
            def f(e, oe=out_even, oo=out_odd, ia=in_ap, s=sem):
                with e.If_eq(regs["s_par"], 0):
                    e.dma_start(out=oe, in_=ia).then_inc(r.sems[s], 16)
                with e.Else():
                    e.dma_start(out=oo, in_=ia).then_inc(r.sems[s], 16)
                e.end_ifs()
            r.on("sync", f)
            return v

        def flag_write(col):
            return par_dma(sh_fl[0:1, col:col + 1], sh_fl[1:2, col:col + 1],
                           mt[0:1, 1:2], "s_kv")

        def poll_both(col):
            v = r.inc("c_rdy", 1)
            def f(e, col=col):
                for row in (0, 1):
                    e.reg_mov(regs["g_go"], 1)
                    with e.While(regs["g_go"]):
                        e.reg_add(regs["g_cnt"], regs["g_cnt"], 1)
                        e.dma_start(out=flg.ap().rearrange("o (r c) -> o r c", r=2),
                                    in_=sh_fl.rearrange("(o r) c -> o r c", o=1)).then_inc(r.sems["s_pl"], 16)
                        e.reg_mov(regs["g_w16"], regs["g_cnt"])
                        e.reg_mul(regs["g_w16"], regs["g_w16"], 16)
                        e.wait_ge(r.sems["s_pl"], regs["g_w16"])
                        e.reg_mov(regs["g_t0"], e.value_load(flg[0:1, row * 16 + col: row * 16 + col + 1]))
                        e.reg_alu(regs["g_go"], regs["g_t0"], regs["g_nonce"], ALU.not_equal)
                e.engine_nop().then_inc(r.sems["c_rdy"], 1)
            r.on("gpsimd", f)
            return v

        def phase_barrier(engs=("tensor", "vector", "scalar")):
            for eng in engs:
                for s in ("c_pe", "c_dv", "c_ac", "c_ev", "s_kv", "s_sp", "s_scr"):
                    r.wait(eng, s)

        for ct in range(CT):
            r.emit_inc("sync", "s_x", 16,
                       lambda e, ct=ct: e.dma_start(out=xT_a[:, ct * TB:(ct + 1) * TB],
                                                    in_=xT[0, ct * 128:(ct + 1) * 128, :]))
        X_OWN = r.cur("s_x")

        def gamma_jobs(jobs, bufs, dump, queue_fn, pre_gates=None):
            ac_base = r.cur("c_ac")
            gated = set()
            for i, (dram, r0, c0, width, col) in enumerate(jobs):
                qb = bufs[i % 2]
                eng, sem = queue_fn(i)
                if pre_gates and eng not in gated:
                    for (gs, gv) in pre_gates:
                        r.wait(eng, gs, gv)
                    gated.add(eng)
                if i >= 2:
                    r.wait(eng, "c_ac", ac_base + (i - 2) + 1)
                v = r.emit_inc(eng, sem, 16,
                               lambda e, r0=r0, c0=c0, d=dram, w=width, qb=qb:
                               e.dma_start(out=qb[:, 0:w], in_=d[r0:r0 + 128, c0:c0 + w]))
                r.wait("scalar", sem, v)
                if i == 0:
                    r.wait("scalar", "c_po", SETUP_PO)
                r.emit_inc("scalar", "c_ac", 1,
                           lambda e, w=width, col=col, qb=qb, dump=dump:
                           e.activation(out=dump[:, 0:w], in_=qb[:, 0:w],
                                        func=AF.Abs, accum_out=g_acc[:, col:col + 1]))
            return r.cur("c_ac")

        def gamma_finish(ac_done, gacc_lo, gacc_hi, g3_cols, gm_cols, flag_col, nw_scales):
            r.wait("vector", "c_ac", ac_done)
            for ci, (lo, hi) in enumerate(zip(gacc_lo, gacc_hi)):
                r.emit_inc("vector", "c_dv", 1,
                           lambda e, lo=lo, hi=hi, gc=g3_cols[ci]:
                           e.reduce_sum(g3[:, gc:gc + 1], g_acc[:, lo:hi], axis=mybir.AxisListType.X))
            GS = r.cur("c_dv")
            r.wait("tensor", "c_dv", GS)
            r.wait("tensor", "c_po", SETUP_PO)
            nclz = len(g3_cols)
            gc0 = g3_cols[0]
            v = r.emit_inc("tensor", "c_pe", 1,
                           lambda e, gc0=gc0, n=nclz: e.matmul(out=p0[0:1, gc0:gc0 + n], lhsT=ones_l[:],
                                                               rhs=g3[:, gc0:gc0 + n], start=True, stop=True))
            r.wait("vector", "c_pe", v)
            GP = r.emit_inc("vector", "c_dv", 1,
                            lambda e, gc0=gc0, n=nclz: e.tensor_copy(out=gpart[0:1, 0:n], in_=p0[0:1, gc0:gc0 + n]))
            r.wait("sync", "c_dv", GP)
            par_dma(sh_gm[0:1, gm_cols[0]:gm_cols[1]], sh_gm[1:2, gm_cols[0]:gm_cols[1]],
                    gpart[0:1, 0:nclz], "s_kv")
            r.wait("sync", "s_kv")
            flag_write(flag_col)
            RDY = poll_both(flag_col)
            r.wait("sync", "c_rdy", RDY)
            GM2 = r.emit_inc("sync", "s_q", 16,
                             lambda e, a=gm_cols[0]:
                             e.dma_start(out=gm2.ap().rearrange("o (r c) -> o r c", r=2),
                                         in_=sh_gm[0:2, a:a + 4].rearrange("(o r) c -> o r c", o=1)))
            r.wait("vector", "s_q", GM2)
            GT = r.emit_inc("vector", "c_dv", 1,
                            lambda e, n=nclz: e.tensor_add(gmrow[0:1, 0:n], gm2[0:1, 0:n], gm2[0:1, 4:4 + n]))
            r.wait("scalar", "c_dv", GT)
            for ci, sc in enumerate(nw_scales):
                r.emit_inc("scalar", "c_ac", 1,
                           lambda e, ci=ci, sc=sc: e.activation(out=rinv[0:1, ci:ci + 1], in_=gmrow[0:1, ci:ci + 1],
                                                                func=AF.Identity, scale=sc, bias=b_e8[:, :]))
            r.emit_inc("scalar", "c_ac", 1,
                       lambda e, n=nclz: e.activation(out=rinv[0:1, 0:n], in_=rinv[0:1, 0:n], func=AF.Ln))
            RI = r.emit_inc("scalar", "c_ac", 1,
                            lambda e, n=nclz: e.activation(out=rinv[0:1, 0:n], in_=rinv[0:1, 0:n],
                                                           func=AF.Exp, scale=-1.0))
            r.wait("tensor", "c_ac", RI)
            v = r.emit_inc("tensor", "c_pe", 1,
                           lambda e, n=nclz: e.matmul(out=p0[:, 8:8 + n], lhsT=ones_b[:, :],
                                                      rhs=rinv[0:1, 0:n], start=True, stop=True))
            r.wait("vector", "c_pe", v)
            RC = r.emit_inc("vector", "c_dv", 1,
                            lambda e, gc0=g3_cols[0], n=nclz:
                            e.tensor_copy(out=rcol[:, gc0:gc0 + n], in_=p0[:, 8:8 + n]))
            return RC

        def quant_stage(jobs, rc_ready, flag_map, ibase, queue_fn,
                        qin_gate_ac=None, prev_dv=None, prev_scr=None):
            dv_done = {}
            scr_done = {}
            r.wait("vector", "c_dv", rc_ready)
            if prev_scr is not None:
                r.wait("vector", "s_scr", prev_scr)
            gated = set()
            for i0, (wn, dram, b, hh, width, shdst, half) in enumerate(jobs):
                i = ibase + i0
                qbuf = qin0 if i % 2 == 0 else qin1
                obuf = qout0 if i % 2 == 0 else qout1
                wi = {"w1": 0, "w2": 1, "w3": 2}[wn]
                eng, sem = queue_fn(i0)
                if eng not in gated:
                    if qin_gate_ac is not None:
                        r.wait(eng, "c_ac", qin_gate_ac)
                    if prev_dv is not None:
                        r.wait(eng, "c_dv", prev_dv)
                    gated.add(eng)
                if i0 >= 2:
                    r.wait(eng, "c_dv", dv_done[i0 - 2])
                v = r.emit_inc(eng, sem, 16,
                               lambda e, b=b, hh=hh, d=dram, w=width, qb=qbuf:
                               e.dma_start(out=qb[:, 0:w],
                                           in_=d[b * 128:(b + 1) * 128, hh * 2816: hh * 2816 + w]))
                r.wait("vector", sem, v)
                if i0 >= 2:
                    r.wait("vector", "s_scr", scr_done[i0 - 2])
                r.emit_inc("vector", "c_dv", 1,
                           lambda e, w=width, wi=wi, qb=qbuf:
                           e.tensor_scalar(out=qpos[:, 0:w], in0=qb[:, 0:w],
                                           scalar1=rcol[:, wi:wi + 1], scalar2=0.5,
                                           op0=ALU.mult, op1=ALU.is_ge))
                r.emit_inc("vector", "c_dv", 1,
                           lambda e, w=width, wi=wi, qb=qbuf:
                           e.tensor_scalar(out=qneg[:, 0:w], in0=qb[:, 0:w],
                                           scalar1=rcol[:, wi:wi + 1], scalar2=-0.5,
                                           op0=ALU.mult, op1=ALU.is_le))
                r.emit_inc("vector", "c_dv", 1,
                           lambda e, w=width, ob=obuf:
                           e.tensor_sub(ob[:, 0:w], qpos[:, 0:w], qneg[:, 0:w]))
                if wn == "w3":
                    r.emit_inc("vector", "c_dv", 1,
                               lambda e, w=width, ob=obuf:
                               e.tensor_scalar_mul(ob[:, 0:w], ob[:, 0:w], 0.5))
                dv_done[i0] = r.cur("c_dv")
                r.wait("sync", "c_dv", dv_done[i0])
                if wn == "w3":
                    scr_done[i0] = par_dma(shdst[b * 128:(b + 1) * 128, :],
                                           shdst[half + b * 128: half + (b + 1) * 128, :],
                                           obuf[:, 0:width], "s_scr")
                else:
                    scr_done[i0] = par_dma(
                        shdst[b * 128:(b + 1) * 128, hh * 2816: (hh + 1) * 2816],
                        shdst[half + b * 128: half + (b + 1) * 128, hh * 2816: (hh + 1) * 2816],
                        obuf[:, 0:width], "s_scr")
                if (wn, b, hh) in flag_map:
                    r.wait("sync", "s_scr")
                    flag_write(flag_map[(wn, b, hh)])
            return ibase + len(jobs)

        g12jobs = [(w1th, b * 128, hh * 2816, 2816, b * 2 + hh) for b in range(8) for hh in range(2)] + \
                  [(w2th, b * 128, hh * 2816, 2816, 16 + b * 2 + hh) for b in range(8) for hh in range(2)]
        g3jobs = [(w3th, b * 128, hh * 1024, 1024, 32 + b * 2 + hh)
                  for b in range(22) for hh in range(2)]
        q12jobs = [("w1", w1th, b, hh, 2816, sh_w1T, 1024) for b in range(8) for hh in range(2)] + \
                  [("w2", w2th, b, hh, 2816, sh_w2T, 1024) for b in range(8) for hh in range(2)]
        q3jobs = [("w3", w3th, b, 0, C, sh_w3T, 2816) for b in range(22)]

        def emit_ln():
            phase_barrier()
            for eng in ("tensor", "scalar", "vector"):
                r.wait(eng, "c_po", SETUP_PO)
            pe_hist = {}
            for ct in range(CT):
                sq_t = sqb if ct % 2 == 0 else tmpf
                if ct >= 2:
                    r.wait("scalar", "c_pe", pe_hist[ct - 2])
                va = r.emit_inc("scalar", "c_ac", 1,
                                lambda e, ct=ct, sq_t=sq_t: e.activation(out=sq_t[:, :],
                                                              in_=xT_a[:, ct * TB:(ct + 1) * TB],
                                                              func=AF.Square))
                r.wait("tensor", "c_ac", va)
                for th in range(2):
                    r.emit_inc("tensor", "c_pe", 1,
                               lambda e, ct=ct, th=th:
                               e.matmul(out=p2[0:1, th * 512:(th + 1) * 512], lhsT=ones_l[:],
                                        rhs=xT_a[:, ct * TB + th * 512: ct * TB + (th + 1) * 512],
                                        start=(ct == 0), stop=(ct == CT - 1)))
                    r.emit_inc("tensor", "c_pe", 1,
                               lambda e, th=th, ct=ct, sq_t=sq_t:
                               e.matmul(out=p3[0:1, th * 512:(th + 1) * 512], lhsT=ones_l[:],
                                        rhs=sq_t[:, th * 512:(th + 1) * 512],
                                        start=(ct == 0), stop=(ct == CT - 1)))
                pe_hist[ct] = r.cur("c_pe")
            r.wait("vector", "c_pe")
            r.emit_inc("vector", "c_dv", 1, lambda e: e.tensor_scalar_mul(rowsA[:, :], p2[0:1, :], 1.0 / C))
            r.emit_inc("vector", "c_dv", 1, lambda e: e.tensor_scalar_mul(rowsB[:, :], p3[0:1, :], 1.0 / C))
            r.emit_inc("vector", "c_dv", 1, lambda e: e.tensor_mul(sqb[0:1, 0:TB], rowsA[:, :], rowsA[:, :]))
            v_dv = r.emit_inc("vector", "c_dv", 1,
                              lambda e: e.tensor_sub(rowsB[:, :], rowsB[:, :], sqb[0:1, 0:TB]))
            r.wait("scalar", "c_dv", v_dv)
            r.emit_inc("scalar", "c_ac", 1,
                       lambda e: e.activation(out=rowsB[:, :], in_=rowsB[:, :], func=AF.Ln, bias=b_eln[:, :]))
            v_ac = r.emit_inc("scalar", "c_ac", 1,
                              lambda e: e.activation(out=rowsB[:, :], in_=rowsB[:, :], func=AF.Exp, scale=-0.5))
            r.wait("tensor", "c_ac", v_ac)
            for th in range(2):
                r.emit_inc("tensor", "c_pe", 1,
                           lambda e, th=th: e.matmul(out=p2[:, th * 512:(th + 1) * 512], lhsT=ones_b[:, :],
                                                     rhs=rowsA[0:1, th * 512:(th + 1) * 512], start=True, stop=True))
                r.emit_inc("tensor", "c_pe", 1,
                           lambda e, th=th: e.matmul(out=p3[:, th * 512:(th + 1) * 512], lhsT=ones_b[:, :],
                                                     rhs=rowsB[0:1, th * 512:(th + 1) * 512], start=True, stop=True))
            r.wait("vector", "c_pe")
            for ct in range(CT):
                for th in range(2):
                    r.emit_inc("vector", "c_dv", 1,
                               lambda e, ct=ct, th=th:
                               e.tensor_sub(tmpf[:, th * 512:(th + 1) * 512],
                                            xT_a[:, ct * TB + th * 512: ct * TB + (th + 1) * 512],
                                            p2[:, th * 512:(th + 1) * 512]))
                    r.emit_inc("vector", "c_dv", 1,
                               lambda e, ct=ct, th=th:
                               e.tensor_mul(lnT_a[:, ct * TB + th * 512: ct * TB + (th + 1) * 512],
                                            tmpf[:, th * 512:(th + 1) * 512],
                                            p3[:, th * 512:(th + 1) * 512]))
            return r.cur("c_dv")

        def emit_ffn_pass(ln_dv, pending=None, out_cb=None):
            pending = list(pending or [])
            defer_w3 = bool(pending)
            phase_barrier(("tensor",))
            r.wait("sync", "c_rdy")
            r.wait("sync", "c_pe")
            r.wait("sync", "c_dv")
            hj_pe = {}
            sig_ac = {}
            hct_dv = {}
            y_done_prev = 0
            for chn in range(NCHUNK):
                for j in range(CHUNK):
                    jj = chn * CHUNK + j
                    buf = jj % 2
                    if jj >= 2:
                        r.wait("sync", "c_pe", hj_pe[jj - 2])
                    r.emit_inc("sync", "s_lw", 16,
                               lambda e, jj=jj, buf=buf:
                               e.dma_start(out=w12b[:, buf * C:(buf + 1) * C]
                                           .rearrange("p (ct h) -> p ct h", ct=CT),
                                           in_=sh_w1T[:, jj * 128:(jj + 1) * 128]
                                           .rearrange("(ct p) h -> p ct h", p=128)))
                    v12 = r.emit_inc("sync", "s_lw", 16,
                                     lambda e, jj=jj, buf=buf:
                                     e.dma_start(out=w12b[:, (2 + buf) * C:(3 + buf) * C]
                                                 .rearrange("p (ct h) -> p ct h", ct=CT),
                                                 in_=sh_w2T[:, jj * 128:(jj + 1) * 128]
                                                 .rearrange("(ct p) h -> p ct h", p=128)))
                    if chn == 0 and defer_w3:
                        pass
                    else:
                        if chn >= 1:
                            r.wait("sync", "c_pe", y_done_prev)
                        r.emit_inc("sync", "s_lw", 16,
                                   lambda e, jj=jj, j=j:
                                   e.dma_start(out=w3ch[:, j * C:(j + 1) * C],
                                               in_=sh_w3T[jj * 128:(jj + 1) * 128, :]))
                    r.wait("tensor", "s_lw", v12)
                    if jj == 0:
                        r.wait("tensor", "c_dv", ln_dv)
                    if jj >= 1:
                        r.wait("tensor", "c_ac", sig_ac[jj - 1])
                    for th in range(2):
                        for ct in range(CT):
                            r.emit_inc("tensor", "c_pe", 1,
                                       lambda e, th=th, ct=ct, buf=buf:
                                       e.matmul(out=p0[:, th * 512:(th + 1) * 512],
                                                lhsT=w12b[:, buf * C + ct * 128: buf * C + (ct + 1) * 128],
                                                rhs=lnT_a[:, ct * TB + th * 512: ct * TB + (th + 1) * 512],
                                                start=(ct == 0), stop=(ct == CT - 1)))
                    v_x1 = r.cur("c_pe")
                    if jj >= 1:
                        r.wait("tensor", "c_dv", hct_dv[jj - 1])
                    for th in range(2):
                        for ct in range(CT):
                            r.emit_inc("tensor", "c_pe", 1,
                                       lambda e, th=th, ct=ct, buf=buf:
                                       e.matmul(out=p1[:, th * 512:(th + 1) * 512],
                                                lhsT=w12b[:, (2 + buf) * C + ct * 128: (2 + buf) * C + (ct + 1) * 128],
                                                rhs=lnT_a[:, ct * TB + th * 512: ct * TB + (th + 1) * 512],
                                                start=(ct == 0), stop=(ct == CT - 1)))
                    hj_pe[jj] = r.cur("c_pe")
                    r.wait("scalar", "c_pe", v_x1)
                    if jj >= 1:
                        r.wait("scalar", "c_dv", hct_dv[jj - 1])
                    sig_ac[jj] = r.emit_inc("scalar", "c_ac", 1,
                                            lambda e: e.activation(out=sgb[:, :], in_=p0[:, :], func=AF.Sigmoid))
                    r.wait("vector", "c_ac", sig_ac[jj])
                    r.wait("vector", "c_pe", hj_pe[jj])
                    hct_dv[jj] = r.emit_inc("vector", "c_dv", 1,
                                            lambda e, j=j: e.tensor_mul(hact[:, j * TB:(j + 1) * TB],
                                                                        sgb[:, :], p1[:, :]))
                    if pending and chn == 0:
                        if j == 6:
                            r.wait("vector", "s_kv")
                        for _ in range(3):
                            if pending:
                                pending.pop(0)()
                while pending:
                    pending.pop(0)()
                if chn == 0 and defer_w3:
                    r.wait("sync", "c_dv")
                    r.wait("sync", "c_ac")
                    for j2 in range(CHUNK):
                        jj2 = chn * CHUNK + j2
                        r.emit_inc("sync", "s_lw", 16,
                                   lambda e, jj2=jj2, j2=j2:
                                   e.dma_start(out=w3ch[:, j2 * C:(j2 + 1) * C],
                                               in_=sh_w3T[jj2 * 128:(jj2 + 1) * 128, :]))
                if chn == 0:
                    r.wait("tensor", "c_dv")
                    r.wait("tensor", "c_ac")
                r.wait("tensor", "c_dv", hct_dv[chn * CHUNK + CHUNK - 1])
                r.wait("tensor", "s_lw")
                yev = {}
                for ct in range(CT):
                    py = p2 if ct % 2 == 0 else p3
                    if ct >= 2:
                        r.wait("tensor", "c_ev", yev[ct - 2])
                    for j in range(CHUNK):
                        for th in range(2):
                            r.emit_inc("tensor", "c_pe", 1,
                                       lambda e, j=j, th=th, ct=ct, py=py:
                                       e.matmul(out=py[:, th * 512:(th + 1) * 512],
                                                lhsT=w3ch[:, j * C + ct * 128: j * C + (ct + 1) * 128],
                                                rhs=hact[:, j * TB + th * 512: j * TB + (th + 1) * 512],
                                                start=(j == 0), stop=(j == CHUNK - 1)))
                    v_y = r.cur("c_pe")
                    r.wait("vector", "c_pe", v_y)
                    for th in range(2):
                        r.emit_inc("vector", "c_dv", 1,
                                   lambda e, ct=ct, th=th, py=py:
                                   e.tensor_add(xT_a[:, ct * TB + th * 512: ct * TB + (th + 1) * 512],
                                                xT_a[:, ct * TB + th * 512: ct * TB + (th + 1) * 512],
                                                py[:, th * 512:(th + 1) * 512]))
                    yev[ct] = r.emit_inc("vector", "c_ev", 1, lambda e: e.engine_nop())
                    if out_cb is not None and chn == NCHUNK - 1:
                        out_cb(ct, yev[ct])
                y_done_prev = r.cur("c_pe")
            return r.cur("c_ev")

        kraw = [arena[:, 16384 + i * TB: 16384 + (i + 1) * TB] for i in range(NKV)]
        wkv2 = arena[:, 24576:24576 + CT * 512]

        def emit_qk(slot, ln_dv, do_q):
            phase_barrier()
            r.wait("gpsimd", "c_pe")
            r.wait("gpsimd", "c_dv")
            tiles = []
            if do_q:
                tiles += [("q", h) for h in range(NQH)]
            tiles += [("k", g) for g in range(NKV)]
            loads = {}
            evac_v = {}
            kload_v = None
            for i, (kind, idx) in enumerate(tiles):
                pp = (p0, p1)[i % 2]
                if kind == "q":
                    buf = i % 2
                    if i >= 2:
                        r.wait("gpsimd", "c_pe", evac_v.get(i - 2, (0, 0))[1])
                    loads[i] = r.emit_inc("gpsimd", "s_aw", 16,
                                          lambda e, idx=idx, buf=buf:
                                          e.dma_start(out=wqb[:, buf * C:(buf + 1) * C]
                                                      .rearrange("p (ct d) -> p ct d", ct=CT),
                                                      in_=wq[:, idx * 128:(idx + 1) * 128]
                                                      .rearrange("(ct p) d -> p ct d", p=128)))
                elif kload_v is None:
                    if do_q:
                        r.wait("gpsimd", "c_pe", evac_v[i - 1][1])
                    kload_v = r.emit_inc("gpsimd", "s_aw", 16,
                                         lambda e: e.dma_start(
                                             out=wkv2.rearrange("p (ct d) -> p ct d", ct=CT),
                                             in_=wk.rearrange("(ct p) d -> p ct d", p=128)))
                    loads[i] = kload_v
                else:
                    loads[i] = kload_v
                r.wait("tensor", "s_aw", loads[i])
                if i == 0:
                    r.wait("tensor", "c_dv", ln_dv)
                if i >= 2:
                    r.wait("tensor", "c_ac", evac_v[i - 2][0])
                for th in range(2):
                    for ct in range(CT):
                        def mk(e, kind=kind, idx=idx, th=th, ct=ct, pp=pp, buf=(i % 2)):
                            if kind == "q":
                                lh = wqb[:, buf * C + ct * 128: buf * C + (ct + 1) * 128]
                            else:
                                lh = wkv2[:, ct * 512 + idx * 128: ct * 512 + idx * 128 + 128]
                            return e.matmul(out=pp[:, th * 512:(th + 1) * 512], lhsT=lh,
                                            rhs=lnT_a[:, ct * TB + th * 512: ct * TB + (th + 1) * 512],
                                            start=(ct == 0), stop=(ct == CT - 1))
                        r.emit_inc("tensor", "c_pe", 1, mk)
                proj_v = r.cur("c_pe")
                r.wait("scalar", "c_pe", proj_v)
                dst = qT_a[:, idx * TB:(idx + 1) * TB] if kind == "q" else kraw[idx]
                va = r.emit_inc("scalar", "c_ac", 1,
                                lambda e, pp=pp, dst=dst: e.activation(out=dst, in_=pp[:, :], func=AF.Identity))
                evac_v[i] = (va, proj_v)
            EV_ALL = r.cur("c_ac")
            r.wait("gpsimd", "c_pe")
            v_wv = r.emit_inc("gpsimd", "s_aw", 16,
                              lambda e: e.dma_start(
                                  out=wkvb.rearrange("p (ct d) -> p ct d", ct=CT),
                                  in_=wv.rearrange("(ct p) d -> p ct d", p=128)))
            vdv = {}
            for tt in range(8):
                pv = p2 if tt % 2 == 0 else p3
                r.wait("tensor", "s_aw", v_wv)
                r.wait("tensor", "c_dv", ln_dv)
                if tt >= 2:
                    r.wait("tensor", "c_dv", vdv[tt - 2])
                for ct in range(CT):
                    r.emit_inc("tensor", "c_pe", 1,
                               lambda e, tt=tt, ct=ct, pv=pv:
                               e.matmul(out=pv[:, 0:512],
                                        lhsT=lnT_a[:, ct * TB + tt * 128: ct * TB + tt * 128 + 128],
                                        rhs=wkvb[:, ct * 512:(ct + 1) * 512],
                                        start=(ct == 0), stop=(ct == CT - 1)))
                v_v = r.cur("c_pe")
                r.wait("vector", "c_pe", v_v)
                if tt == 0:
                    r.wait("vector", "s_kv")
                vdv[tt] = r.emit_inc("vector", "c_dv", 1,
                                     lambda e, tt=tt, pv=pv:
                                     e.tensor_copy(out=vstage[:, tt * 512:(tt + 1) * 512], in_=pv[:, 0:512]))
            def v_write_cl():
                r.wait("sync", "c_dv", vdv[7])
                par_dma(sh_v[slot * TB:(slot + 1) * TB, :].rearrange("(a p) d -> p a d", p=128),
                        sh_v[4096 + slot * TB: 4096 + (slot + 1) * TB, :].rearrange("(a p) d -> p a d", p=128),
                        vstage.rearrange("p (a d) -> p a d", a=8), "s_kv")
            pending = [v_write_cl]
            sq_v = {}
            ss_v = {}
            le_v = {}
            bc_v = {}
            mu_v = {}
            wr_v = {}

            def tile_chain(i, kind, idx):
                pn = (p2, p3)[i % 2]
                sq_t = (sqb, tmpf)[i % 2]
                row_t = (rowsA, rowsB)[i % 2]
                src_t = qT_a[:, idx * TB:(idx + 1) * TB] if kind == "q" else kraw[idx]
                r.wait("scalar", "c_ac", evac_v[i][0])
                if i >= 2:
                    r.wait("scalar", "c_pe", ss_v[i - 2])
                sq_v[i] = r.emit_inc("scalar", "c_ac", 1,
                                     lambda e, src_t=src_t, sq_t=sq_t:
                                     e.activation(out=sq_t[:, :], in_=src_t, func=AF.Square))
                r.wait("tensor", "c_ac", sq_v[i])
                if i >= 2:
                    r.wait("tensor", "c_dv", mu_v[i - 2])
                elif i < 2:
                    r.wait("tensor", "c_dv", vdv[7] if (6 + i) not in vdv else vdv[6 + i])
                for th in range(2):
                    r.emit_inc("tensor", "c_pe", 1,
                               lambda e, th=th, sq_t=sq_t, pn=pn:
                               e.matmul(out=pn[0:1, th * 512:(th + 1) * 512], lhsT=ones_l[:],
                                        rhs=sq_t[:, th * 512:(th + 1) * 512], start=True, stop=True))
                ss_v[i] = r.cur("c_pe")
                r.wait("scalar", "c_pe", ss_v[i])
                if i >= 2:
                    r.wait("scalar", "c_pe", bc_v[i - 2])
                r.emit_inc("scalar", "c_ac", 1,
                           lambda e, pn=pn, row_t=row_t: e.activation(out=row_t[:, :], in_=pn[0:1, :], func=AF.Ln,
                                                                      scale=1.0 / D, bias=b_eqk[:, :]))
                le_v[i] = r.emit_inc("scalar", "c_ac", 1,
                                     lambda e, row_t=row_t: e.activation(out=row_t[:, :], in_=row_t[:, :],
                                                                         func=AF.Exp, scale=-0.5))
                r.wait("tensor", "c_ac", le_v[i])
                for th in range(2):
                    r.emit_inc("tensor", "c_pe", 1,
                               lambda e, th=th, pn=pn, row_t=row_t:
                               e.matmul(out=pn[:, th * 512:(th + 1) * 512], lhsT=ones_b[:, :],
                                        rhs=row_t[0:1, th * 512:(th + 1) * 512], start=True, stop=True))
                bc_v[i] = r.cur("c_pe")
                r.wait("vector", "c_pe", bc_v[i])
                if kind == "k" and i >= 2 and (i - 2) in wr_v:
                    r.wait("vector", "s_kv", wr_v[i - 2])
                mu_v[i] = r.emit_inc("vector", "c_dv", 1,
                                     lambda e, src_t=src_t, pn=pn:
                                     e.tensor_mul(src_t, src_t, pn[:, :]))
                if kind == "k":
                    r.wait("sync", "c_dv", mu_v[i])
                    wr_v[i] = par_dma(
                        sh_kT[idx * 128:(idx + 1) * 128, slot * TB:(slot + 1) * TB],
                        sh_kT[idx * 128:(idx + 1) * 128, 4096 + slot * TB: 4096 + (slot + 1) * TB],
                        src_t, "s_kv")

            for i, (kind, idx) in enumerate(tiles):
                pending.append(lambda i=i, kind=kind, idx=idx: tile_chain(i, kind, idx))

            def flag_cl():
                r.wait("sync", "s_kv")
                flag_write(4 + slot)
            pending.append(flag_cl)
            if do_q:
                def spill_q_cl():
                    r.wait("sync", "c_dv", mu_v[NQH - 1] if (NQH - 1) in mu_v else r.cur("c_dv"))
                    for h in range(NQH):
                        r.wait("sync", "c_dv", mu_v.get(h, 0))
                        r.emit_inc("sync", "s_sp", 16,
                                   lambda e, h=h: e.dma_start(out=qsp[h * 128:(h + 1) * 128, :],
                                                              in_=qT_a[:, h * TB:(h + 1) * TB]))
                pending.append(spill_q_cl)
            return pending

        for eng in ("scalar", "tensor", "vector"):
            r.wait(eng, "s_x", X_OWN)
        ln1 = emit_ln()
        ln1_pe = r.cur("c_pe")
        q12_alt = lambda i: ("sync", "s_g") if i % 2 == 0 else ("gpsimd", "s_g2")
        gp_only = lambda i: ("gpsimd", "s_g2")
        G12_AC = gamma_jobs(g12jobs, (qin0, qin1), gdump, q12_alt)
        RC12 = gamma_finish(G12_AC, [0, 16], [16, 32], [0, 1], (0, 2), 0,
                            [1.0 / NW_12, 1.0 / NW_12])
        G3_AC = gamma_jobs(g3jobs, (sqb, tmpf), sgb, gp_only,
                           pre_gates=[("c_dv", ln1), ("c_pe", ln1_pe)])
        nxt = quant_stage(q12jobs, RC12, {("w2", 7, 1): 1}, 0, q12_alt,
                          qin_gate_ac=G12_AC)
        DV12 = r.cur("c_dv")
        RC3 = gamma_finish(G3_AC, [32], [76], [2], (4, 5), 8, [1.0 / NW_3])
        quant_stage(q3jobs, RC3, {("w3", 21, 0): 3}, nxt, gp_only, prev_dv=DV12,
                    prev_scr=r.cur("s_scr"))
        poll_both(1)
        poll_both(3)
        emit_ffn_pass(ln1)
        ln2 = emit_ln()
        ln2_marks = (r.cur("c_dv"), r.cur("c_ac"), r.cur("c_pe"))
        own_pending = emit_qk(0, ln2, do_q=True)
        r.wait("sync", "c_dv", ln2_marks[0])
        r.wait("sync", "c_ac", ln2_marks[1])
        r.wait("sync", "c_pe", ln2_marks[2])
        for ct in range(CT):
            r.emit_inc("sync", "s_sp", 16,
                       lambda e, ct=ct: e.dma_start(out=xsp[ct * 128:(ct + 1) * 128, :],
                                                    in_=xT_a[:, ct * TB:(ct + 1) * TB]))
        SPILL_V = None

        prev_marks = ln2_marks
        pend = own_pending
        for blk in range(1, NBLK):
            r.wait("sync", "c_dv", prev_marks[0])
            r.wait("sync", "c_ac", prev_marks[1])
            r.wait("sync", "c_pe", prev_marks[2])
            for ct in range(CT):
                r.emit_inc("sync", "s_x", 16,
                           lambda e, ct=ct, blk=blk: e.dma_start(out=xT_a[:, ct * TB:(ct + 1) * TB],
                                                                 in_=xT[blk, ct * 128:(ct + 1) * 128, :]))
            vx = r.cur("s_x")
            for eng in ("scalar", "tensor", "vector"):
                r.wait(eng, "s_x", vx)
            lnb = emit_ln()
            emit_ffn_pass(lnb, pending=pend)
            ln2b = emit_ln()
            prev_marks = (r.cur("c_dv"), r.cur("c_ac"), r.cur("c_pe"))
            pend = emit_qk(blk, ln2b, do_q=False)
        while pend:
            pend.pop(0)()

        for s_ in range(4):
            poll_both(4 + s_)
        KV_RDY = r.cur("c_rdy")
        r.wait("sync", "c_rdy", KV_RDY)
        for s_ in ("c_dv", "c_ev", "c_pe", "c_ac"):
            r.wait("sync", s_)
        for h in range(NQH):
            r.emit_inc("sync", "s_q", 16,
                       lambda e, h=h: e.dma_start(out=qT_a[:, h * TB:(h + 1) * TB],
                                                  in_=qsp[h * 128:(h + 1) * 128, :]))
        QT_V = r.cur("s_q")
        for ct in range(CT):
            r.emit_inc("sync", "s_sp", 16,
                       lambda e, ct=ct: e.dma_start(out=xT_a[:, ct * TB:(ct + 1) * TB],
                                                    in_=xsp[ct * 128:(ct + 1) * 128, :]))
        XRL_V = r.cur("s_sp")

        ot_dv = 0
        exp_vals = []
        otm_vals = {}
        it = 0
        for g in range(NKV):
            if g >= 1:
                r.wait("sync", "c_pe")
            r.emit_inc("sync", "s_q", 16,
                       lambda e, g=g: e.dma_start(out=kTg, in_=sh_kT[g * 128:(g + 1) * 128, :]))
            KV_V = r.emit_inc("sync", "s_q", 16,
                              lambda e, g=g: e.dma_start(
                                  out=vg.rearrange("p (kt d) -> p kt d", kt=KT),
                                  in_=sh_v[:, g * 128:(g + 1) * 128].rearrange("(kt p) d -> p kt d", p=128)))
            for h4 in range(4):
                h = g * 4 + h4
                for qb in range(2):
                    pX = (p2, p3)[it % 2]
                    if it == 0:
                        r.wait("tensor", "s_q", max(KV_V, QT_V))
                        r.wait("tensor", "c_ac")
                        r.wait("tensor", "c_dv")
                    elif qb == 0 and h4 == 0:
                        r.wait("tensor", "s_q", KV_V)

                    def emit_pv(p_, first, pX=pX):
                        r.wait("tensor", "c_ac", exp_vals[-1 - ((len(exp_vals) - 1) - p_ - it * 32)]
                               if False else exp_vals[it * 32 + p_])
                        if first and it >= 2:
                            r.wait("tensor", "c_dv", otm_vals[it - 2])
                        for k2 in (2 * p_, 2 * p_ + 1):
                            r.emit_inc("tensor", "c_pe", 1,
                                       lambda e, k2=k2, pX=pX:
                                       e.matmul(out=pX[:, 0:512],
                                                lhsT=vg[:, k2 * 128:(k2 + 1) * 128],
                                                rhs=etb[:, (k2 % ETS) * 512:((k2 % ETS) + 1) * 512],
                                                start=(k2 == 0), stop=(k2 == KT - 1)))
                            r.emit_inc("tensor", "c_pe", 1,
                                       lambda e, k2=k2, pX=pX:
                                       e.matmul(out=pX[0:1, 512:1024],
                                                lhsT=ones_lb[:],
                                                rhs=etb[:, (k2 % ETS) * 512:((k2 % ETS) + 1) * 512],
                                                start=(k2 == 0), stop=(k2 == KT - 1)))

                    for p_ in range(KT // 2):
                        for half in (0, 1):
                            kt = 2 * p_ + half
                            gidx = it * KT + kt
                            stg = p0 if p_ % 2 == 0 else p1
                            if gidx >= 4 and (gidx - 4) // 2 < len(exp_vals):
                                r.wait("tensor", "c_ac", exp_vals[(gidx - 4) // 2])
                            r.emit_inc("tensor", "c_pe", 1,
                                       lambda e, kt=kt, h=h, qb=qb, stg=stg, half=half:
                                       e.matmul(out=stg[:, half * 512:(half + 1) * 512],
                                                lhsT=kTg[:, kt * 128:(kt + 1) * 128],
                                                rhs=qT_a[:, h * TB + qb * 512: h * TB + (qb + 1) * 512],
                                                start=True, stop=True))
                        v_st = r.cur("c_pe")
                        slot = (2 * p_) % ETS
                        stg = p0 if p_ % 2 == 0 else p1
                        r.wait("scalar", "c_pe", v_st)
                        va = r.emit_inc("scalar", "c_ac", 1,
                                        lambda e, slot=slot, stg=stg:
                                        e.activation(out=etb[:, slot * 512: (slot + 2) * 512],
                                                     in_=stg[:, :], func=AF.Exp,
                                                     scale=INV_SQRT_D, bias=b_shift[:, :]))
                        exp_vals.append(va)
                        if p_ >= 1:
                            emit_pv(p_ - 1, first=(p_ == 1))
                    emit_pv(KT // 2 - 1, first=False)
                    v_pv = r.cur("c_pe")
                    r.wait("scalar", "c_pe", v_pv)
                    r.emit_inc("scalar", "c_ac", 1,
                               lambda e, pX=pX: e.activation(out=rowsB[0:1, 0:512], in_=pX[0:1, 512:1024],
                                                             func=AF.Ln))
                    v_rc = r.emit_inc("scalar", "c_ac", 1,
                                      lambda e: e.activation(out=rowsB[0:1, 0:512], in_=rowsB[0:1, 0:512],
                                                             func=AF.Exp, scale=-1.0))
                    r.wait("tensor", "c_ac", v_rc)
                    v_bc = r.emit_inc("tensor", "c_pe", 1,
                                      lambda e, pX=pX: e.matmul(out=pX[:, 512:1024], lhsT=ones_b[:, :],
                                                                rhs=rowsB[0:1, 0:512], start=True, stop=True))
                    r.wait("vector", "c_pe", v_bc)
                    r.emit_inc("vector", "c_dv", 1,
                               lambda e, pX=pX: e.tensor_copy(out=tmpf[:, 0:512], in_=pX[:, 0:512]))
                    otm_vals[it] = r.emit_inc("vector", "c_dv", 1,
                                              lambda e, h=h, qb=qb, pX=pX:
                                              e.tensor_mul(lnT_a[:, h * TB + qb * 512: h * TB + (qb + 1) * 512],
                                                           tmpf[:, 0:512], pX[:, 512:1024]))
                    it += 1
        ot_dv = otm_vals[it - 1]

        wo_pe = {}
        for chw in range(4):
            buf = chw % 2
            if chw <= 1:
                r.wait("gpsimd", "c_dv", ot_dv)
            else:
                r.wait("gpsimd", "c_pe", wo_pe[chw - 2])
            v_wo = r.emit_inc("gpsimd", "s_aw", 16,
                              lambda e, chw=chw, buf=buf:
                              e.dma_start(out=wob[:, buf * 4 * C:(buf + 1) * 4 * C]
                                          .rearrange("p (b c) -> p b c", b=4),
                                          in_=wo[chw * 512:(chw + 1) * 512, :]
                                          .rearrange("(b p) c -> p b c", p=128)))
            r.wait("tensor", "s_aw", v_wo)
            r.wait("tensor", "c_dv", ot_dv)
            if chw == 0:
                r.wait("tensor", "s_sp", XRL_V)
                r.wait("vector", "s_sp", XRL_V)
            yev = {}
            for ct in range(CT):
                pyw = p2 if ct % 2 == 0 else p3
                if ct >= 2:
                    r.wait("tensor", "c_ev", yev[ct - 2])
                for b4 in range(4):
                    hd = chw * 4 + b4
                    for th in range(2):
                        r.emit_inc("tensor", "c_pe", 1,
                                   lambda e, b4=b4, hd=hd, th=th, ct=ct, pyw=pyw, buf=buf:
                                   e.matmul(out=pyw[:, th * 512:(th + 1) * 512],
                                            lhsT=wob[:, buf * 4 * C + b4 * C + ct * 128:
                                                     buf * 4 * C + b4 * C + (ct + 1) * 128],
                                            rhs=lnT_a[:, hd * TB + th * 512: hd * TB + (th + 1) * 512],
                                            start=(b4 == 0), stop=(b4 == 3)))
                v_y = r.cur("c_pe")
                r.wait("vector", "c_pe", v_y)
                for th in range(2):
                    r.emit_inc("vector", "c_dv", 1,
                               lambda e, ct=ct, th=th, pyw=pyw:
                               e.tensor_add(xT_a[:, ct * TB + th * 512: ct * TB + (th + 1) * 512],
                                            xT_a[:, ct * TB + th * 512: ct * TB + (th + 1) * 512],
                                            pyw[:, th * 512:(th + 1) * 512]))
                yev[ct] = r.emit_inc("vector", "c_ev", 1, lambda e: e.engine_nop())
            wo_pe[chw] = r.cur("c_pe")

        ln3 = emit_ln()

        def _out_ct(ct, ev):
            r.wait("sync", "c_ev", ev)
            r.emit_inc("sync", "s_out", 16,
                       lambda e, ct=ct: e.dma_start(out=yT[ct * 128:(ct + 1) * 128, :],
                                                    in_=xT_a[:, ct * TB:(ct + 1) * TB]))
        ffn3 = emit_ffn_pass(ln3, out_cb=_out_ct)
        r.wait("sync", "s_out")

        with nc.Block() as block:
            @block.sync
            def _(e):
                for f in r.ops["sync"]:
                    f(e)

            @block.gpsimd
            def _(e):
                for f in r.ops["gpsimd"]:
                    f(e)

            @block.vector
            def _(e):
                for f in r.ops["vector"]:
                    f(e)

            @block.scalar
            def _(e):
                for f in r.ops["scalar"]:
                    f(e)

            @block.tensor
            def _(e):
                for f in r.ops["tensor"]:
                    f(e)

    return nc


_NC_CACHE = {}


def kernel(L0_attn_x_q, L0_attn_Wq, L0_attn_Wk, L0_attn_Wv, L0_attn_Wo,
           L0_ffn_w1, L0_ffn_w2, L0_ffn_w3):
    x = np.ascontiguousarray(np.asarray(L0_attn_x_q, dtype=np.float32)[0])
    wq_ = np.ascontiguousarray(np.asarray(L0_attn_Wq, dtype=np.float32).reshape(C, 2048))
    wk_ = np.ascontiguousarray(np.asarray(L0_attn_Wk, dtype=np.float32).reshape(C, 512))
    wv_ = np.ascontiguousarray(np.asarray(L0_attn_Wv, dtype=np.float32).reshape(C, 512))
    wo_ = np.ascontiguousarray(np.asarray(L0_attn_Wo, dtype=np.float32))
    w1T = np.ascontiguousarray(np.asarray(L0_ffn_w1, dtype=np.float32).T)
    w2T = np.ascontiguousarray(np.asarray(L0_ffn_w2, dtype=np.float32).T)
    w3T = np.ascontiguousarray(np.asarray(L0_ffn_w3, dtype=np.float32).T)

    xTb = np.ascontiguousarray(x.reshape(8, TB, C).transpose(0, 2, 1))

    nonce = int(np.random.default_rng().integers(1 << 20, (1 << 31) - 1))
    in_maps = []
    for core in range(8):
        par = core % 2
        blocks = [core] + [b for b in range(par, 8, 2) if b != core]
        m = np.zeros((1, 16), np.int32)
        m[0, 0] = par
        m[0, 1] = nonce
        in_maps.append({
            "meta": m,
            "xT": np.ascontiguousarray(xTb[blocks]),
            "w1th": np.ascontiguousarray(w1T[par * 1024:(par + 1) * 1024, :]),
            "w2th": np.ascontiguousarray(w2T[par * 1024:(par + 1) * 1024, :]),
            "w3th": np.ascontiguousarray(w3T[par * 2816:(par + 1) * 2816, :]),
            "wq": wq_, "wk": wk_, "wv": wv_, "wo": wo_,
        })
    if "nc" not in _NC_CACHE:
        _NC_CACHE["nc"] = build_program()
    res = run_bass_kernel_spmd(_NC_CACHE["nc"], in_maps, core_ids=list(range(8)))
    out = np.empty((SEQ, C), np.float32)
    for core in range(8):
        out[core * TB:(core + 1) * TB, :] = res.results[core]["yT"].T
    return out[None, :, :]
